# version 15
# baseline (speedup 1.0000x reference)
import numpy as np
import contextlib
import concourse.bass as bass
import concourse.mybir as mybir
from concourse.bass_utils import run_bass_kernel_spmd
from concourse.alu_op_type import AluOpType as Op

F32 = mybir.dt.float32
BF16 = mybir.dt.bfloat16
AF = mybir.ActivationFunctionType

P = 128
D = 1024
DCH = 8
F_FF = 4096
FCH = 32
TCORE = 4096
NT = 512
NTILES = TCORE // NT
C6 = NT + 4
N_BIS = 36
GC1 = 0.7978845608028654
GC2 = GC1 * 0.044715
EPS = 1e-5

N_W = D * F_FF
K_W = int(N_W * 0.1 / 2)
K_B1 = max(1, int(F_FF * 0.1 / 2))
K_B2 = max(1, int(D * 0.1 / 2))


def build_program(debug=False, dbg_tile=0):
    nc = bass.Bass("TRN2", target_bir_lowering=False)

    def din(name, shape, dt=F32):
        return nc.dram_tensor(name, shape, dt, kind="ExternalInput").ap()

    x_fm = din("x_fm", [D, TCORE + 4])
    bis_data = din("bis_data", [P, 32768])
    bis_k = din("bis_k", [P, 1])
    oddflag = din("oddflag", [P, 1])
    w_attn = din("w_attn", [D, D])
    dec_in = din("dec_in", [P, DCH])
    battn_in = din("battn_in", [P, DCH])
    convb_in = din("convb_in", [P, DCH])
    convw_in = din("convw_in", [P, 5 * DCH])
    w1_in = din("w1_in", [D, F_FF])
    b1c_in = din("b1c_in", [P, FCH])
    w2_in = din("w2_in", [F_FF, D])
    b2c_in = din("b2c_in", [P, DCH])
    gam_in = din("gam_in", [P, DCH])
    beta_in = din("beta_in", [P, DCH])
    wr_in = din("wr_in", [P, 8 * DCH])
    br_in = din("br_in", [1, 8])

    out_fm = nc.dram_tensor("out_fm", [D, TCORE], F32, kind="ExternalOutput").ap()
    dbg = {}
    if debug:
        for nm, shape, dt in [
            ("dbg_small", [P, 64], F32), ("dbg_bis", [P, 8], F32),
            ("dbg_thrs", [P, 8], F32), ("dbg_carry", [P, 16], F32),
            ("dbg_sin", [P, DCH * NT], BF16), ("dbg_h1b", [P, DCH * (NT + 4)], BF16),
            ("dbg_x1", [P, DCH * NT], F32), ("dbg_zb", [P, DCH * NT], BF16),
            ("dbg_lg", [P, 32], F32), ("dbg_sm", [P, 16], F32),
            ("dbg_g0s", [P, NT], F32), ("dbg_y0", [P, DCH * NT], F32),
            ("dbg_stats", [P, 3 * NT], F32), ("dbg_act", [P, 2 * NT], BF16),
            ("dbg_bis0", [P, 8], F32), ("dbg_bisf", [P, 8], F32),
            ("dbg_trc", [P, 3 * N_BIS], F32), ("dbg_aw", [P, 96], F32),
            ("dbg_b2", [P, 16], F32),
        ]:
            dbg[nm] = nc.dram_tensor(nm, shape, dt, kind="ExternalOutput").ap()

    s_scr = nc.dram_tensor("s_scr", [DCH, P, TCORE], BF16)
    wt_scr = nc.dram_tensor("wt_scr", [DCH, P, D], BF16)
    w1_scr = nc.dram_tensor("w1_scr", [FCH, P, D], BF16)
    pw1_scr = nc.dram_tensor("pw1_scr", [FCH, P, D], BF16)
    w2_scr = nc.dram_tensor("w2_scr", [DCH, P, F_FF], BF16)
    pw2_scr = nc.dram_tensor("pw2_scr", [DCH, P, F_FF], BF16)
    carry_in_d = nc.dram_tensor("carry_in_d", [P, DCH], F32)
    carry_out_d = nc.dram_tensor("carry_out_d", [2 * P, DCH], F32)
    thr_in_d = nc.dram_tensor("thr_in_d", [1, 16], F32)
    thr_out_d = nc.dram_tensor("thr_out_d", [8, 16], F32)

    es = contextlib.ExitStack()
    with es:
        blk = es.enter_context(nc.Block())
        sems = {}
        cnt = {}
        for s in ("sLD", "sLB", "sWQ", "sV", "sA", "sPE", "sPO", "sCC", "sST"):
            sems[s] = es.enter_context(nc.semaphore(s))
            cnt[s] = 0

        def inc(inst, s, n):
            inst.then_inc(sems[s], n)
            cnt[s] += n
            return (s, cnt[s])

        def evt(s):
            return (s, cnt[s])

        def wt(eng, *evts):
            for e in evts:
                if e is None:
                    continue
                s, v = e
                if v > 0:
                    eng.wait_ge(sems[s], v)

        def sb(name, shape, dt=F32):
            return es.enter_context(nc.sbuf_tensor(name, shape, dt))

        ident = sb("ident", [P, P])
        ident1 = sb("ident1", [1, 1])
        ones_row = sb("ones_row", [1, NT], BF16)
        ones1b = sb("ones1b", [1, P], BF16)
        ones1f = sb("ones1f", [1, P])
        ones128f = sb("ones128f", [P, P])
        sMean = sb("sMean", [P, P], BF16)
        sSq = sb("sSq", [P, P], BF16)
        iota_row = sb("iota_row", [P, NT])
        eps_t = sb("eps_t", [P, 1])
        small = sb("small", [P, 64])
        convw = sb("convw", [P, 5 * DCH])
        convb_f = sb("convb_f", [P, DCH])
        gam_b = sb("gam_b", [P, DCH], BF16)
        beta_b = sb("beta_b", [P, DCH], BF16)
        wr_f = sb("wr_f", [P, 8 * DCH])
        wr_b = sb("wr_b", [P, 8 * DCH], BF16)
        br_f = sb("br_f", [1, 8])
        br_b = sb("br_b", [1, 8], BF16)
        c1r = sb("c1r", [1, 8], BF16)
        c2row_b = sb("c2row_b", [1, 8], BF16)
        c2bc = sb("c2bc", [P, 8])
        diag = sb("diag", [P, 40 * P], BF16)
        carry_state = sb("carry_state", [P, DCH])
        carry_eff = sb("carry_eff", [P, DCH])
        thrs = sb("thrs", [P, 8])
        thr_rows = sb("thr_rows", [1, 8])
        thrpad = sb("thrpad", [1, 16])
        spacer = sb("spacer", [P, NT])
        spacerA = spacer

        bank = [es.enter_context(nc.psum_tensor(f"bank{i}", [P, NT], F32))
                for i in range(8)]

        @blk.sync
        def _(sy):
            inc(sy.dma_start(out=small[:, 0:8], in_=dec_in), "sLD", 16)
            inc(sy.dma_start(out=small[:, 32:40], in_=battn_in), "sLD", 16)
            inc(sy.dma_start(out=convb_f[:], in_=convb_in), "sLD", 16)
            inc(sy.dma_start(out=convw[:], in_=convw_in), "sLD", 16)
            inc(sy.dma_start(out=small[:, 40:48], in_=gam_in), "sLD", 16)
            inc(sy.dma_start(out=small[:, 48:56], in_=beta_in), "sLD", 16)
            inc(sy.dma_start(out=small[:, 56:57], in_=bis_k), "sLD", 16)
            inc(sy.dma_start(out=small[:, 57:58], in_=oddflag), "sLD", 16)
            inc(sy.dma_start(out=wr_f[:], in_=wr_in), "sLD", 16)
            inc(sy.dma_start(out=br_f[:], in_=br_in), "sLD", 16)
        e_small = evt("sLD")

        @blk.gpsimd
        def _(g):
            g.memset(ones_row[:], 1.0)
            g.memset(ones1b[:], 1.0)
            g.memset(ones1f[:], 1.0)
            g.memset(ones128f[:], 1.0)
            g.memset(sMean[:], -1.0 / D)
            g.memset(sSq[:], 1.0 / D)
            g.memset(eps_t[:], EPS)
            g.memset(ident1[:], 1.0)
            g.memset(thrpad[:], 0.0)
            g.memset(carry_state[:], 0.0)
            g.iota(out=ident[:], pattern=[[1, P]], base=0, channel_multiplier=-1,
                   allow_small_or_imprecise_dtypes=True)
            inc(g.iota(out=iota_row[:], pattern=[[1, NT]], base=0,
                       channel_multiplier=0,
                       allow_small_or_imprecise_dtypes=True), "sPO", 1)
        e_const = evt("sPO")

        @blk.scalar
        def _(s):
            wt(s, e_small)
            s.activation(out=small[:, 8:16], in_=small[:, 0:8], func=AF.Sigmoid)
            s.activation(out=small[:, 16:24], in_=small[:, 0:8], func=AF.Sigmoid,
                         scale=-1.0)
            s.copy(out=spacerA[:], in_=iota_row[:])
            inc(s.activation(out=small[:, 24:32], in_=small[:, 8:16], func=AF.Ln),
                "sA", 1)
        e_sig = evt("sA")

        @blk.vector
        def _(v):
            wt(v, e_const, e_sig)
            v.tensor_scalar(out=ident[:], in0=ident[:], scalar1=0.0, scalar2=None,
                            op0=Op.is_equal)
            v.tensor_tensor(out=small[:, 32:40], in0=small[:, 32:40],
                            in1=convb_f[:], op=Op.add)
            v.tensor_copy(out=gam_b[:], in_=small[:, 40:48])
            v.tensor_copy(out=beta_b[:], in_=small[:, 48:56])
            v.tensor_copy(out=br_b[:], in_=br_f[:])
            for ch in range(DCH):
                v.tensor_scalar(out=wr_b[:, 8 * ch:8 * ch + 8],
                                in0=wr_f[:, 8 * ch:8 * ch + 8],
                                scalar1=small[:, 40 + ch:41 + ch], scalar2=None,
                                op0=Op.mult)
            for j in range(40):
                v.tensor_scalar(out=diag[:, j * P:(j + 1) * P], in0=ident[:],
                                scalar1=convw[:, j:j + 1], scalar2=None,
                                op0=Op.mult)
            inc(v.engine_nop(), "sV", 1)
        e_prolog = evt("sV")

        @blk.tensor
        def _(t):
            wt(t, e_prolog)
            for ch in range(DCH):
                t.matmul(bank[1][0:1, 0:8], gam_b[:, ch:ch + 1],
                         wr_b[:, 8 * ch:8 * ch + 8],
                         start=(ch == 0), stop=(ch == DCH - 1))
            for ch in range(DCH):
                t.matmul(bank[1][0:1, 8:16], beta_b[:, ch:ch + 1],
                         wr_b[:, 8 * ch:8 * ch + 8],
                         start=(ch == 0), stop=False)
            t.matmul(bank[1][0:1, 8:16], ones1b[0:1, 0:1], br_b[:],
                     start=False, stop=True)
            inc(t.matmul(bank[1][0:64, 128:256], wr_b[:, 0:64], diag[:, 0:P],
                         start=True, stop=True), "sPE", 1)
        e_c12 = evt("sPE")

        @blk.scalar
        def _(s):
            wt(s, e_c12)
            s.copy(out=spacerA[:], in_=iota_row[:])
            s.copy(out=c1r[:], in_=bank[1][0:1, 0:8])
            s.copy(out=c2row_b[:], in_=bank[1][0:1, 8:16])
            inc(s.copy(out=spacerA[:], in_=iota_row[:]), "sA", 1)
        e_c12s = evt("sA")

        @blk.tensor
        def _(t):
            wt(t, e_c12s)
            t.transpose(bank[2][:, 128:256], ident[:], ident[:])
            inc(t.matmul(bank[2][:, 0:8], ones1b[:], c2row_b[:],
                         start=True, stop=True), "sPE", 1)
        e_c2bcp = evt("sPE")

        @blk.scalar
        def _(s):
            wt(s, e_c2bcp)
            s.copy(out=spacerA[:], in_=iota_row[:])
            s.copy(out=c2bc[:], in_=bank[2][:, 0:8])
            inc(s.copy(out=spacerA[:], in_=iota_row[:]), "sA", 1)
        e_c2done = evt("sA")

        with contextlib.ExitStack() as es_s:
            aw = es_s.enter_context(nc.sbuf_tensor("aw", [P, 32768], F32))
            trc = es_s.enter_context(nc.sbuf_tensor("trc", [P, 3 * N_BIS], F32)) \
                if debug else None
            dump = es_s.enter_context(nc.sbuf_tensor("dump", [P, 8192], BF16))
            dump2 = es_s.enter_context(nc.sbuf_tensor("dump2", [P, 8192], BF16))
            loW = es_s.enter_context(nc.sbuf_tensor("loW", [P, NT], F32))
            hiW = es_s.enter_context(nc.sbuf_tensor("hiW", [P, NT], F32))
            midW = es_s.enter_context(nc.sbuf_tensor("midW", [P, NT], F32))
            totW = es_s.enter_context(nc.sbuf_tensor("totW", [P, NT], F32))
            prW = es_s.enter_context(nc.sbuf_tensor("prW", [P, NT], F32))
            pr2W = es_s.enter_context(nc.sbuf_tensor("pr2W", [P, NT], F32))
            t6W = es_s.enter_context(nc.sbuf_tensor("t6W", [P, NT], F32))
            t7W = es_s.enter_context(nc.sbuf_tensor("t7W", [P, NT], F32))
            kW = es_s.enter_context(nc.sbuf_tensor("kW", [P, NT], F32))
            bis = es_s.enter_context(nc.sbuf_tensor("bis", [P, 8], F32))
            maxrow = es_s.enter_context(nc.sbuf_tensor("maxrow", [1, P], F32))
            maxsc = es_s.enter_context(nc.sbuf_tensor("maxsc", [1, 1], F32))
            maxb = es_s.enter_context(nc.sbuf_tensor("maxb", [1, 1], BF16))

            @blk.sync
            def _(sy):
                for q in range(4):
                    inc(sy.dma_start(out=aw[:, q * 8192:(q + 1) * 8192],
                                     in_=bis_data[:, q * 8192:(q + 1) * 8192]),
                        "sLD", 16)
            e_bld = evt("sLD")

            @blk.scalar
            def _(s):
                wt(s, e_bld)
                s.activation(out=aw[:, 0:16384], in_=aw[:, 0:16384], func=AF.Abs)
                inc(s.activation(out=aw[:, 16384:32768], in_=aw[:, 16384:32768],
                                 func=AF.Abs), "sA", 1)
            e_abs = evt("sA")

            @blk.vector
            def _(v):
                wt(v, e_abs)
                for q in range(4):
                    v.tensor_reduce(out=bis[:, 3 + q:4 + q],
                                    in_=aw[:, q * 8192:(q + 1) * 8192],
                                    axis=mybir.AxisListType.X, op=Op.max)
                v.tensor_copy(out=spacer[:], in_=iota_row[:])
                v.tensor_tensor(out=bis[:, 6:7], in0=bis[:, 3:4], in1=bis[:, 4:5],
                                op=Op.max)
                v.tensor_tensor(out=bis[:, 6:7], in0=bis[:, 6:7], in1=bis[:, 5:6],
                                op=Op.max)
                v.tensor_copy(out=spacer[:], in_=iota_row[:])
                v.tensor_tensor(out=bis[:, 6:7], in0=bis[:, 6:7], in1=bis[:, 3 + 3:4 + 3],
                                op=Op.max)
                inc(v.tensor_copy(out=spacer[:], in_=iota_row[:]), "sV", 1)
            e_pmax = evt("sV")

            if debug:
                @blk.gpsimd
                def _(g):
                    wt(g, e_pmax)
                    inc(g.dma_start(out=dbg["dbg_aw"][:, 0:32],
                                    in_=aw[:, 16368:16400]), "sWQ", 16)
                    inc(g.dma_start(out=dbg["dbg_aw"][:, 32:64],
                                    in_=aw[:, 32700:32732]), "sWQ", 16)
                    inc(g.dma_start(out=dbg["dbg_aw"][:, 64:96],
                                    in_=aw[:, 24000:24032]), "sWQ", 16)

            @blk.vector
            def _(v):
                v.tensor_tensor(out=bis[:, 5:6], in0=bis[:, 6:7],
                                in1=bis[:, 6:7], op=Op.mult)
                inc(v.tensor_copy(out=spacer[:], in_=iota_row[:]), "sV", 1)
            e_pm2 = evt("sV")

            @blk.tensor
            def _(t):
                wt(t, e_pm2)
                inc(t.matmul(bank[3][:, 0:1], ones128f[:], bis[:, 5:6],
                             start=True, stop=True), "sPE", 1)
            e_pmsum = evt("sPE")

            @blk.scalar
            def _(s):
                wt(s, e_pmsum)
                s.copy(out=spacerA[:], in_=iota_row[:])
                s.activation(out=bis[:, 1:2], in_=bank[3][:, 0:1], func=AF.Sqrt)
                inc(s.copy(out=spacerA[:], in_=iota_row[:]), "sA", 1)
            e_gmax = evt("sA")

            @blk.vector
            def _(v):
                wt(v, e_gmax)
                v.tensor_copy(out=spacer[:], in_=iota_row[:])
                v.tensor_scalar(out=hiW[:], in0=iota_row[:], scalar1=0.0,
                                scalar2=bis[:, 1:2], op0=Op.mult, op1=Op.add)
                v.tensor_scalar(out=hiW[:], in0=hiW[:], scalar1=1.05,
                                scalar2=None, op0=Op.mult)
                v.memset(loW[:], -1.0)
                v.tensor_scalar(out=kW[:], in0=iota_row[:], scalar1=0.0,
                                scalar2=small[:, 56:57], op0=Op.mult, op1=Op.add)
                v.tensor_scalar(out=kW[:], in0=kW[:], scalar1=1.0 / D,
                                scalar2=None, op0=Op.mult)
                v.tensor_tensor(out=midW[:], in0=loW[:], in1=hiW[:], op=Op.add)
                inc(v.tensor_scalar(out=midW[:], in0=midW[:], scalar1=0.5,
                                    scalar2=None, op0=Op.mult), "sV", 1)

            dumps = [dump, dump2]
            peU = [None, None]
            e_postR = None
            for it in range(N_BIS):
                e_cmp = []
                for q in range(4):
                    @blk.vector
                    def _(v):
                        wt(v, peU[q % 2])
                        inc(v.tensor_scalar(out=dumps[q % 2][:],
                                            in0=aw[:, q * 8192:(q + 1) * 8192],
                                            scalar1=midW[:, 0:1], scalar2=None,
                                            op0=Op.is_le), "sV", 1)
                    e_cmp.append(evt("sV"))
                for q in range(4):
                    @blk.tensor
                    def _(t):
                        wt(t, e_cmp[q])
                        if q == 0:
                            wt(t, e_postR)
                        mm = None
                        for c in range(16):
                            mm = t.matmul(bank[3][:, 0:NT], sSq[:],
                                          dumps[q % 2][:, c * NT:(c + 1) * NT],
                                          start=(q == 0 and c == 0),
                                          stop=(q == 3 and c == 15))
                        inc(mm, "sPE", 1)
                    peU[q % 2] = evt("sPE")
                e_tt = peU[1]

                @blk.vector
                def _(v):
                    wt(v, e_tt)
                    v.tensor_copy(out=spacer[:], in_=iota_row[:])
                    v.tensor_scalar(out=t6W[:], in0=bank[3][:, 0:NT],
                                    scalar1=1.0, scalar2=0.0, op0=Op.mult,
                                    op1=Op.add, accum_out=bis[:, 3:4])
                    v.tensor_copy(out=spacer[:], in_=iota_row[:])
                    v.tensor_scalar(out=totW[:], in0=iota_row[:], scalar1=0.0,
                                    scalar2=bis[:, 3:4], op0=Op.mult,
                                    op1=Op.add)
                    v.tensor_tensor(out=prW[:], in0=totW[:], in1=kW[:],
                                    op=Op.is_lt)
                    v.tensor_scalar(out=pr2W[:], in0=prW[:], scalar1=-1.0,
                                    scalar2=1.0, op0=Op.mult, op1=Op.add)
                    v.tensor_tensor(out=t6W[:], in0=midW[:], in1=loW[:],
                                    op=Op.subtract)
                    v.tensor_tensor(out=t6W[:], in0=t6W[:], in1=prW[:],
                                    op=Op.mult)
                    v.tensor_tensor(out=loW[:], in0=loW[:], in1=t6W[:],
                                    op=Op.add)
                    v.tensor_tensor(out=t7W[:], in0=midW[:], in1=hiW[:],
                                    op=Op.subtract)
                    v.tensor_tensor(out=t7W[:], in0=t7W[:], in1=pr2W[:],
                                    op=Op.mult)
                    v.tensor_tensor(out=hiW[:], in0=hiW[:], in1=t7W[:],
                                    op=Op.add)
                    v.tensor_tensor(out=midW[:], in0=loW[:], in1=hiW[:],
                                    op=Op.add)
                    inc(v.tensor_scalar(out=midW[:], in0=midW[:], scalar1=0.5,
                                        scalar2=None, op0=Op.mult), "sV", 1)
                e_postR = evt("sV")
            @blk.vector
            def _(v):
                v.tensor_copy(out=spacer[:], in_=iota_row[:])
                v.tensor_copy(out=thrpad[0:1, 0:1], in_=hiW[0:1, 0:1])
                inc(v.tensor_copy(out=spacer[:], in_=iota_row[:]), "sV", 1)
            e_bisd = evt("sV")

            @blk.gpsimd
            def _(g):
                wt(g, e_bisd)
                inc(g.dma_start(out=thr_in_d.ap(), in_=thrpad[:]), "sWQ", 16)
                if debug:
                    inc(g.dma_start(out=dbg["dbg_bisf"], in_=bis[:]), "sWQ", 16)
                    inc(g.dma_start(out=dbg["dbg_trc"], in_=trc[:]), "sWQ", 16)
            e_thrpub = evt("sWQ")

        @blk.gpsimd
        def _(g):
            wt(g, e_thrpub)
            inc(g.collective_compute(
                "AllGather", Op.bypass,
                replica_groups=[[0, 1, 2, 3, 4, 5, 6, 7]],
                ins=[thr_in_d.ap().opt()], outs=[thr_out_d.ap().opt()]),
                "sCC", 1)
        e_thcc = evt("sCC")

        with contextlib.ExitStack() as es_a:
            def sba(name, shape, dt=F32):
                return es_a.enter_context(nc.sbuf_tensor(name, shape, dt))
            xaA = sba("xaA", [P, DCH * NT])
            xb16A = sba("xb16A", [P, DCH * NT], BF16)
            xsqA = sba("xsqA", [P, DCH * NT], BF16)
            a_bc = sba("a_bcA", [P, DCH * NT])
            negmA = sba("negmA", [P, NT])
            rA = sba("rA", [P, NT])
            varA = sba("varA", [P, NT])
            h1f = sba("h1fA", [P, NT])
            uf = sba("ufA", [P, NT])
            sf = sba("sfA", [P, NT])
            sb16 = [sba(f"sb16A{i}", [P, NT], BF16) for i in range(2)]

            @blk.vector
            def _(v):
                wt(v, e_sig, e_bisd)
                for ch in range(DCH):
                    v.tensor_scalar(out=a_bc[:, ch * NT:(ch + 1) * NT],
                                    in0=iota_row[:], scalar1=0.0,
                                    scalar2=small[:, 8 + ch:9 + ch],
                                    op0=Op.mult, op1=Op.add)
                inc(v.engine_nop(), "sV", 1)
            e_abc = evt("sV")

            sdma_evt = [None, None]
            e_var_prev = None
            e_negm_prev = None
            e_xaA_free = None
            for ti in range(NTILES):
                c0 = ti * NT + 2

                @blk.sync
                def _(sy):
                    if ti == 0:
                        wt(sy, e_bisd, e_thrpub)
                    wt(sy, e_xaA_free)
                    for ch in range(DCH):
                        inc(sy.dma_start(out=xaA[:, ch * NT:(ch + 1) * NT],
                                         in_=x_fm[ch * P:(ch + 1) * P, c0:c0 + NT]),
                            "sLD", 16)
                e_x = evt("sLD")

                @blk.vector
                def _(v):
                    wt(v, e_x, e_abc)
                    inc(v.tensor_copy(out=xb16A[:], in_=xaA[:]), "sV", 1)
                e_cast = evt("sV")

                @blk.scalar
                def _(s):
                    wt(s, e_x)
                    inc(s.activation(out=xsqA[:], in_=xaA[:], func=AF.Square),
                        "sA", 1)
                e_sq = evt("sA")

                @blk.tensor
                def _(t):
                    wt(t, e_cast, e_sq, e_var_prev, e_negm_prev)
                    for ch in range(DCH):
                        t.matmul(bank[4][:], sMean[:],
                                 xb16A[:, ch * NT:(ch + 1) * NT],
                                 start=(ch == 0), stop=(ch == DCH - 1))
                    for ch in range(DCH):
                        mm = t.matmul(bank[5][:], sSq[:],
                                      xsqA[:, ch * NT:(ch + 1) * NT],
                                      start=(ch == 0), stop=(ch == DCH - 1))
                    inc(mm, "sPE", 1)
                e_stat = evt("sPE")

                @blk.scalar
                def _(s):
                    wt(s, e_stat)
                    inc(s.copy(out=negmA[:], in_=bank[4][:]), "sA", 1)
                e_negm_prev = evt("sA")
                e_negm = e_negm_prev

                @blk.vector
                def _(v):
                    wt(v, e_negm)
                    v.tensor_tensor(out=varA[:], in0=negmA[:], in1=negmA[:],
                                    op=Op.mult)
                    inc(v.tensor_tensor(out=varA[:], in0=bank[5][:], in1=varA[:],
                                        op=Op.subtract), "sV", 1)
                e_var_prev = evt("sV")
                e_var = e_var_prev

                @blk.scalar
                def _(s):
                    wt(s, e_var)
                    s.activation(out=rA[:], in_=varA[:], func=AF.Ln, bias=eps_t[:])
                    inc(s.activation(out=rA[:], in_=rA[:], func=AF.Exp,
                                     scale=-0.5), "sA", 1)
                e_rs = evt("sA")

                for ch in range(DCH):
                    par = ch % 2

                    @blk.vector
                    def _(v):
                        if ch == 0:
                            wt(v, e_rs)
                        wt(v, sdma_evt[par])
                        v.tensor_tensor(out=h1f[:],
                                        in0=xaA[:, ch * NT:(ch + 1) * NT],
                                        in1=negmA[:], op=Op.add)
                        v.tensor_tensor(out=h1f[:], in0=h1f[:], in1=rA[:],
                                        op=Op.mult)
                        v.tensor_scalar(out=uf[:], in0=h1f[:],
                                        scalar1=small[:, 16 + ch:17 + ch],
                                        scalar2=None, op0=Op.mult)
                        v.tensor_tensor_scan(out=sf[:],
                                             data0=a_bc[:, ch * NT:(ch + 1) * NT],
                                             data1=uf[:],
                                             initial=carry_state[:, ch:ch + 1],
                                             op0=Op.mult, op1=Op.add)
                        v.tensor_copy(out=sb16[par][:], in_=sf[:])
                        inc(v.tensor_copy(out=carry_state[:, ch:ch + 1],
                                          in_=sf[:, NT - 1:NT]), "sV", 1)
                    e_sc = evt("sV")

                    @blk.gpsimd
                    def _(g):
                        wt(g, e_sc)
                        inc(g.dma_start(
                            out=s_scr.ap()[ch, :, ti * NT:(ti + 1) * NT],
                            in_=sb16[par][:]), "sWQ", 16)
                    sdma_evt[par] = evt("sWQ")
                e_xaA_free = evt("sV")

            @blk.vector
            def _(v):
                wt(v, sdma_evt[0], sdma_evt[1])
                inc(v.engine_nop(), "sV", 1)
            e_phA = evt("sV")

        @blk.gpsimd
        def _(g):
            wt(g, e_thcc)
            for r in range(8):
                inc(g.dma_start(out=thr_rows[0:1, r:r + 1],
                                in_=thr_out_d.ap()[r:r + 1, 0:1]), "sWQ", 16)
        e_thg = evt("sWQ")

        @blk.gpsimd
        def _(g):
            wt(g, e_phA)
            e1 = inc(g.dma_start(out=carry_in_d.ap(), in_=carry_state[:]),
                     "sWQ", 16)
            wt(g, e1)
            inc(g.collective_compute(
                "AllGather", Op.bypass,
                replica_groups=[[0, 1], [2, 3], [4, 5], [6, 7]],
                ins=[carry_in_d.ap().opt()], outs=[carry_out_d.ap().opt()]),
                "sCC", 1)
            wt(g, evt("sCC"))
            inc(g.dma_start(out=carry_eff[:], in_=carry_out_d.ap()[0:P, :]),
                "sWQ", 16)
        e_ccd = evt("sWQ")

        @blk.vector
        def _(v):
            wt(v, e_ccd)
            inc(v.tensor_scalar(out=carry_eff[:], in0=carry_eff[:],
                                scalar1=small[:, 57:58], scalar2=None,
                                op0=Op.mult), "sV", 1)
        e_ceff = evt("sV")

        @blk.tensor
        def _(t):
            wt(t, e_thg)
            mm = None
            for r in range(8):
                mm = t.matmul(bank[3][:, r:r + 1], ones1f[:],
                              thr_rows[0:1, r:r + 1], start=True, stop=True)
            inc(mm, "sPE", 1)
        e_thbc = evt("sPE")

        @blk.vector
        def _(v):
            wt(v, e_thbc)
            v.tensor_copy(out=spacer[:], in_=iota_row[:])
            v.tensor_copy(out=thrs[:], in_=bank[3][:, 0:8])
            inc(v.tensor_copy(out=spacer[:], in_=iota_row[:]), "sV", 1)
        e_thrs = evt("sV")

        if debug:
            @blk.gpsimd
            def _(g):
                wt(g, e_thrs, e_ceff, e_sig)
                inc(g.dma_start(out=dbg["dbg_small"], in_=small[:]), "sWQ", 16)
                inc(g.dma_start(out=dbg["dbg_thrs"], in_=thrs[:]), "sWQ", 16)
                inc(g.dma_start(out=dbg["dbg_carry"][:, 0:8], in_=carry_state[:]),
                    "sWQ", 16)
                inc(g.dma_start(out=dbg["dbg_carry"][:, 8:16], in_=carry_eff[:]),
                    "sWQ", 16)

        b1c = sb("b1cK", [P, FCH])
        pb1c = sb("pb1cK", [P, FCH])
        b2c = sb("b2cK", [P, DCH])
        pb2c = sb("pb2cK", [P, DCH])
        with contextlib.ExitStack() as es_m:
            def sbm(name, shape, dt=F32):
                return es_m.enter_context(nc.sbuf_tensor(name, shape, dt))
            wrow = [sbm(f"wrowM{i}", [P, D]) for i in range(2)]
            wrb16 = [sbm(f"wrbM{i}", [P, D], BF16) for i in range(2)]
            wtT = [sbm(f"wtTM{i}", [P, DCH, P], BF16) for i in range(2)]
            awb = sbm("awbM", [P, D])
            t1m = sbm("t1M", [P, D])
            t2m = sbm("t2M", [P, D])
            pwb = [sbm(f"pwbM{i}", [P, D], BF16) for i in range(2)]

            use_evt = [None, None]
            wq_evt = [None, None]
            for r in range(DCH):
                pr = r % 2

                @blk.sync
                def _(sy):
                    if r == 0:
                        wt(sy, e_phA)
                    wt(sy, use_evt[pr])
                    e_l = inc(sy.dma_start(out=wrow[pr][:],
                                           in_=w_attn[r * P:(r + 1) * P, :]),
                              "sLD", 16)
                e_ld = evt("sLD")

                @blk.vector
                def _(v):
                    wt(v, e_ld, wq_evt[pr])
                    inc(v.tensor_copy(out=wrb16[pr][:], in_=wrow[pr][:]), "sV", 1)
                use_evt[pr] = evt("sV")
                e_cast = evt("sV")

                @blk.sync
                def _(sy):
                    wt(sy, e_cast)
                    inc(sy.dma_start_transpose(out=wtT[pr][:], in_=wrb16[pr][:]),
                        "sLD", 16)
                e_tr = evt("sLD")

                @blk.sync
                def _(sy):
                    wt(sy, e_tr)
                    inc(sy.dma_start(out=wt_scr.ap()[r],
                                     in_=wtT[pr][:].rearrange("p m q -> p (m q)")),
                        "sWQ", 16)
                wq_evt[pr] = evt("sWQ")

            use_evt = [evt("sWQ"), evt("sWQ")]
            for f in range(FCH):
                pr = f % 2

                @blk.sync
                def _(sy):
                    wt(sy, use_evt[pr])
                    for k in range(DCH):
                        inc(sy.dma_start(
                            out=wrow[pr][:, k * P:(k + 1) * P],
                            in_=w1_in[k * P:(k + 1) * P, f * P:(f + 1) * P]),
                            "sLD", 16)
                e_ld = evt("sLD")

                @blk.scalar
                def _(s):
                    wt(s, e_ld, e_thrs)
                    inc(s.activation(out=awb[:], in_=wrow[pr][:], func=AF.Abs),
                        "sA", 1)
                e_ab = evt("sA")

                @blk.vector
                def _(v):
                    wt(v, e_ab)
                    v.tensor_scalar(out=t1m[:], in0=awb[:], scalar1=thrs[:, 0:1],
                                    scalar2=0.8, op0=Op.is_le, op1=Op.mult)
                    v.tensor_scalar(out=t2m[:], in0=awb[:], scalar1=thrs[:, 1:2],
                                    scalar2=-0.8, op0=Op.is_ge, op1=Op.mult)
                    v.scalar_tensor_tensor(out=t1m[:], in0=t1m[:], scalar=1.0,
                                           in1=t2m[:], op0=Op.add, op1=Op.add)
                    v.tensor_tensor(out=pwb[pr][:], in0=wrow[pr][:], in1=t1m[:],
                                    op=Op.mult)
                    inc(v.tensor_copy(out=wrb16[pr][:], in_=wrow[pr][:]), "sV", 1)
                e_wv = evt("sV")

                @blk.gpsimd
                def _(g):
                    wt(g, e_wv)
                    inc(g.dma_start(out=w1_scr.ap()[f], in_=wrb16[pr][:]),
                        "sWQ", 16)
                    inc(g.dma_start(out=pw1_scr.ap()[f], in_=pwb[pr][:]),
                        "sWQ", 16)
                use_evt[pr] = evt("sWQ")

            for fc in range(FCH):
                pr = fc % 2

                @blk.sync
                def _(sy):
                    wt(sy, use_evt[pr])
                    inc(sy.dma_start(out=wrow[pr][:],
                                     in_=w2_in[fc * P:(fc + 1) * P, :]), "sLD", 16)
                e_ld = evt("sLD")

                @blk.scalar
                def _(s):
                    wt(s, e_ld)
                    inc(s.activation(out=awb[:], in_=wrow[pr][:], func=AF.Abs),
                        "sA", 1)
                e_ab = evt("sA")

                @blk.vector
                def _(v):
                    wt(v, e_ab)
                    v.tensor_scalar(out=t1m[:], in0=awb[:], scalar1=thrs[:, 2:3],
                                    scalar2=0.8, op0=Op.is_le, op1=Op.mult)
                    v.tensor_scalar(out=t2m[:], in0=awb[:], scalar1=thrs[:, 3:4],
                                    scalar2=-0.8, op0=Op.is_ge, op1=Op.mult)
                    v.scalar_tensor_tensor(out=t1m[:], in0=t1m[:], scalar=1.0,
                                           in1=t2m[:], op0=Op.add, op1=Op.add)
                    v.tensor_tensor(out=pwb[pr][:], in0=wrow[pr][:], in1=t1m[:],
                                    op=Op.mult)
                    inc(v.tensor_scalar(out=wrb16[pr][:], in0=wrow[pr][:],
                                        scalar1=1.0, scalar2=None, op0=Op.mult),
                        "sV", 1)
                e_wv = evt("sV")

                @blk.gpsimd
                def _(g):
                    wt(g, e_wv)
                    for d in range(DCH):
                        inc(g.dma_start(
                            out=w2_scr.ap()[d, :, fc * P:(fc + 1) * P],
                            in_=wrb16[pr][:, d * P:(d + 1) * P]), "sWQ", 16)
                        inc(g.dma_start(
                            out=pw2_scr.ap()[d, :, fc * P:(fc + 1) * P],
                            in_=pwb[pr][:, d * P:(d + 1) * P]), "sWQ", 16)
                use_evt[pr] = evt("sWQ")

            @blk.sync
            def _(sy):
                inc(sy.dma_start(out=b1c[:], in_=b1c_in), "sLD", 16)
                inc(sy.dma_start(out=b2c[:], in_=b2c_in), "sLD", 16)
            e_bc = evt("sLD")

            @blk.scalar
            def _(s):
                wt(s, e_bc)
                s.activation(out=awb[:, 0:FCH], in_=b1c[:], func=AF.Abs)
                s.activation(out=awb[:, 64:64 + DCH], in_=b2c[:], func=AF.Abs)
                inc(s.copy(out=t2m[:, 0:NT], in_=awb[:, 0:NT]), "sA", 1)
            e_abc = evt("sA")

            @blk.vector
            def _(v):
                wt(v, e_abc, e_thrs)
                v.tensor_scalar(out=t1m[:, 0:FCH], in0=awb[:, 0:FCH],
                                scalar1=thrs[:, 4:5], scalar2=0.8,
                                op0=Op.is_le, op1=Op.mult)
                v.tensor_scalar(out=t1m[:, 64:64 + FCH], in0=awb[:, 0:FCH],
                                scalar1=thrs[:, 5:6], scalar2=-0.8,
                                op0=Op.is_ge, op1=Op.mult)
                v.tensor_scalar(out=t1m[:, 128:128 + DCH],
                                in0=awb[:, 64:64 + DCH],
                                scalar1=thrs[:, 6:7], scalar2=0.8,
                                op0=Op.is_le, op1=Op.mult)
                v.tensor_scalar(out=t1m[:, 192:192 + DCH],
                                in0=awb[:, 64:64 + DCH],
                                scalar1=thrs[:, 7:8], scalar2=-0.8,
                                op0=Op.is_ge, op1=Op.mult)
                v.tensor_copy(out=t2m[:, 0:NT], in_=t1m[:, 0:NT])
                v.scalar_tensor_tensor(out=t1m[:, 0:FCH], in0=t1m[:, 0:FCH],
                                       scalar=1.0, in1=t1m[:, 64:64 + FCH],
                                       op0=Op.add, op1=Op.add)
                v.scalar_tensor_tensor(out=t1m[:, 128:128 + DCH],
                                       in0=t1m[:, 128:128 + DCH], scalar=1.0,
                                       in1=t1m[:, 192:192 + DCH],
                                       op0=Op.add, op1=Op.add)
                v.tensor_tensor(out=pb1c[:], in0=b1c[:], in1=t1m[:, 0:FCH],
                                op=Op.mult)
                v.tensor_tensor(out=pb2c[:], in0=b2c[:],
                                in1=t1m[:, 128:128 + DCH], op=Op.mult)
                inc(v.tensor_copy(out=t2m[:, 0:NT], in_=t1m[:, 0:NT]), "sV", 1)
            e_m_done = evt("sWQ")
            e_m_v = evt("sV")
            e_m_a = evt("sA")

        xa = sb("xaB", [P, DCH * C6])
        castA = sb("castAB", [P, DCH * C6], BF16)
        sqA = sb("sqAB", [P, DCH * C6], BF16)
        h1b = sb("h1bB", [P, DCH * C6], BF16)
        s_in = sb("s_inB", [P, DCH * NT], BF16)
        x1 = [sb(f"x1B{i}", [P, DCH * NT]) for i in range(2)]
        zbuf = [sb(f"zbB{i}", [P, DCH * NT], BF16) for i in range(2)]
        act_sb = sb("actB", [P, FCH * NT], BF16)
        y0s = sb("y0sB", [P, DCH * NT])
        negm_s = sb("negmB", [P, NT])
        r_s = sb("rB", [P, NT])
        negmhW = sb("negmhB", [P, NT])
        rhW = sb("rhB", [P, NT])
        hsc = sb("hscsB", [P, 4])
        pw_one = sb("pwB0", [P, NT])
        pw_t = [pw_one, pw_one]
        bct_all = sb("bctB", [P, 8])
        qs = sb("qsB", [P, NT])
        vs = sb("vsB", [P, NT])
        h1tmp = sb("h1tmpB", [P, NT])
        ws = h1tmp
        hscW = qs
        var_s = vs
        lg = sb("lgB", [P, 32])
        sm = sb("smB", [P, 16])
        g0r = sb("g0rB", [1, NT], BF16)
        rzc = sb("rzcB", [P, 4])
        nmzr = sb("nmzrB", [1, NT], BF16)
        g0s = [sb(f"g0sB{i}", [P, NT]) for i in range(2)]
        wbuf = [sb(f"wbufB{i}", [P, D], BF16) for i in range(3)]
        wtbuf = wbuf
        w2buf = [sb(f"w2bufB{i}", [P, 16 * P], BF16) for i in range(4)]
        otile = [pw_one, rhW]

        gateB = (e_m_done, e_m_v, e_m_a, e_ceff, e_thrs, e_c2done)

        R = {
            "xa_free": None, "sin_free": None, "castA_free": None,
            "sqA_free": None, "b0_free": None, "b1_free": None,
            "b2_free": None, "b7_free": None,
            "zb_free": [None, None], "x1_free": [None, None],
            "g0s_free": [None, None],
            "att_free": [None, None], "pre_free": [None, None],
            "y_free": [None, None], "wt_roll": [None, None],
            "w1_roll": [None, None, None], "w2_roll": [None, None],
            "ot_dma": [None, None], "y0s_free": None, "act_free": None,
        }

        def front_a(ti):
            cs = ti * NT
            pz = ti % 2

            @blk.sync
            def _(sy):
                if ti == 0:
                    wt(sy, *gateB)
                wt(sy, R["xa_free"], R["sin_free"])
                for ch in range(DCH):
                    inc(sy.dma_start(out=xa[:, ch * C6:ch * C6 + C6],
                                     in_=x_fm[ch * P:(ch + 1) * P, cs:cs + C6]),
                        "sLD", 16)
                for ch in range(DCH):
                    inc(sy.dma_start(out=s_in[:, ch * NT:(ch + 1) * NT],
                                     in_=s_scr.ap()[ch, :, ti * NT:(ti + 1) * NT]),
                        "sLD", 16)
            e_x = evt("sLD")

            @blk.vector
            def _(v):
                if ti == 0:
                    wt(v, *gateB)
                wt(v, e_x, R["castA_free"])
                inc(v.tensor_copy(out=castA[:], in_=xa[:]), "sV", 1)
            e_cast = evt("sV")

            @blk.vector
            def _(v):
                if ti == 0:
                    wt(v, *gateB)
                wt(v, e_x, R["sqA_free"])
                inc(v.tensor_tensor(out=sqA[:], in0=xa[:], in1=xa[:],
                                    op=Op.mult), "sV", 1)
            e_sq = evt("sV")

            @blk.tensor
            def _(t):
                if ti == 0:
                    wt(t, *gateB)
                wt(t, e_cast, e_sq, R["b0_free"], R["b1_free"], R["b2_free"])
                for ch in range(DCH):
                    t.matmul(bank[0][:], sMean[:],
                             castA[:, ch * C6 + 2:ch * C6 + 2 + NT],
                             start=(ch == 0), stop=(ch == DCH - 1))
                for ch in range(DCH):
                    t.matmul(bank[1][:], sSq[:],
                             sqA[:, ch * C6 + 2:ch * C6 + 2 + NT],
                             start=(ch == 0), stop=(ch == DCH - 1))
                for ch in range(DCH):
                    t.matmul(bank[2][:, 0:2], sMean[:],
                             castA[:, ch * C6:ch * C6 + 2],
                             start=(ch == 0), stop=(ch == DCH - 1))
                for ch in range(DCH):
                    t.matmul(bank[2][:, 2:4], sMean[:],
                             castA[:, ch * C6 + 514:ch * C6 + 516],
                             start=(ch == 0), stop=(ch == DCH - 1))
                for ch in range(DCH):
                    t.matmul(bank[2][:, 4:6], sSq[:],
                             sqA[:, ch * C6:ch * C6 + 2],
                             start=(ch == 0), stop=(ch == DCH - 1))
                for ch in range(DCH):
                    t.matmul(bank[2][:, 6:8], sSq[:],
                             sqA[:, ch * C6 + 514:ch * C6 + 516],
                             start=(ch == 0), stop=(ch == DCH - 1))
                inc(t.matmul(bank[2][:, 128:384], sMean[:], castA[:, 0:256],
                             start=True, stop=True), "sPE", 1)
            e_stat1 = evt("sPE")

            @blk.vector
            def _(v):
                wt(v, e_stat1, R["ot_dma"][0], R["ot_dma"][1])
                v.tensor_copy(out=spacer[:], in_=iota_row[:])
                v.tensor_scalar(out=negm_s[:], in0=bank[0][:], scalar1=1.0,
                                scalar2=None, op0=Op.mult)
                v.tensor_copy(out=negmhW[:], in_=bank[2][:])
                v.tensor_tensor(out=var_s[:], in0=negm_s[:], in1=negm_s[:],
                                op=Op.mult)
                v.tensor_tensor(out=var_s[:], in0=bank[1][:], in1=var_s[:],
                                op=Op.subtract)
                v.tensor_tensor(out=hscW[:], in0=negmhW[:], in1=negmhW[:],
                                op=Op.mult)
                v.tensor_tensor(out=rhW[:, 0:NT - 4], in0=bank[2][:, 4:NT],
                                in1=hscW[:, 0:NT - 4], op=Op.subtract)
                inc(v.tensor_copy(out=spacer[:], in_=iota_row[:]), "sV", 1)
            e_var1 = evt("sV")
            e_negm1 = e_var1

            @blk.scalar
            def _(s):
                wt(s, e_var1)
                s.activation(out=rhW[:, 0:8], in_=rhW[:, 0:8], func=AF.Ln,
                             bias=eps_t[:])
                s.activation(out=r_s[:], in_=var_s[:], func=AF.Ln, bias=eps_t[:])
                s.activation(out=rhW[:, 0:8], in_=rhW[:, 0:8], func=AF.Exp,
                             scale=-0.5)
                inc(s.activation(out=r_s[:], in_=r_s[:], func=AF.Exp, scale=-0.5),
                    "sA", 1)
            e_rs1 = evt("sA")

            @blk.vector
            def _(v):
                wt(v, e_rs1)
                for ch in range(DCH):
                    v.tensor_tensor(out=h1tmp[:],
                                    in0=xa[:, ch * C6 + 2:ch * C6 + 2 + NT],
                                    in1=negm_s[:], op=Op.add)
                    v.tensor_tensor(out=h1b[:, ch * C6 + 2:ch * C6 + 2 + NT],
                                    in0=h1tmp[:], in1=r_s[:], op=Op.mult)
                    v.tensor_tensor(out=hsc[:, 0:2],
                                    in0=xa[:, ch * C6:ch * C6 + 2],
                                    in1=negmhW[:, 0:2], op=Op.add)
                    v.tensor_tensor(out=hsc[:, 2:4],
                                    in0=xa[:, ch * C6 + 514:ch * C6 + 516],
                                    in1=negmhW[:, 2:4], op=Op.add)
                    v.tensor_copy(out=spacer[:], in_=iota_row[:])
                    v.tensor_tensor(out=h1b[:, ch * C6:ch * C6 + 2],
                                    in0=hsc[:, 0:2], in1=rhW[:, 0:2], op=Op.mult)
                    v.tensor_tensor(out=h1b[:, ch * C6 + 514:ch * C6 + 516],
                                    in0=hsc[:, 2:4], in1=rhW[:, 2:4], op=Op.mult)
                inc(v.engine_nop(), "sV", 1)
            e_h1 = evt("sV")

            @blk.scalar
            def _(s):
                for ch in range(DCH):
                    s.activation(out=bct_all[:, ch:ch + 1],
                                 in_=small[:, 24 + ch:25 + ch],
                                 func=AF.Copy, scale=float(ti * NT + 1))
                inc(s.copy(out=spacerA[:], in_=iota_row[:]), "sA", 1)
            e_pw = [None, None]
            for ch in range(DCH):
                par = ch % 2

                @blk.scalar
                def _(s):
                    wt(s, e_pw[par])
                    if ch == 0:
                        wt(s, e_x)
                    inc(s.activation(out=pw_t[par][:], in_=iota_row[:],
                                     func=AF.Exp,
                                     scale=small[:, 24 + ch:25 + ch],
                                     bias=bct_all[:, ch:ch + 1]), "sA", 1)
                e_pwa = evt("sA")

                @blk.vector
                def _(v):
                    wt(v, e_pwa)
                    inc(v.scalar_tensor_tensor(
                        out=s_in[:, ch * NT:(ch + 1) * NT], in0=pw_t[par][:],
                        scalar=carry_eff[:, ch:ch + 1],
                        in1=s_in[:, ch * NT:(ch + 1) * NT],
                        op0=Op.mult, op1=Op.add), "sV", 1)
                e_pw[par] = evt("sV")
            e_scorr = evt("sV")
            return {"e_x": e_x, "e_cast": e_cast, "e_sq": e_sq,
                    "e_stat1": e_stat1, "e_negm1": e_negm1, "e_var1": e_var1,
                    "e_rs1": e_rs1, "e_h1": e_h1, "e_scorr": e_scorr}

        def front_b(ti, fa):
            pz = ti % 2
            e_attmm = [None, None]
            e_attev = [None, None]
            for e in range(DCH):
                par = e % 2

                @blk.sync
                def _(sy):
                    wt(sy, R["w1_roll"][par])
                    inc(sy.dma_start(out=wtbuf[par][:], in_=wt_scr.ap()[e]),
                        "sLD", 16)
                e_wtld = evt("sLD")

                @blk.tensor
                def _(t):
                    wt(t, e_wtld, e_attev[par])
                    if e == 0:
                        wt(t, fa["e_scorr"], fa["e_h1"], fa["e_negm1"])
                    if e == 1:
                        wt(t, fa["e_var1"])
                    for k_ in range(DCH):
                        t.matmul(bank[par][:], wtbuf[par][:, k_ * P:(k_ + 1) * P],
                                 s_in[:, k_ * NT:(k_ + 1) * NT],
                                 start=(k_ == 0), stop=False)
                    for k_ in range(5):
                        mm = t.matmul(bank[par][:],
                                      diag[:, (e * 5 + k_) * P:(e * 5 + k_ + 1) * P],
                                      h1b[:, e * C6 + k_:e * C6 + k_ + NT],
                                      start=False, stop=(k_ == 4))
                    inc(mm, "sPE", 1)
                e_attmm[par] = evt("sPE")
                R["w1_roll"][par] = e_attmm[par]

                @blk.vector
                def _(v):
                    wt(v, e_attmm[par])
                    if e == 0:
                        wt(v, R["x1_free"][pz])
                    inc(v.scalar_tensor_tensor(
                        out=x1[pz][:, e * NT:(e + 1) * NT], in0=bank[par][:],
                        scalar=small[:, 32 + e:33 + e],
                        in1=xa[:, e * C6 + 2:e * C6 + 2 + NT],
                        op0=Op.add, op1=Op.add), "sV", 1)
                e_attev[par] = evt("sV")
            e_x1done = evt("sV")
            R["xa_free"] = e_x1done
            R["sin_free"] = e_attmm[1]

            @blk.vector
            def _(v):
                wt(v, e_x1done)
                v.tensor_copy(out=castA[:, 0:DCH * NT], in_=x1[pz][:])
                inc(v.tensor_tensor(out=sqA[:, 0:DCH * NT], in0=x1[pz][:],
                                    in1=x1[pz][:], op=Op.mult), "sV", 1)
            e_x1b = evt("sV")
            e_sq2 = e_x1b

            @blk.tensor
            def _(t):
                wt(t, e_x1b, e_sq2, fa["e_rs1"], e_attev[0], e_attev[1])
                for ch in range(DCH):
                    t.matmul(bank[0][:], sMean[:],
                             castA[:, ch * NT:(ch + 1) * NT],
                             start=(ch == 0), stop=(ch == DCH - 1))
                for ch in range(DCH):
                    mm = t.matmul(bank[1][:], sSq[:],
                                  sqA[:, ch * NT:(ch + 1) * NT],
                                  start=(ch == 0), stop=(ch == DCH - 1))
                inc(mm, "sPE", 1)
            e_stat2 = evt("sPE")
            R["castA_free"] = e_stat2

            @blk.vector
            def _(v):
                wt(v, e_stat2, fa["e_h1"])
                v.tensor_scalar(out=negm_s[:], in0=bank[0][:], scalar1=1.0,
                                scalar2=None, op0=Op.mult)
                v.tensor_tensor(out=var_s[:], in0=negm_s[:], in1=negm_s[:],
                                op=Op.mult)
                inc(v.tensor_tensor(out=var_s[:], in0=bank[1][:], in1=var_s[:],
                                    op=Op.subtract), "sV", 1)
            e_var2 = evt("sV")
            e_negm2 = e_var2

            @blk.scalar
            def _(s):
                wt(s, e_var2)
                s.activation(out=r_s[:], in_=var_s[:], func=AF.Ln, bias=eps_t[:])
                inc(s.activation(out=r_s[:], in_=r_s[:], func=AF.Exp, scale=-0.5),
                    "sA", 1)
            e_rs2 = evt("sA")
            R["b1_free"] = e_var2

            @blk.vector
            def _(v):
                wt(v, e_rs2, R["zb_free"][pz])
                for ch in range(DCH):
                    v.tensor_tensor(out=h1tmp[:],
                                    in0=x1[pz][:, ch * NT:(ch + 1) * NT],
                                    in1=negm_s[:], op=Op.add)
                    v.tensor_tensor(out=zbuf[pz][:, ch * NT:(ch + 1) * NT],
                                    in0=h1tmp[:], in1=r_s[:], op=Op.mult)
                inc(v.engine_nop(), "sV", 1)
            e_zb = evt("sV")

            @blk.vector
            def _(v):
                wt(v, e_zb)
                inc(v.tensor_tensor(out=sqA[:, 0:DCH * NT], in0=zbuf[pz][:],
                                    in1=zbuf[pz][:], op=Op.mult), "sV", 1)
            e_sq3 = evt("sV")

            @blk.tensor
            def _(t):
                wt(t, e_zb, e_sq3, e_negm2)
                for ch in range(DCH):
                    mm = t.matmul(bank[0][:], sMean[:],
                                  zbuf[pz][:, ch * NT:(ch + 1) * NT],
                                  start=(ch == 0), stop=(ch == DCH - 1))
                inc(mm, "sPE", 1)
            e_stat3 = evt("sPE")

            @blk.vector
            def _(v):
                wt(v, e_stat3)
                v.tensor_copy(out=spacer[:], in_=iota_row[:])
                v.tensor_copy(out=nmzr[:], in_=bank[0][0:1, :])
                inc(v.tensor_copy(out=spacer[:], in_=iota_row[:]), "sV", 1)
            e_negm3 = evt("sV")
            R["b0_free"] = e_negm3

            @blk.tensor
            def _(t):
                wt(t, e_zb, e_negm3, R["b2_free"] or fa["e_var1"], fa["e_var1"])
                for g_ in range(4):
                    for ch in range(DCH):
                        t.matmul(bank[2][:, 8 * g_:8 * g_ + 8],
                                 zbuf[pz][:, ch * NT + g_ * P:ch * NT + (g_ + 1) * P],
                                 wr_b[:, 8 * ch:8 * ch + 8],
                                 start=(ch == 0), stop=False)
                    t.matmul(bank[2][:, 8 * g_:8 * g_ + 8],
                             nmzr[0:1, g_ * P:(g_ + 1) * P], c1r[:],
                             start=False, stop=True)
                for g_ in range(4):
                    for ch in range(DCH):
                        t.matmul(bank[2][:, 32 + g_:33 + g_],
                                 zbuf[pz][:, ch * NT + g_ * P:ch * NT + (g_ + 1) * P],
                                 sMean[:, 0:1],
                                 start=(ch == 0), stop=(ch == DCH - 1))
                for g_ in range(4):
                    for ch in range(DCH):
                        t.matmul(bank[2][:, 36 + g_:37 + g_],
                                 sqA[:, ch * NT + g_ * P:ch * NT + (g_ + 1) * P],
                                 sSq[:, 0:1],
                                 start=(ch == 0), stop=(ch == DCH - 1))
                inc(t.matmul(bank[2][:, 128:384], sMean[:], zbuf[pz][:, 256:512],
                             start=True, stop=True), "sPE", 1)
            e_logit = evt("sPE")
            R["sqA_free"] = e_logit

            @blk.vector
            def _(v):
                wt(v, e_logit)
                v.tensor_copy(out=spacer[:], in_=iota_row[:])
                v.tensor_copy(out=lg[:], in_=bank[2][:, 0:32])
                inc(v.tensor_copy(out=spacer[:], in_=iota_row[:]), "sV", 1)
            e_lg = evt("sV")

            @blk.vector
            def _(v):
                wt(v, e_lg)
                v.tensor_copy(out=spacer[:], in_=iota_row[:])
                v.tensor_copy(out=negmhW[:, 0:128], in_=bank[2][:, 0:128])
                v.tensor_copy(out=spacer[:], in_=iota_row[:])
                v.tensor_tensor(out=hscW[:, 0:128], in0=negmhW[:, 0:128],
                                in1=negmhW[:, 0:128], op=Op.mult)
                v.tensor_tensor(out=rzc[:], in0=bank[2][:, 36:40],
                                in1=hscW[:, 32:36], op=Op.subtract)
                inc(v.tensor_copy(out=spacer[:], in_=iota_row[:]), "sV", 1)
            e_rzv = evt("sV")
            R["b2_free"] = e_rzv

            @blk.scalar
            def _(s):
                wt(s, e_rzv)
                s.activation(out=rzc[:], in_=rzc[:], func=AF.Ln, bias=eps_t[:])
                s.activation(out=spacerA[:], in_=iota_row[:], func=AF.Ln,
                             bias=eps_t[:])
                s.activation(out=rzc[:], in_=rzc[:], func=AF.Exp, scale=-0.5)
                inc(s.activation(out=spacerA[:], in_=iota_row[:], func=AF.Exp,
                                 scale=-1.0), "sA", 1)
            e_rzc = evt("sA")

            @blk.vector
            def _(v):
                wt(v, e_rzc)
                for g_ in range(4):
                    v.tensor_scalar(out=lg[:, 8 * g_:8 * g_ + 8],
                                    in0=lg[:, 8 * g_:8 * g_ + 8],
                                    scalar1=rzc[:, g_:g_ + 1],
                                    scalar2=None, op0=Op.mult)
                inc(v.engine_nop(), "sV", 1)
            e_rzap = evt("sV")

            @blk.vector
            def _(v):
                wt(v, e_rzap)
                for g_ in range(4):
                    v.tensor_tensor(out=lg[:, 8 * g_:8 * g_ + 8],
                                    in0=lg[:, 8 * g_:8 * g_ + 8], in1=c2bc[:],
                                    op=Op.add)
                    v.tensor_reduce(out=sm[:, g_:g_ + 1],
                                    in_=lg[:, 8 * g_:8 * g_ + 8],
                                    axis=mybir.AxisListType.X, op=Op.max,
                                    negate=True)
                inc(v.tensor_copy(out=spacer[:], in_=iota_row[:]), "sV", 1)
            e_nm = evt("sV")

            @blk.scalar
            def _(s):
                wt(s, e_nm)
                s.activation(out=spacerA[:], in_=iota_row[:], func=AF.Exp,
                             scale=-1.0)
                for g_ in range(4):
                    ex = s.activation(out=lg[:, 8 * g_:8 * g_ + 8],
                                      in_=lg[:, 8 * g_:8 * g_ + 8], func=AF.Exp,
                                      bias=sm[:, g_:g_ + 1])
                inc(ex, "sA", 1)
            e_exp = evt("sA")

            @blk.vector
            def _(v):
                wt(v, e_exp)
                v.tensor_copy(out=spacer[:], in_=iota_row[:])
                for g_ in range(4):
                    v.tensor_reduce(out=sm[:, 4 + g_:5 + g_],
                                    in_=lg[:, 8 * g_:8 * g_ + 8],
                                    axis=mybir.AxisListType.X, op=Op.add)
                v.tensor_copy(out=spacer[:], in_=iota_row[:])
                v.reciprocal(out=sm[:, 8:12], in_=sm[:, 4:8])
                v.tensor_copy(out=spacer[:], in_=iota_row[:])
                for g_ in range(4):
                    v.tensor_tensor(out=sm[:, 12 + g_:13 + g_],
                                    in0=lg[:, 8 * g_:8 * g_ + 1],
                                    in1=sm[:, 8 + g_:9 + g_], op=Op.mult)
                inc(v.tensor_copy(out=spacer[:], in_=iota_row[:]), "sV", 1)
            e_soft = evt("sV")

            @blk.tensor
            def _(t):
                wt(t, e_soft, R["b7_free"])
                t.transpose(bank[2][:, 128:256], ident[:], ident[:])
                for g_ in range(4):
                    tr = t.transpose(bank[7][0:1, g_ * P:(g_ + 1) * P],
                                     sm[:, 12 + g_:13 + g_], ident[:])
                inc(tr, "sPE", 1)
            e_p0T = evt("sPE")

            @blk.vector
            def _(v):
                wt(v, e_p0T)
                v.tensor_copy(out=spacer[:], in_=iota_row[:])
                v.tensor_copy(out=g0r[:], in_=bank[7][0:1, :])
                inc(v.tensor_copy(out=spacer[:], in_=iota_row[:]), "sV", 1)
            e_g0r = evt("sV")

            @blk.tensor
            def _(t):
                wt(t, e_g0r)
                inc(t.matmul(bank[7][:], ones1b[:], g0r[:], start=True,
                             stop=True), "sPE", 1)
            e_g0bc = evt("sPE")

            @blk.vector
            def _(v):
                wt(v, e_g0bc, R["g0s_free"][pz])
                inc(v.tensor_scalar(out=g0s[pz][:], in0=bank[7][:],
                                    scalar1=1.0, scalar2=None, op0=Op.mult),
                    "sV", 1)
            e_g0sE = evt("sV")
            R["b7_free"] = e_g0sE
            return {"e_zb": e_zb, "e_g0sE": e_g0sE, "e_x1done": e_x1done}

        def back_pre(ti, exp, fb):
            pz = ti % 2
            wsrc = w1_scr if exp == 0 else pw1_scr
            bcol = b1c if exp == 0 else pb1c
            e_pre = {}
            e_actstt = {}
            for f in range(FCH):
                par = f % 2
                p4 = f % 3

                @blk.sync
                def _(sy):
                    wt(sy, R["w1_roll"][p4])
                    inc(sy.dma_start(out=wbuf[p4][:], in_=wsrc.ap()[f]),
                        "sLD", 16)
                e_wld = evt("sLD")

                @blk.tensor
                def _(t):
                    wt(t, e_wld, R["pre_free"][par])
                    if f == 0:
                        wt(t, fb["e_zb"])
                        if exp == 0:
                            wt(t, R["act_free"])
                    mm = None
                    for k_ in range(DCH):
                        mm = t.matmul(bank[5 + par][:],
                                      wbuf[p4][:, k_ * P:(k_ + 1) * P],
                                      zbuf[pz][:, k_ * NT:(k_ + 1) * NT],
                                      start=(k_ == 0), stop=(k_ == DCH - 1))
                    inc(mm, "sPE", 1)
                e_pre[f] = evt("sPE")
                R["w1_roll"][p4] = e_pre[f]

                @blk.scalar
                def _(s):
                    wt(s, e_pre[f])
                    inc(s.activation(out=act_sb[:, f * NT:(f + 1) * NT],
                                     in_=bank[5 + par][:],
                                     func=AF.Gelu_apprx_tanh,
                                     bias=bcol[:, f:f + 1]), "sA", 1)
                e_actstt[f] = evt("sA")
                R["pre_free"][par] = e_actstt[f]
            return {"e_pre": e_pre, "e_actstt": e_actstt}

        def back_y(ti, exp, fb, pre_ev):
            pz = ti % 2
            w2src = w2_scr if exp == 0 else pw2_scr
            e_yfree = {}
            for d in range(DCH):
                par = d % 2

                @blk.sync
                def _(sy):
                    wt(sy, R["w2_roll"][par])
                    inc(sy.dma_start(
                        out=w2buf[2 * par + 0][:],
                        in_=w2src.ap()[d, :, 0:2048]), "sLD", 16)
                    inc(sy.dma_start(
                        out=w2buf[2 * par + 1][:],
                        in_=w2src.ap()[d, :, 2048:4096]), "sLD", 16)
                e_w2ld = evt("sLD")

                @blk.tensor
                def _(t):
                    wt(t, e_w2ld, R["y_free"][par])
                    if d == 0:
                        wt(t, pre_ev["e_actstt"][FCH - 1])
                    mm = None
                    for f in range(FCH):
                        mm = t.matmul(
                            bank[3 + par][:],
                            w2buf[2 * par + f // 16][:, (f % 16) * P:
                                                     (f % 16 + 1) * P],
                            act_sb[:, f * NT:(f + 1) * NT],
                            start=(f == 0), stop=(f == FCH - 1))
                    inc(mm, "sPE", 1)
                e_y = evt("sPE")
                R["w2_roll"][par] = e_y

                if exp == 0:
                    @blk.vector
                    def _(v):
                        wt(v, e_y)
                        if d == 0:
                            wt(v, R["y0s_free"])
                        inc(v.tensor_scalar(out=y0s[:, d * NT:(d + 1) * NT],
                                            in0=bank[3 + par][:],
                                            scalar1=b2c[:, d:d + 1],
                                            scalar2=None, op0=Op.add),
                            "sV", 1)
                    e_yfree[(exp, d)] = evt("sV")
                else:
                    @blk.vector
                    def _(v):
                        wt(v, e_y, fb["e_g0sE"], R["ot_dma"][par])
                        v.tensor_scalar(out=h1tmp[:], in0=bank[3 + par][:],
                                        scalar1=pb2c[:, d:d + 1],
                                        scalar2=None, op0=Op.add)
                        v.tensor_tensor(out=qs[:],
                                        in0=y0s[:, d * NT:(d + 1) * NT],
                                        in1=h1tmp[:], op=Op.subtract)
                        v.tensor_tensor(out=qs[:], in0=qs[:], in1=g0s[pz][:],
                                        op=Op.mult)
                        v.tensor_tensor(out=qs[:], in0=qs[:],
                                        in1=h1tmp[:], op=Op.add)
                        inc(v.tensor_tensor(out=otile[par][:], in0=qs[:],
                                            in1=x1[pz][:, d * NT:(d + 1) * NT],
                                            op=Op.add), "sV", 1)
                    e_yfree[(exp, d)] = evt("sV")

                    @blk.gpsimd
                    def _(g):
                        wt(g, e_yfree[(exp, d)])
                        inc(g.dma_start(
                            out=out_fm[d * P:(d + 1) * P,
                                       ti * NT:(ti + 1) * NT],
                            in_=otile[par][:]), "sST", 16)
                    R["ot_dma"][par] = evt("sST")
                R["y_free"][par] = e_yfree[(exp, d)]
            return e_yfree

        fa = front_a(0)
        fb = front_b(0, fa)
        fronts = {0: fb}
        for t_i in range(NTILES):
            fbt = fronts[t_i]
            pre0 = back_pre(t_i, 0, fbt)
            fa2 = None
            if t_i + 1 < NTILES:
                fa2 = front_a(t_i + 1)
            y0ev = back_y(t_i, 0, fbt, pre0)
            pre1 = back_pre(t_i, 1, fbt)
            if t_i + 1 < NTILES:
                fb2 = front_b(t_i + 1, fa2)
                fronts[t_i + 1] = fb2
            y1ev = back_y(t_i, 1, fbt, pre1)
            pz = t_i % 2
            R["zb_free"][pz] = pre1["e_pre"][FCH - 1]
            R["x1_free"][pz] = y1ev[(1, DCH - 1)]
            R["g0s_free"][pz] = y1ev[(1, DCH - 1)]
            R["y0s_free"] = y1ev[(1, DCH - 1)]
            R["act_free"] = y1ev[(1, DCH - 1)]

        @blk.gpsimd
        def _(g):
            g.wait_ge(sems["sST"], cnt["sST"])

    return nc


_NC_CACHE = {}


def _get_nc():
    if "nc" not in _NC_CACHE:
        _NC_CACHE["nc"] = build_program()
    return _NC_CACHE["nc"]


def _chunk_layout(v):
    return np.ascontiguousarray(v.reshape(DCH, P).T)


def kernel(L0_attn_x, L0_attn_decay_logit, L0_attn_W, L0_attn_b, L0_attn_conv_w,
           L0_attn_conv_b, L0_ffn_W1, L0_ffn_b1, L0_ffn_W2, L0_ffn_b2,
           L0_ffn_ln_gamma, L0_ffn_ln_beta, L0_ffn_Wr, L0_ffn_br):
    x = np.asarray(L0_attn_x, dtype=np.float32)
    B, T, _ = x.shape
    W1 = np.ascontiguousarray(np.asarray(L0_ffn_W1, dtype=np.float32))
    W2 = np.ascontiguousarray(np.asarray(L0_ffn_W2, dtype=np.float32))
    b1 = np.asarray(L0_ffn_b1, dtype=np.float32)
    b2 = np.asarray(L0_ffn_b2, dtype=np.float32)
    Wr = np.asarray(L0_ffn_Wr, dtype=np.float32)
    br = np.asarray(L0_ffn_br, dtype=np.float32)
    Wat = np.ascontiguousarray(np.asarray(L0_attn_W, dtype=np.float32))
    dec = np.asarray(L0_attn_decay_logit, dtype=np.float32)
    batt = np.asarray(L0_attn_b, dtype=np.float32)
    cw = np.asarray(L0_attn_conv_w, dtype=np.float32).reshape(D, 5)
    cb = np.asarray(L0_attn_conv_b, dtype=np.float32)
    gam = np.asarray(L0_ffn_ln_gamma, dtype=np.float32)
    bet = np.asarray(L0_ffn_ln_beta, dtype=np.float32)

    w1flat = W1.reshape(P, 32768)
    w2flat = W2.reshape(P, 32768)
    b1tile = np.ascontiguousarray(
        np.broadcast_to(b1.reshape(1, F_FF), (1024, F_FF))).reshape(P, 32768)
    b2tile = np.ascontiguousarray(
        np.broadcast_to(b2.reshape(1, D), (4096, D))).reshape(P, 32768)
    bis_datas = [w1flat, w1flat, w2flat, w2flat, b1tile, b1tile, b2tile, b2tile]
    bis_ks = [float(K_W), float(N_W - K_W + 1),
              float(K_W), float(N_W - K_W + 1),
              float(1024 * K_B1), float(1024 * (F_FF - K_B1 + 1)),
              float(4096 * K_B2), float(4096 * (D - K_B2 + 1))]

    convw_l = np.ascontiguousarray(
        cw.reshape(DCH, P, 5).transpose(1, 0, 2).reshape(P, 5 * DCH))
    wr_l = np.ascontiguousarray(
        Wr.reshape(DCH, P, 8).transpose(1, 0, 2).reshape(P, 8 * DCH))

    in_maps = []
    for c in range(8):
        b_, h_ = c // 2, c % 2
        xf = np.zeros((D, TCORE + 4), dtype=np.float32)
        lo = h_ * TCORE - 2
        hi = h_ * TCORE + TCORE + 2
        slo, shi = max(lo, 0), min(hi, T)
        xf[:, (slo - lo):(slo - lo) + (shi - slo)] = x[b_, slo:shi, :].T
        in_maps.append({
            "x_fm": np.ascontiguousarray(xf),
            "bis_data": np.ascontiguousarray(bis_datas[c]),
            "bis_k": np.full((P, 1), bis_ks[c], dtype=np.float32),
            "oddflag": np.full((P, 1), float(h_), dtype=np.float32),
            "w_attn": Wat,
            "dec_in": _chunk_layout(dec),
            "battn_in": _chunk_layout(batt),
            "convb_in": _chunk_layout(cb),
            "convw_in": convw_l,
            "w1_in": W1,
            "b1c_in": np.ascontiguousarray(b1.reshape(FCH, P).T),
            "w2_in": W2,
            "b2c_in": np.ascontiguousarray(b2.reshape(DCH, P).T),
            "gam_in": _chunk_layout(gam),
            "beta_in": _chunk_layout(bet),
            "wr_in": wr_l,
            "br_in": br.reshape(1, 8),
        })

    nc = _get_nc()
    res = run_bass_kernel_spmd(nc, in_maps, core_ids=list(range(8)))
    out = np.empty((B, T, D), dtype=np.float32)
    for c in range(8):
        b_, h_ = c // 2, c % 2
        out[b_, h_ * TCORE:(h_ + 1) * TCORE, :] = res.results[c]["out_fm"].T
    return out


# version 17
# speedup vs baseline: 1.1506x; 1.1506x over previous
import numpy as np
import contextlib
import concourse.bass as bass
import concourse.mybir as mybir
from concourse.bass_utils import run_bass_kernel_spmd
from concourse.alu_op_type import AluOpType as Op

F32 = mybir.dt.float32
BF16 = mybir.dt.bfloat16
AF = mybir.ActivationFunctionType

P = 128
D = 1024
DCH = 8
F_FF = 4096
FCH = 32
TCORE = 4096
NT = 512
NTILES = TCORE // NT
C6 = NT + 4
N_BIS = 36
GC1 = 0.7978845608028654
GC2 = GC1 * 0.044715
EPS = 1e-5

N_W = D * F_FF
K_W = int(N_W * 0.1 / 2)
K_B1 = max(1, int(F_FF * 0.1 / 2))
K_B2 = max(1, int(D * 0.1 / 2))


def build_program(debug=False, dbg_tile=0):
    nc = bass.Bass("TRN2", target_bir_lowering=False)

    def din(name, shape, dt=F32):
        return nc.dram_tensor(name, shape, dt, kind="ExternalInput").ap()

    x_fm = din("x_fm", [D, TCORE + 4])
    bis_data = din("bis_data", [P, 32768])
    bis_k = din("bis_k", [P, 1])
    oddflag = din("oddflag", [P, 1])
    w_attn = din("w_attn", [D, D])
    dec_in = din("dec_in", [P, DCH])
    battn_in = din("battn_in", [P, DCH])
    convb_in = din("convb_in", [P, DCH])
    convw_in = din("convw_in", [P, 5 * DCH])
    w1_in = din("w1_in", [D, F_FF])
    b1c_in = din("b1c_in", [P, FCH])
    w2_in = din("w2_in", [F_FF, D])
    b2c_in = din("b2c_in", [P, DCH])
    gam_in = din("gam_in", [P, DCH])
    beta_in = din("beta_in", [P, DCH])
    wr_in = din("wr_in", [P, 8 * DCH])
    br_in = din("br_in", [1, 8])

    out_fm = nc.dram_tensor("out_fm", [D, TCORE], F32, kind="ExternalOutput").ap()
    dbg = {}
    if debug:
        for nm, shape, dt in [
            ("dbg_small", [P, 64], F32), ("dbg_bis", [P, 8], F32),
            ("dbg_thrs", [P, 8], F32), ("dbg_carry", [P, 16], F32),
            ("dbg_sin", [P, DCH * NT], BF16), ("dbg_h1b", [P, DCH * (NT + 4)], BF16),
            ("dbg_x1", [P, DCH * NT], F32), ("dbg_zb", [P, DCH * NT], BF16),
            ("dbg_lg", [P, 32], F32), ("dbg_sm", [P, 16], F32),
            ("dbg_g0s", [P, NT], F32), ("dbg_y0", [P, DCH * NT], F32),
            ("dbg_stats", [P, 3 * NT], F32), ("dbg_act", [P, 2 * NT], BF16),
            ("dbg_bis0", [P, 8], F32), ("dbg_bisf", [P, 8], F32),
            ("dbg_trc", [P, 3 * N_BIS], F32), ("dbg_aw", [P, 96], F32),
            ("dbg_b2", [P, 16], F32),
        ]:
            dbg[nm] = nc.dram_tensor(nm, shape, dt, kind="ExternalOutput").ap()

    s_scr = nc.dram_tensor("s_scr", [DCH, P, TCORE], BF16)
    wt_scr = nc.dram_tensor("wt_scr", [DCH, P, D], BF16)
    w1_scr = nc.dram_tensor("w1_scr", [FCH, P, D], BF16)
    pw1_scr = nc.dram_tensor("pw1_scr", [FCH, P, D], BF16)
    w2_scr = nc.dram_tensor("w2_scr", [DCH, P, F_FF], BF16)
    pw2_scr = nc.dram_tensor("pw2_scr", [DCH, P, F_FF], BF16)
    carry_in_d = nc.dram_tensor("carry_in_d", [P, DCH], F32)
    carry_out_d = nc.dram_tensor("carry_out_d", [2 * P, DCH], F32)
    thr_in_d = nc.dram_tensor("thr_in_d", [1, 16], F32)
    thr_out_d = nc.dram_tensor("thr_out_d", [8, 16], F32)

    es = contextlib.ExitStack()
    with es:
        blk = es.enter_context(nc.Block())
        sems = {}
        cnt = {}
        for s in ("sLD", "sLB", "sWQ", "sV", "sA", "sPE", "sPO", "sCC", "sST"):
            sems[s] = es.enter_context(nc.semaphore(s))
            cnt[s] = 0

        def inc(inst, s, n):
            inst.then_inc(sems[s], n)
            cnt[s] += n
            return (s, cnt[s])

        def evt(s):
            return (s, cnt[s])

        def wt(eng, *evts):
            for e in evts:
                if e is None:
                    continue
                s, v = e
                if v > 0:
                    eng.wait_ge(sems[s], v)

        def sb(name, shape, dt=F32):
            return es.enter_context(nc.sbuf_tensor(name, shape, dt))

        ident = sb("ident", [P, P])
        ident1 = sb("ident1", [1, 1])
        ones_row = sb("ones_row", [1, NT], BF16)
        ones1b = sb("ones1b", [1, P], BF16)
        ones1f = sb("ones1f", [1, P])
        ones128f = sb("ones128f", [P, P])
        sMean = sb("sMean", [P, P], BF16)
        sSq = sb("sSq", [P, P], BF16)
        iota_row = sb("iota_row", [P, NT])
        eps_t = sb("eps_t", [P, 1])
        small = sb("small", [P, 64])
        convw = sb("convw", [P, 5 * DCH])
        convb_f = sb("convb_f", [P, DCH])
        gam_b = sb("gam_b", [P, DCH], BF16)
        beta_b = sb("beta_b", [P, DCH], BF16)
        wr_f = sb("wr_f", [P, 8 * DCH])
        wr_b = sb("wr_b", [P, 8 * DCH], BF16)
        br_f = sb("br_f", [1, 8])
        br_b = sb("br_b", [1, 8], BF16)
        c1r = sb("c1r", [1, 8], BF16)
        c2row_b = sb("c2row_b", [1, 8], BF16)
        c2bc = sb("c2bc", [P, 8])
        diag = sb("diag", [P, 40 * P], BF16)
        carry_state = sb("carry_state", [P, DCH])
        carry_eff = sb("carry_eff", [P, DCH])
        thrs = sb("thrs", [P, 8])
        thr_rows = sb("thr_rows", [1, 8])
        thrpad = sb("thrpad", [1, 16])
        spacer = sb("spacer", [P, NT])
        spacerA = spacer

        bank = [es.enter_context(nc.psum_tensor(f"bank{i}", [P, NT], F32))
                for i in range(8)]

        @blk.sync
        def _(sy):
            inc(sy.dma_start(out=small[:, 0:8], in_=dec_in), "sLD", 16)
            inc(sy.dma_start(out=small[:, 32:40], in_=battn_in), "sLD", 16)
            inc(sy.dma_start(out=convb_f[:], in_=convb_in), "sLD", 16)
            inc(sy.dma_start(out=convw[:], in_=convw_in), "sLD", 16)
            inc(sy.dma_start(out=small[:, 40:48], in_=gam_in), "sLD", 16)
            inc(sy.dma_start(out=small[:, 48:56], in_=beta_in), "sLD", 16)
            inc(sy.dma_start(out=small[:, 56:57], in_=bis_k), "sLD", 16)
            inc(sy.dma_start(out=small[:, 57:58], in_=oddflag), "sLD", 16)
            inc(sy.dma_start(out=wr_f[:], in_=wr_in), "sLD", 16)
            inc(sy.dma_start(out=br_f[:], in_=br_in), "sLD", 16)
        e_small = evt("sLD")

        @blk.gpsimd
        def _(g):
            g.memset(ones_row[:], 1.0)
            g.memset(ones1b[:], 1.0)
            g.memset(ones1f[:], 1.0)
            g.memset(ones128f[:], 1.0)
            g.memset(sMean[:], -1.0 / D)
            g.memset(sSq[:], 1.0 / D)
            g.memset(eps_t[:], EPS)
            g.memset(ident1[:], 1.0)
            g.memset(thrpad[:], 0.0)
            g.memset(carry_state[:], 0.0)
            g.iota(out=ident[:], pattern=[[1, P]], base=0, channel_multiplier=-1,
                   allow_small_or_imprecise_dtypes=True)
            inc(g.iota(out=iota_row[:], pattern=[[1, NT]], base=0,
                       channel_multiplier=0,
                       allow_small_or_imprecise_dtypes=True), "sPO", 1)
        e_const = evt("sPO")

        @blk.scalar
        def _(s):
            wt(s, e_small)
            s.activation(out=small[:, 8:16], in_=small[:, 0:8], func=AF.Sigmoid)
            s.activation(out=small[:, 16:24], in_=small[:, 0:8], func=AF.Sigmoid,
                         scale=-1.0)
            s.copy(out=spacerA[:], in_=iota_row[:])
            inc(s.activation(out=small[:, 24:32], in_=small[:, 8:16], func=AF.Ln),
                "sA", 1)
        e_sig = evt("sA")

        @blk.vector
        def _(v):
            wt(v, e_const, e_sig)
            v.tensor_scalar(out=ident[:], in0=ident[:], scalar1=0.0, scalar2=None,
                            op0=Op.is_equal)
            v.tensor_tensor(out=small[:, 32:40], in0=small[:, 32:40],
                            in1=convb_f[:], op=Op.add)
            v.tensor_copy(out=gam_b[:], in_=small[:, 40:48])
            v.tensor_copy(out=beta_b[:], in_=small[:, 48:56])
            v.tensor_copy(out=br_b[:], in_=br_f[:])
            for ch in range(DCH):
                v.tensor_scalar(out=wr_b[:, 8 * ch:8 * ch + 8],
                                in0=wr_f[:, 8 * ch:8 * ch + 8],
                                scalar1=small[:, 40 + ch:41 + ch], scalar2=None,
                                op0=Op.mult)
            for j in range(40):
                v.tensor_scalar(out=diag[:, j * P:(j + 1) * P], in0=ident[:],
                                scalar1=convw[:, j:j + 1], scalar2=None,
                                op0=Op.mult)
            inc(v.engine_nop(), "sV", 1)
        e_prolog = evt("sV")

        @blk.tensor
        def _(t):
            wt(t, e_prolog)
            for ch in range(DCH):
                t.matmul(bank[1][0:1, 0:8], gam_b[:, ch:ch + 1],
                         wr_b[:, 8 * ch:8 * ch + 8],
                         start=(ch == 0), stop=(ch == DCH - 1))
            for ch in range(DCH):
                t.matmul(bank[1][0:1, 8:16], beta_b[:, ch:ch + 1],
                         wr_b[:, 8 * ch:8 * ch + 8],
                         start=(ch == 0), stop=False)
            t.matmul(bank[1][0:1, 8:16], ones1b[0:1, 0:1], br_b[:],
                     start=False, stop=True)
            inc(t.matmul(bank[1][0:64, 128:256], wr_b[:, 0:64], diag[:, 0:P],
                         start=True, stop=True), "sPE", 1)
        e_c12 = evt("sPE")

        @blk.scalar
        def _(s):
            wt(s, e_c12)
            s.copy(out=spacerA[:], in_=iota_row[:])
            s.copy(out=c1r[:], in_=bank[1][0:1, 0:8])
            s.copy(out=c2row_b[:], in_=bank[1][0:1, 8:16])
            inc(s.copy(out=spacerA[:], in_=iota_row[:]), "sA", 1)
        e_c12s = evt("sA")

        @blk.tensor
        def _(t):
            wt(t, e_c12s)
            t.transpose(bank[2][:, 128:256], ident[:], ident[:])
            inc(t.matmul(bank[2][:, 0:8], ones1b[:], c2row_b[:],
                         start=True, stop=True), "sPE", 1)
        e_c2bcp = evt("sPE")

        @blk.scalar
        def _(s):
            wt(s, e_c2bcp)
            s.copy(out=spacerA[:], in_=iota_row[:])
            s.copy(out=c2bc[:], in_=bank[2][:, 0:8])
            inc(s.copy(out=spacerA[:], in_=iota_row[:]), "sA", 1)
        e_c2done = evt("sA")

        with contextlib.ExitStack() as es_s:
            aw = es_s.enter_context(nc.sbuf_tensor("aw", [P, 32768], F32))
            trc = es_s.enter_context(nc.sbuf_tensor("trc", [P, 3 * N_BIS], F32)) \
                if debug else None
            dump = es_s.enter_context(nc.sbuf_tensor("dump", [P, 8192], BF16))
            dump2 = es_s.enter_context(nc.sbuf_tensor("dump2", [P, 8192], BF16))
            loW = es_s.enter_context(nc.sbuf_tensor("loW", [P, NT], F32))
            hiW = es_s.enter_context(nc.sbuf_tensor("hiW", [P, NT], F32))
            midW = es_s.enter_context(nc.sbuf_tensor("midW", [P, NT], F32))
            totW = es_s.enter_context(nc.sbuf_tensor("totW", [P, NT], F32))
            prW = es_s.enter_context(nc.sbuf_tensor("prW", [P, NT], F32))
            pr2W = es_s.enter_context(nc.sbuf_tensor("pr2W", [P, NT], F32))
            t6W = es_s.enter_context(nc.sbuf_tensor("t6W", [P, NT], F32))
            t7W = es_s.enter_context(nc.sbuf_tensor("t7W", [P, NT], F32))
            kW = es_s.enter_context(nc.sbuf_tensor("kW", [P, NT], F32))
            bis = es_s.enter_context(nc.sbuf_tensor("bis", [P, 8], F32))
            maxrow = es_s.enter_context(nc.sbuf_tensor("maxrow", [1, P], F32))
            maxsc = es_s.enter_context(nc.sbuf_tensor("maxsc", [1, 1], F32))
            maxb = es_s.enter_context(nc.sbuf_tensor("maxb", [1, 1], BF16))

            @blk.sync
            def _(sy):
                for q in range(4):
                    inc(sy.dma_start(out=aw[:, q * 8192:(q + 1) * 8192],
                                     in_=bis_data[:, q * 8192:(q + 1) * 8192]),
                        "sLD", 16)
            e_bld = evt("sLD")

            @blk.scalar
            def _(s):
                wt(s, e_bld)
                s.activation(out=aw[:, 0:16384], in_=aw[:, 0:16384], func=AF.Abs)
                inc(s.activation(out=aw[:, 16384:32768], in_=aw[:, 16384:32768],
                                 func=AF.Abs), "sA", 1)
            e_abs = evt("sA")

            @blk.vector
            def _(v):
                wt(v, e_abs)
                for q in range(4):
                    v.tensor_reduce(out=bis[:, 3 + q:4 + q],
                                    in_=aw[:, q * 8192:(q + 1) * 8192],
                                    axis=mybir.AxisListType.X, op=Op.max)
                v.tensor_copy(out=spacer[:], in_=iota_row[:])
                v.tensor_tensor(out=bis[:, 6:7], in0=bis[:, 3:4], in1=bis[:, 4:5],
                                op=Op.max)
                v.tensor_tensor(out=bis[:, 6:7], in0=bis[:, 6:7], in1=bis[:, 5:6],
                                op=Op.max)
                v.tensor_copy(out=spacer[:], in_=iota_row[:])
                v.tensor_tensor(out=bis[:, 6:7], in0=bis[:, 6:7], in1=bis[:, 3 + 3:4 + 3],
                                op=Op.max)
                inc(v.tensor_copy(out=spacer[:], in_=iota_row[:]), "sV", 1)
            e_pmax = evt("sV")

            if debug:
                @blk.gpsimd
                def _(g):
                    wt(g, e_pmax)
                    inc(g.dma_start(out=dbg["dbg_aw"][:, 0:32],
                                    in_=aw[:, 16368:16400]), "sWQ", 16)
                    inc(g.dma_start(out=dbg["dbg_aw"][:, 32:64],
                                    in_=aw[:, 32700:32732]), "sWQ", 16)
                    inc(g.dma_start(out=dbg["dbg_aw"][:, 64:96],
                                    in_=aw[:, 24000:24032]), "sWQ", 16)

            @blk.vector
            def _(v):
                v.tensor_tensor(out=bis[:, 5:6], in0=bis[:, 6:7],
                                in1=bis[:, 6:7], op=Op.mult)
                inc(v.tensor_copy(out=spacer[:], in_=iota_row[:]), "sV", 1)
            e_pm2 = evt("sV")

            @blk.tensor
            def _(t):
                wt(t, e_pm2)
                inc(t.matmul(bank[3][:, 0:1], ones128f[:], bis[:, 5:6],
                             start=True, stop=True), "sPE", 1)
            e_pmsum = evt("sPE")

            @blk.scalar
            def _(s):
                wt(s, e_pmsum)
                s.copy(out=spacerA[:], in_=iota_row[:])
                s.activation(out=bis[:, 1:2], in_=bank[3][:, 0:1], func=AF.Sqrt)
                inc(s.copy(out=spacerA[:], in_=iota_row[:]), "sA", 1)
            e_gmax = evt("sA")

            @blk.vector
            def _(v):
                wt(v, e_gmax)
                v.tensor_copy(out=spacer[:], in_=iota_row[:])
                v.tensor_scalar(out=hiW[:], in0=iota_row[:], scalar1=0.0,
                                scalar2=bis[:, 1:2], op0=Op.mult, op1=Op.add)
                v.tensor_scalar(out=hiW[:], in0=hiW[:], scalar1=1.05,
                                scalar2=None, op0=Op.mult)
                v.memset(loW[:], -1.0)
                v.tensor_scalar(out=kW[:], in0=iota_row[:], scalar1=0.0,
                                scalar2=small[:, 56:57], op0=Op.mult, op1=Op.add)
                v.tensor_scalar(out=kW[:], in0=kW[:], scalar1=1.0 / D,
                                scalar2=None, op0=Op.mult)
                v.tensor_tensor(out=midW[:], in0=loW[:], in1=hiW[:], op=Op.add)
                inc(v.tensor_scalar(out=midW[:], in0=midW[:], scalar1=0.5,
                                    scalar2=None, op0=Op.mult), "sV", 1)

            dumps = [dump, dump2]
            peU = [None, None]
            e_postR = None
            for it in range(N_BIS):
                e_cmp = []
                for q in range(4):
                    @blk.vector
                    def _(v):
                        wt(v, peU[q % 2])
                        inc(v.tensor_scalar(out=dumps[q % 2][:],
                                            in0=aw[:, q * 8192:(q + 1) * 8192],
                                            scalar1=midW[:, 0:1], scalar2=None,
                                            op0=Op.is_le), "sV", 1)
                    e_cmp.append(evt("sV"))
                for q in range(4):
                    @blk.tensor
                    def _(t):
                        wt(t, e_cmp[q])
                        if q == 0:
                            wt(t, e_postR)
                        mm = None
                        for c in range(16):
                            mm = t.matmul(bank[3][:, 0:NT], sSq[:],
                                          dumps[q % 2][:, c * NT:(c + 1) * NT],
                                          start=(q == 0 and c == 0),
                                          stop=(q == 3 and c == 15))
                        inc(mm, "sPE", 1)
                    peU[q % 2] = evt("sPE")
                e_tt = peU[1]

                @blk.vector
                def _(v):
                    wt(v, e_tt)
                    v.tensor_copy(out=spacer[:], in_=iota_row[:])
                    v.tensor_scalar(out=t6W[:], in0=bank[3][:, 0:NT],
                                    scalar1=1.0, scalar2=0.0, op0=Op.mult,
                                    op1=Op.add, accum_out=bis[:, 3:4])
                    v.tensor_copy(out=spacer[:], in_=iota_row[:])
                    v.tensor_scalar(out=totW[:], in0=iota_row[:], scalar1=0.0,
                                    scalar2=bis[:, 3:4], op0=Op.mult,
                                    op1=Op.add)
                    v.tensor_tensor(out=prW[:], in0=totW[:], in1=kW[:],
                                    op=Op.is_lt)
                    v.tensor_scalar(out=pr2W[:], in0=prW[:], scalar1=-1.0,
                                    scalar2=1.0, op0=Op.mult, op1=Op.add)
                    v.tensor_tensor(out=t6W[:], in0=midW[:], in1=loW[:],
                                    op=Op.subtract)
                    v.tensor_tensor(out=t6W[:], in0=t6W[:], in1=prW[:],
                                    op=Op.mult)
                    v.tensor_tensor(out=loW[:], in0=loW[:], in1=t6W[:],
                                    op=Op.add)
                    v.tensor_tensor(out=t7W[:], in0=midW[:], in1=hiW[:],
                                    op=Op.subtract)
                    v.tensor_tensor(out=t7W[:], in0=t7W[:], in1=pr2W[:],
                                    op=Op.mult)
                    v.tensor_tensor(out=hiW[:], in0=hiW[:], in1=t7W[:],
                                    op=Op.add)
                    v.tensor_tensor(out=midW[:], in0=loW[:], in1=hiW[:],
                                    op=Op.add)
                    inc(v.tensor_scalar(out=midW[:], in0=midW[:], scalar1=0.5,
                                        scalar2=None, op0=Op.mult), "sV", 1)
                e_postR = evt("sV")
            @blk.vector
            def _(v):
                v.tensor_copy(out=spacer[:], in_=iota_row[:])
                v.tensor_copy(out=thrpad[0:1, 0:1], in_=hiW[0:1, 0:1])
                inc(v.tensor_copy(out=spacer[:], in_=iota_row[:]), "sV", 1)
            e_bisd = evt("sV")

            @blk.gpsimd
            def _(g):
                wt(g, e_bisd)
                inc(g.dma_start(out=thr_in_d.ap(), in_=thrpad[:]), "sWQ", 16)
                if debug:
                    inc(g.dma_start(out=dbg["dbg_bisf"], in_=bis[:]), "sWQ", 16)
                    inc(g.dma_start(out=dbg["dbg_trc"], in_=trc[:]), "sWQ", 16)
            e_thrpub = evt("sWQ")

        @blk.gpsimd
        def _(g):
            wt(g, e_thrpub)
            inc(g.collective_compute(
                "AllGather", Op.bypass,
                replica_groups=[[0, 1, 2, 3, 4, 5, 6, 7]],
                ins=[thr_in_d.ap().opt()], outs=[thr_out_d.ap().opt()]),
                "sCC", 1)
        e_thcc = evt("sCC")

        with contextlib.ExitStack() as es_a:
            def sba(name, shape, dt=F32):
                return es_a.enter_context(nc.sbuf_tensor(name, shape, dt))
            xaA = sba("xaA", [P, DCH * NT])
            xb16A = sba("xb16A", [P, DCH * NT], BF16)
            xsqA = sba("xsqA", [P, DCH * NT], BF16)
            a_bc = sba("a_bcA", [P, DCH * NT])
            negmA = sba("negmA", [P, NT])
            rA = sba("rA", [P, NT])
            varA = sba("varA", [P, NT])
            h1f = sba("h1fA", [P, NT])
            uf = sba("ufA", [P, NT])
            sf = sba("sfA", [P, NT])
            sb16 = [sba(f"sb16A{i}", [P, NT], BF16) for i in range(2)]

            @blk.vector
            def _(v):
                wt(v, e_sig, e_bisd)
                for ch in range(DCH):
                    v.tensor_scalar(out=a_bc[:, ch * NT:(ch + 1) * NT],
                                    in0=iota_row[:], scalar1=0.0,
                                    scalar2=small[:, 8 + ch:9 + ch],
                                    op0=Op.mult, op1=Op.add)
                inc(v.engine_nop(), "sV", 1)
            e_abc = evt("sV")

            sdma_evt = [None, None]
            e_var_prev = None
            e_negm_prev = None
            e_xaA_free = None
            for ti in range(NTILES):
                c0 = ti * NT + 2

                @blk.sync
                def _(sy):
                    if ti == 0:
                        wt(sy, e_bisd, e_thrpub)
                    wt(sy, e_xaA_free)
                    for ch in range(DCH):
                        inc(sy.dma_start(out=xaA[:, ch * NT:(ch + 1) * NT],
                                         in_=x_fm[ch * P:(ch + 1) * P, c0:c0 + NT]),
                            "sLD", 16)
                e_x = evt("sLD")

                @blk.vector
                def _(v):
                    wt(v, e_x, e_abc)
                    inc(v.tensor_copy(out=xb16A[:], in_=xaA[:]), "sV", 1)
                e_cast = evt("sV")

                @blk.scalar
                def _(s):
                    wt(s, e_x)
                    inc(s.activation(out=xsqA[:], in_=xaA[:], func=AF.Square),
                        "sA", 1)
                e_sq = evt("sA")

                @blk.tensor
                def _(t):
                    wt(t, e_cast, e_sq, e_var_prev, e_negm_prev)
                    for ch in range(DCH):
                        t.matmul(bank[4][:], sMean[:],
                                 xb16A[:, ch * NT:(ch + 1) * NT],
                                 start=(ch == 0), stop=(ch == DCH - 1))
                    for ch in range(DCH):
                        mm = t.matmul(bank[5][:], sSq[:],
                                      xsqA[:, ch * NT:(ch + 1) * NT],
                                      start=(ch == 0), stop=(ch == DCH - 1))
                    inc(mm, "sPE", 1)
                e_stat = evt("sPE")

                @blk.scalar
                def _(s):
                    wt(s, e_stat)
                    inc(s.copy(out=negmA[:], in_=bank[4][:]), "sA", 1)
                e_negm_prev = evt("sA")
                e_negm = e_negm_prev

                @blk.vector
                def _(v):
                    wt(v, e_negm)
                    v.tensor_tensor(out=varA[:], in0=negmA[:], in1=negmA[:],
                                    op=Op.mult)
                    inc(v.tensor_tensor(out=varA[:], in0=bank[5][:], in1=varA[:],
                                        op=Op.subtract), "sV", 1)
                e_var_prev = evt("sV")
                e_var = e_var_prev

                @blk.scalar
                def _(s):
                    wt(s, e_var)
                    s.activation(out=rA[:], in_=varA[:], func=AF.Ln, bias=eps_t[:])
                    inc(s.activation(out=rA[:], in_=rA[:], func=AF.Exp,
                                     scale=-0.5), "sA", 1)
                e_rs = evt("sA")

                for ch in range(DCH):
                    par = ch % 2

                    @blk.vector
                    def _(v):
                        if ch == 0:
                            wt(v, e_rs)
                        wt(v, sdma_evt[par])
                        v.tensor_tensor(out=h1f[:],
                                        in0=xaA[:, ch * NT:(ch + 1) * NT],
                                        in1=negmA[:], op=Op.add)
                        v.tensor_tensor(out=h1f[:], in0=h1f[:], in1=rA[:],
                                        op=Op.mult)
                        v.tensor_scalar(out=uf[:], in0=h1f[:],
                                        scalar1=small[:, 16 + ch:17 + ch],
                                        scalar2=None, op0=Op.mult)
                        v.tensor_tensor_scan(out=sf[:],
                                             data0=a_bc[:, ch * NT:(ch + 1) * NT],
                                             data1=uf[:],
                                             initial=carry_state[:, ch:ch + 1],
                                             op0=Op.mult, op1=Op.add)
                        v.tensor_copy(out=sb16[par][:], in_=sf[:])
                        inc(v.tensor_copy(out=carry_state[:, ch:ch + 1],
                                          in_=sf[:, NT - 1:NT]), "sV", 1)
                    e_sc = evt("sV")

                    @blk.gpsimd
                    def _(g):
                        wt(g, e_sc)
                        inc(g.dma_start(
                            out=s_scr.ap()[ch, :, ti * NT:(ti + 1) * NT],
                            in_=sb16[par][:]), "sWQ", 16)
                    sdma_evt[par] = evt("sWQ")
                e_xaA_free = evt("sV")

            @blk.vector
            def _(v):
                wt(v, sdma_evt[0], sdma_evt[1])
                inc(v.engine_nop(), "sV", 1)
            e_phA = evt("sV")

        @blk.gpsimd
        def _(g):
            wt(g, e_thcc)
            for r in range(8):
                inc(g.dma_start(out=thr_rows[0:1, r:r + 1],
                                in_=thr_out_d.ap()[r:r + 1, 0:1]), "sWQ", 16)
        e_thg = evt("sWQ")

        @blk.gpsimd
        def _(g):
            wt(g, e_phA)
            e1 = inc(g.dma_start(out=carry_in_d.ap(), in_=carry_state[:]),
                     "sWQ", 16)
            wt(g, e1)
            inc(g.collective_compute(
                "AllGather", Op.bypass,
                replica_groups=[[0, 1], [2, 3], [4, 5], [6, 7]],
                ins=[carry_in_d.ap().opt()], outs=[carry_out_d.ap().opt()]),
                "sCC", 1)
            wt(g, evt("sCC"))
            inc(g.dma_start(out=carry_eff[:], in_=carry_out_d.ap()[0:P, :]),
                "sWQ", 16)
        e_ccd = evt("sWQ")

        @blk.vector
        def _(v):
            wt(v, e_ccd)
            inc(v.tensor_scalar(out=carry_eff[:], in0=carry_eff[:],
                                scalar1=small[:, 57:58], scalar2=None,
                                op0=Op.mult), "sV", 1)
        e_ceff = evt("sV")

        @blk.tensor
        def _(t):
            wt(t, e_thg)
            mm = None
            for r in range(8):
                mm = t.matmul(bank[3][:, r:r + 1], ones1f[:],
                              thr_rows[0:1, r:r + 1], start=True, stop=True)
            inc(mm, "sPE", 1)
        e_thbc = evt("sPE")

        @blk.vector
        def _(v):
            wt(v, e_thbc)
            v.tensor_copy(out=spacer[:], in_=iota_row[:])
            v.tensor_copy(out=thrs[:], in_=bank[3][:, 0:8])
            inc(v.tensor_copy(out=spacer[:], in_=iota_row[:]), "sV", 1)
        e_thrs = evt("sV")

        if debug:
            @blk.gpsimd
            def _(g):
                wt(g, e_thrs, e_ceff, e_sig)
                inc(g.dma_start(out=dbg["dbg_small"], in_=small[:]), "sWQ", 16)
                inc(g.dma_start(out=dbg["dbg_thrs"], in_=thrs[:]), "sWQ", 16)
                inc(g.dma_start(out=dbg["dbg_carry"][:, 0:8], in_=carry_state[:]),
                    "sWQ", 16)
                inc(g.dma_start(out=dbg["dbg_carry"][:, 8:16], in_=carry_eff[:]),
                    "sWQ", 16)

        b1c = sb("b1cK", [P, FCH])
        pb1c = sb("pb1cK", [P, FCH])
        b2c = sb("b2cK", [P, DCH])
        pb2c = sb("pb2cK", [P, DCH])
        with contextlib.ExitStack() as es_m:
            def sbm(name, shape, dt=F32):
                return es_m.enter_context(nc.sbuf_tensor(name, shape, dt))
            wrow = [sbm(f"wrowM{i}", [P, D]) for i in range(2)]
            wrb16 = [sbm(f"wrbM{i}", [P, D], BF16) for i in range(2)]
            wtT = [sbm(f"wtTM{i}", [P, DCH, P], BF16) for i in range(2)]
            awb = sbm("awbM", [P, D])
            t1m = sbm("t1M", [P, D])
            t2m = sbm("t2M", [P, D])
            pwb = [sbm(f"pwbM{i}", [P, D], BF16) for i in range(2)]

            use_evt = [None, None]
            wq_evt = [None, None]
            for r in range(DCH):
                pr = r % 2

                @blk.sync
                def _(sy):
                    if r == 0:
                        wt(sy, e_phA)
                    wt(sy, use_evt[pr])
                    e_l = inc(sy.dma_start(out=wrow[pr][:],
                                           in_=w_attn[r * P:(r + 1) * P, :]),
                              "sLD", 16)
                e_ld = evt("sLD")

                @blk.vector
                def _(v):
                    wt(v, e_ld, wq_evt[pr])
                    inc(v.tensor_copy(out=wrb16[pr][:], in_=wrow[pr][:]), "sV", 1)
                use_evt[pr] = evt("sV")
                e_cast = evt("sV")

                @blk.sync
                def _(sy):
                    wt(sy, e_cast)
                    inc(sy.dma_start_transpose(out=wtT[pr][:], in_=wrb16[pr][:]),
                        "sLD", 16)
                e_tr = evt("sLD")

                @blk.sync
                def _(sy):
                    wt(sy, e_tr)
                    inc(sy.dma_start(out=wt_scr.ap()[r],
                                     in_=wtT[pr][:].rearrange("p m q -> p (m q)")),
                        "sWQ", 16)
                wq_evt[pr] = evt("sWQ")

            use_evt = [evt("sWQ"), evt("sWQ")]
            for f in range(FCH):
                pr = f % 2

                @blk.sync
                def _(sy):
                    wt(sy, use_evt[pr])
                    for k in range(DCH):
                        inc(sy.dma_start(
                            out=wrow[pr][:, k * P:(k + 1) * P],
                            in_=w1_in[k * P:(k + 1) * P, f * P:(f + 1) * P]),
                            "sLD", 16)
                e_ld = evt("sLD")

                @blk.scalar
                def _(s):
                    wt(s, e_ld, e_thrs)
                    inc(s.activation(out=awb[:], in_=wrow[pr][:], func=AF.Abs),
                        "sA", 1)
                e_ab = evt("sA")

                @blk.vector
                def _(v):
                    wt(v, e_ab)
                    v.tensor_scalar(out=t1m[:], in0=awb[:], scalar1=thrs[:, 0:1],
                                    scalar2=0.8, op0=Op.is_le, op1=Op.mult)
                    v.tensor_scalar(out=t2m[:], in0=awb[:], scalar1=thrs[:, 1:2],
                                    scalar2=-0.8, op0=Op.is_ge, op1=Op.mult)
                    v.scalar_tensor_tensor(out=t1m[:], in0=t1m[:], scalar=1.0,
                                           in1=t2m[:], op0=Op.add, op1=Op.add)
                    v.tensor_tensor(out=pwb[pr][:], in0=wrow[pr][:], in1=t1m[:],
                                    op=Op.mult)
                    inc(v.tensor_copy(out=wrb16[pr][:], in_=wrow[pr][:]), "sV", 1)
                e_wv = evt("sV")

                @blk.gpsimd
                def _(g):
                    wt(g, e_wv)
                    inc(g.dma_start(out=w1_scr.ap()[f], in_=wrb16[pr][:]),
                        "sWQ", 16)
                    inc(g.dma_start(out=pw1_scr.ap()[f], in_=pwb[pr][:]),
                        "sWQ", 16)
                use_evt[pr] = evt("sWQ")

            for fc in range(FCH):
                pr = fc % 2

                @blk.sync
                def _(sy):
                    wt(sy, use_evt[pr])
                    inc(sy.dma_start(out=wrow[pr][:],
                                     in_=w2_in[fc * P:(fc + 1) * P, :]), "sLD", 16)
                e_ld = evt("sLD")

                @blk.scalar
                def _(s):
                    wt(s, e_ld)
                    inc(s.activation(out=awb[:], in_=wrow[pr][:], func=AF.Abs),
                        "sA", 1)
                e_ab = evt("sA")

                @blk.vector
                def _(v):
                    wt(v, e_ab)
                    v.tensor_scalar(out=t1m[:], in0=awb[:], scalar1=thrs[:, 2:3],
                                    scalar2=0.8, op0=Op.is_le, op1=Op.mult)
                    v.tensor_scalar(out=t2m[:], in0=awb[:], scalar1=thrs[:, 3:4],
                                    scalar2=-0.8, op0=Op.is_ge, op1=Op.mult)
                    v.scalar_tensor_tensor(out=t1m[:], in0=t1m[:], scalar=1.0,
                                           in1=t2m[:], op0=Op.add, op1=Op.add)
                    v.tensor_tensor(out=pwb[pr][:], in0=wrow[pr][:], in1=t1m[:],
                                    op=Op.mult)
                    inc(v.tensor_scalar(out=wrb16[pr][:], in0=wrow[pr][:],
                                        scalar1=1.0, scalar2=None, op0=Op.mult),
                        "sV", 1)
                e_wv = evt("sV")

                @blk.gpsimd
                def _(g):
                    wt(g, e_wv)
                    inc(g.dma_start(
                        out=w2_scr.ap()[:, :, fc * P:(fc + 1) * P]
                        .rearrange("d p m -> p d m"),
                        in_=wrb16[pr][:].rearrange("p (d m) -> p d m", d=DCH)),
                        "sWQ", 16)
                    inc(g.dma_start(
                        out=pw2_scr.ap()[:, :, fc * P:(fc + 1) * P]
                        .rearrange("d p m -> p d m"),
                        in_=pwb[pr][:].rearrange("p (d m) -> p d m", d=DCH)),
                        "sWQ", 16)
                use_evt[pr] = evt("sWQ")

            @blk.sync
            def _(sy):
                inc(sy.dma_start(out=b1c[:], in_=b1c_in), "sLD", 16)
                inc(sy.dma_start(out=b2c[:], in_=b2c_in), "sLD", 16)
            e_bc = evt("sLD")

            @blk.scalar
            def _(s):
                wt(s, e_bc)
                s.activation(out=awb[:, 0:FCH], in_=b1c[:], func=AF.Abs)
                s.activation(out=awb[:, 64:64 + DCH], in_=b2c[:], func=AF.Abs)
                inc(s.copy(out=t2m[:, 0:NT], in_=awb[:, 0:NT]), "sA", 1)
            e_abc = evt("sA")

            @blk.vector
            def _(v):
                wt(v, e_abc, e_thrs)
                v.tensor_scalar(out=t1m[:, 0:FCH], in0=awb[:, 0:FCH],
                                scalar1=thrs[:, 4:5], scalar2=0.8,
                                op0=Op.is_le, op1=Op.mult)
                v.tensor_scalar(out=t1m[:, 64:64 + FCH], in0=awb[:, 0:FCH],
                                scalar1=thrs[:, 5:6], scalar2=-0.8,
                                op0=Op.is_ge, op1=Op.mult)
                v.tensor_scalar(out=t1m[:, 128:128 + DCH],
                                in0=awb[:, 64:64 + DCH],
                                scalar1=thrs[:, 6:7], scalar2=0.8,
                                op0=Op.is_le, op1=Op.mult)
                v.tensor_scalar(out=t1m[:, 192:192 + DCH],
                                in0=awb[:, 64:64 + DCH],
                                scalar1=thrs[:, 7:8], scalar2=-0.8,
                                op0=Op.is_ge, op1=Op.mult)
                v.tensor_copy(out=t2m[:, 0:NT], in_=t1m[:, 0:NT])
                v.scalar_tensor_tensor(out=t1m[:, 0:FCH], in0=t1m[:, 0:FCH],
                                       scalar=1.0, in1=t1m[:, 64:64 + FCH],
                                       op0=Op.add, op1=Op.add)
                v.scalar_tensor_tensor(out=t1m[:, 128:128 + DCH],
                                       in0=t1m[:, 128:128 + DCH], scalar=1.0,
                                       in1=t1m[:, 192:192 + DCH],
                                       op0=Op.add, op1=Op.add)
                v.tensor_tensor(out=pb1c[:], in0=b1c[:], in1=t1m[:, 0:FCH],
                                op=Op.mult)
                v.tensor_tensor(out=pb2c[:], in0=b2c[:],
                                in1=t1m[:, 128:128 + DCH], op=Op.mult)
                inc(v.tensor_copy(out=t2m[:, 0:NT], in_=t1m[:, 0:NT]), "sV", 1)
            e_m_done = evt("sWQ")
            e_m_v = evt("sV")
            e_m_a = evt("sA")

        xa = sb("xaB", [P, DCH * C6])
        castA = sb("castAB", [P, DCH * C6], BF16)
        sqA = sb("sqAB", [P, DCH * C6], BF16)
        h1b = sb("h1bB", [P, DCH * C6], BF16)
        s_in = sb("s_inB", [P, DCH * NT], BF16)
        x1 = [sb(f"x1B{i}", [P, DCH * NT]) for i in range(2)]
        zbuf = [sb(f"zbB{i}", [P, DCH * NT], BF16) for i in range(2)]
        act_sb = sb("actB", [P, FCH * NT], BF16)
        y0s = sb("y0sB", [P, DCH * NT])
        negm_s = sb("negmB", [P, NT])
        r_s = sb("rB", [P, NT])
        negmhW = sb("negmhB", [P, NT])
        rhW = sb("rhB", [P, NT])
        hsc = sb("hscsB", [P, 4])
        pw_one = sb("pwB0", [P, NT])
        pw_t = [pw_one, pw_one]
        bct_all = sb("bctB", [P, 8])
        qs = sb("qsB", [P, NT])
        vs = sb("vsB", [P, NT])
        h1tmp = sb("h1tmpB", [P, NT])
        ws = h1tmp
        hscW = qs
        var_s = vs
        lg = sb("lgB", [P, 32])
        sm = sb("smB", [P, 16])
        g0r = sb("g0rB", [1, NT], BF16)
        rzc = sb("rzcB", [P, 4])
        nmzr = sb("nmzrB", [1, NT], BF16)
        g0s = [sb(f"g0sB{i}", [P, NT]) for i in range(2)]
        wbuf = [sb(f"wbufB{i}", [P, D], BF16) for i in range(3)]
        wtbuf = wbuf
        w2buf = [sb(f"w2bufB{i}", [P, 16 * P], BF16) for i in range(4)]
        otile = [pw_one, rhW]

        gateB = (e_m_done, e_m_v, e_m_a, e_ceff, e_thrs, e_c2done)

        R = {
            "xa_free": None, "sin_free": None, "castA_free": None,
            "sqA_free": None, "b0_free": None, "b1_free": None,
            "b2_free": None, "b7_free": None,
            "zb_free": [None, None], "x1_free": [None, None],
            "g0s_free": [None, None],
            "att_free": [None, None], "pre_free": [None, None],
            "y_free": [None, None], "wt_roll": [None, None],
            "w1_roll": [None, None, None], "w2_roll": [None, None],
            "ot_dma": [None, None], "y0s_free": None, "act_free": None,
        }

        def front_a(ti):
            cs = ti * NT
            pz = ti % 2

            @blk.sync
            def _(sy):
                if ti == 0:
                    wt(sy, *gateB)
                wt(sy, R["xa_free"], R["sin_free"])
                for ch in range(DCH):
                    inc(sy.dma_start(out=xa[:, ch * C6:ch * C6 + C6],
                                     in_=x_fm[ch * P:(ch + 1) * P, cs:cs + C6]),
                        "sLD", 16)
                for ch in range(DCH):
                    inc(sy.dma_start(out=s_in[:, ch * NT:(ch + 1) * NT],
                                     in_=s_scr.ap()[ch, :, ti * NT:(ti + 1) * NT]),
                        "sLD", 16)
            e_x = evt("sLD")

            @blk.vector
            def _(v):
                if ti == 0:
                    wt(v, *gateB)
                wt(v, e_x, R["castA_free"])
                inc(v.tensor_copy(out=castA[:], in_=xa[:]), "sV", 1)
            e_cast = evt("sV")

            @blk.vector
            def _(v):
                if ti == 0:
                    wt(v, *gateB)
                wt(v, e_x, R["sqA_free"])
                inc(v.tensor_tensor(out=sqA[:], in0=xa[:], in1=xa[:],
                                    op=Op.mult), "sV", 1)
            e_sq = evt("sV")

            @blk.tensor
            def _(t):
                if ti == 0:
                    wt(t, *gateB)
                wt(t, e_cast, e_sq, R["b0_free"], R["b1_free"], R["b2_free"])
                for ch in range(DCH):
                    t.matmul(bank[0][:], sMean[:],
                             castA[:, ch * C6 + 2:ch * C6 + 2 + NT],
                             start=(ch == 0), stop=(ch == DCH - 1))
                for ch in range(DCH):
                    t.matmul(bank[1][:], sSq[:],
                             sqA[:, ch * C6 + 2:ch * C6 + 2 + NT],
                             start=(ch == 0), stop=(ch == DCH - 1))
                for ch in range(DCH):
                    t.matmul(bank[2][:, 0:2], sMean[:],
                             castA[:, ch * C6:ch * C6 + 2],
                             start=(ch == 0), stop=(ch == DCH - 1))
                for ch in range(DCH):
                    t.matmul(bank[2][:, 2:4], sMean[:],
                             castA[:, ch * C6 + 514:ch * C6 + 516],
                             start=(ch == 0), stop=(ch == DCH - 1))
                for ch in range(DCH):
                    t.matmul(bank[2][:, 4:6], sSq[:],
                             sqA[:, ch * C6:ch * C6 + 2],
                             start=(ch == 0), stop=(ch == DCH - 1))
                for ch in range(DCH):
                    t.matmul(bank[2][:, 6:8], sSq[:],
                             sqA[:, ch * C6 + 514:ch * C6 + 516],
                             start=(ch == 0), stop=(ch == DCH - 1))
                inc(t.matmul(bank[2][:, 128:384], sMean[:], castA[:, 0:256],
                             start=True, stop=True), "sPE", 1)
            e_stat1 = evt("sPE")

            @blk.vector
            def _(v):
                wt(v, e_stat1, R["ot_dma"][0], R["ot_dma"][1])
                v.tensor_copy(out=spacer[:], in_=iota_row[:])
                v.tensor_scalar(out=negm_s[:], in0=bank[0][:], scalar1=1.0,
                                scalar2=None, op0=Op.mult)
                v.tensor_copy(out=negmhW[:], in_=bank[2][:])
                v.tensor_tensor(out=var_s[:], in0=negm_s[:], in1=negm_s[:],
                                op=Op.mult)
                v.tensor_tensor(out=var_s[:], in0=bank[1][:], in1=var_s[:],
                                op=Op.subtract)
                v.tensor_tensor(out=hscW[:], in0=negmhW[:], in1=negmhW[:],
                                op=Op.mult)
                v.tensor_tensor(out=rhW[:, 0:NT - 4], in0=bank[2][:, 4:NT],
                                in1=hscW[:, 0:NT - 4], op=Op.subtract)
                inc(v.tensor_copy(out=spacer[:], in_=iota_row[:]), "sV", 1)
            e_var1 = evt("sV")
            e_negm1 = e_var1

            @blk.scalar
            def _(s):
                wt(s, e_var1)
                s.activation(out=rhW[:, 0:8], in_=rhW[:, 0:8], func=AF.Ln,
                             bias=eps_t[:])
                s.activation(out=r_s[:], in_=var_s[:], func=AF.Ln, bias=eps_t[:])
                s.activation(out=rhW[:, 0:8], in_=rhW[:, 0:8], func=AF.Exp,
                             scale=-0.5)
                inc(s.activation(out=r_s[:], in_=r_s[:], func=AF.Exp, scale=-0.5),
                    "sA", 1)
            e_rs1 = evt("sA")

            @blk.vector
            def _(v):
                wt(v, e_rs1)
                for ch in range(DCH):
                    v.tensor_tensor(out=h1tmp[:],
                                    in0=xa[:, ch * C6 + 2:ch * C6 + 2 + NT],
                                    in1=negm_s[:], op=Op.add)
                    v.tensor_tensor(out=h1b[:, ch * C6 + 2:ch * C6 + 2 + NT],
                                    in0=h1tmp[:], in1=r_s[:], op=Op.mult)
                    v.tensor_tensor(out=hsc[:, 0:2],
                                    in0=xa[:, ch * C6:ch * C6 + 2],
                                    in1=negmhW[:, 0:2], op=Op.add)
                    v.tensor_tensor(out=hsc[:, 2:4],
                                    in0=xa[:, ch * C6 + 514:ch * C6 + 516],
                                    in1=negmhW[:, 2:4], op=Op.add)
                    v.tensor_copy(out=spacer[:], in_=iota_row[:])
                    v.tensor_tensor(out=h1b[:, ch * C6:ch * C6 + 2],
                                    in0=hsc[:, 0:2], in1=rhW[:, 0:2], op=Op.mult)
                    v.tensor_tensor(out=h1b[:, ch * C6 + 514:ch * C6 + 516],
                                    in0=hsc[:, 2:4], in1=rhW[:, 2:4], op=Op.mult)
                inc(v.engine_nop(), "sV", 1)
            e_h1 = evt("sV")

            @blk.scalar
            def _(s):
                for ch in range(DCH):
                    s.activation(out=bct_all[:, ch:ch + 1],
                                 in_=small[:, 24 + ch:25 + ch],
                                 func=AF.Copy, scale=float(ti * NT + 1))
                inc(s.copy(out=spacerA[:], in_=iota_row[:]), "sA", 1)
            e_pw = [None, None]
            for ch in range(DCH):
                par = ch % 2

                @blk.scalar
                def _(s):
                    wt(s, e_pw[par])
                    if ch == 0:
                        wt(s, e_x)
                    inc(s.activation(out=pw_t[par][:], in_=iota_row[:],
                                     func=AF.Exp,
                                     scale=small[:, 24 + ch:25 + ch],
                                     bias=bct_all[:, ch:ch + 1]), "sA", 1)
                e_pwa = evt("sA")

                @blk.vector
                def _(v):
                    wt(v, e_pwa)
                    inc(v.scalar_tensor_tensor(
                        out=s_in[:, ch * NT:(ch + 1) * NT], in0=pw_t[par][:],
                        scalar=carry_eff[:, ch:ch + 1],
                        in1=s_in[:, ch * NT:(ch + 1) * NT],
                        op0=Op.mult, op1=Op.add), "sV", 1)
                e_pw[par] = evt("sV")
            e_scorr = evt("sV")
            return {"e_x": e_x, "e_cast": e_cast, "e_sq": e_sq,
                    "e_stat1": e_stat1, "e_negm1": e_negm1, "e_var1": e_var1,
                    "e_rs1": e_rs1, "e_h1": e_h1, "e_scorr": e_scorr}

        def front_b(ti, fa):
            pz = ti % 2
            e_attmm = [None, None]
            e_attev = [None, None]
            for e in range(DCH):
                par = e % 2

                @blk.sync
                def _(sy):
                    wt(sy, R["w1_roll"][par])
                    inc(sy.dma_start(out=wtbuf[par][:], in_=wt_scr.ap()[e]),
                        "sLD", 16)
                e_wtld = evt("sLD")

                @blk.tensor
                def _(t):
                    wt(t, e_wtld, e_attev[par])
                    if e == 0:
                        wt(t, fa["e_scorr"], fa["e_h1"], fa["e_negm1"])
                    if e == 1:
                        wt(t, fa["e_var1"])
                    for k_ in range(DCH):
                        t.matmul(bank[par][:], wtbuf[par][:, k_ * P:(k_ + 1) * P],
                                 s_in[:, k_ * NT:(k_ + 1) * NT],
                                 start=(k_ == 0), stop=False)
                    for k_ in range(5):
                        mm = t.matmul(bank[par][:],
                                      diag[:, (e * 5 + k_) * P:(e * 5 + k_ + 1) * P],
                                      h1b[:, e * C6 + k_:e * C6 + k_ + NT],
                                      start=False, stop=(k_ == 4))
                    inc(mm, "sPE", 1)
                e_attmm[par] = evt("sPE")
                R["w1_roll"][par] = e_attmm[par]

                @blk.vector
                def _(v):
                    wt(v, e_attmm[par])
                    if e == 0:
                        wt(v, R["x1_free"][pz])
                    inc(v.scalar_tensor_tensor(
                        out=x1[pz][:, e * NT:(e + 1) * NT], in0=bank[par][:],
                        scalar=small[:, 32 + e:33 + e],
                        in1=xa[:, e * C6 + 2:e * C6 + 2 + NT],
                        op0=Op.add, op1=Op.add), "sV", 1)
                e_attev[par] = evt("sV")
            e_x1done = evt("sV")
            R["xa_free"] = e_x1done
            R["sin_free"] = e_attmm[1]

            @blk.vector
            def _(v):
                wt(v, e_x1done)
                v.tensor_copy(out=castA[:, 0:DCH * NT], in_=x1[pz][:])
                inc(v.tensor_tensor(out=sqA[:, 0:DCH * NT], in0=x1[pz][:],
                                    in1=x1[pz][:], op=Op.mult), "sV", 1)
            e_x1b = evt("sV")
            e_sq2 = e_x1b

            @blk.tensor
            def _(t):
                wt(t, e_x1b, e_sq2, fa["e_rs1"], e_attev[0], e_attev[1])
                for ch in range(DCH):
                    t.matmul(bank[0][:], sMean[:],
                             castA[:, ch * NT:(ch + 1) * NT],
                             start=(ch == 0), stop=(ch == DCH - 1))
                for ch in range(DCH):
                    mm = t.matmul(bank[1][:], sSq[:],
                                  sqA[:, ch * NT:(ch + 1) * NT],
                                  start=(ch == 0), stop=(ch == DCH - 1))
                inc(mm, "sPE", 1)
            e_stat2 = evt("sPE")
            R["castA_free"] = e_stat2

            @blk.vector
            def _(v):
                wt(v, e_stat2, fa["e_h1"])
                v.tensor_scalar(out=negm_s[:], in0=bank[0][:], scalar1=1.0,
                                scalar2=None, op0=Op.mult)
                v.tensor_tensor(out=var_s[:], in0=negm_s[:], in1=negm_s[:],
                                op=Op.mult)
                inc(v.tensor_tensor(out=var_s[:], in0=bank[1][:], in1=var_s[:],
                                    op=Op.subtract), "sV", 1)
            e_var2 = evt("sV")
            e_negm2 = e_var2

            @blk.scalar
            def _(s):
                wt(s, e_var2)
                s.activation(out=r_s[:], in_=var_s[:], func=AF.Ln, bias=eps_t[:])
                inc(s.activation(out=r_s[:], in_=r_s[:], func=AF.Exp, scale=-0.5),
                    "sA", 1)
            e_rs2 = evt("sA")
            R["b1_free"] = e_var2

            @blk.vector
            def _(v):
                wt(v, e_rs2, R["zb_free"][pz])
                for ch in range(DCH):
                    v.tensor_tensor(out=h1tmp[:],
                                    in0=x1[pz][:, ch * NT:(ch + 1) * NT],
                                    in1=negm_s[:], op=Op.add)
                    v.tensor_tensor(out=zbuf[pz][:, ch * NT:(ch + 1) * NT],
                                    in0=h1tmp[:], in1=r_s[:], op=Op.mult)
                inc(v.engine_nop(), "sV", 1)
            e_zb = evt("sV")

            @blk.vector
            def _(v):
                wt(v, e_zb)
                inc(v.tensor_tensor(out=sqA[:, 0:DCH * NT], in0=zbuf[pz][:],
                                    in1=zbuf[pz][:], op=Op.mult), "sV", 1)
            e_sq3 = evt("sV")

            @blk.tensor
            def _(t):
                wt(t, e_zb, e_sq3, e_negm2)
                for ch in range(DCH):
                    mm = t.matmul(bank[0][:], sMean[:],
                                  zbuf[pz][:, ch * NT:(ch + 1) * NT],
                                  start=(ch == 0), stop=(ch == DCH - 1))
                inc(mm, "sPE", 1)
            e_stat3 = evt("sPE")

            @blk.vector
            def _(v):
                wt(v, e_stat3)
                v.tensor_copy(out=spacer[:], in_=iota_row[:])
                v.tensor_copy(out=nmzr[:], in_=bank[0][0:1, :])
                inc(v.tensor_copy(out=spacer[:], in_=iota_row[:]), "sV", 1)
            e_negm3 = evt("sV")
            R["b0_free"] = e_negm3

            @blk.tensor
            def _(t):
                wt(t, e_zb, e_negm3, R["b2_free"] or fa["e_var1"], fa["e_var1"])
                for g_ in range(4):
                    for ch in range(DCH):
                        t.matmul(bank[2][:, 8 * g_:8 * g_ + 8],
                                 zbuf[pz][:, ch * NT + g_ * P:ch * NT + (g_ + 1) * P],
                                 wr_b[:, 8 * ch:8 * ch + 8],
                                 start=(ch == 0), stop=False)
                    t.matmul(bank[2][:, 8 * g_:8 * g_ + 8],
                             nmzr[0:1, g_ * P:(g_ + 1) * P], c1r[:],
                             start=False, stop=True)
                for g_ in range(4):
                    for ch in range(DCH):
                        t.matmul(bank[2][:, 32 + g_:33 + g_],
                                 zbuf[pz][:, ch * NT + g_ * P:ch * NT + (g_ + 1) * P],
                                 sMean[:, 0:1],
                                 start=(ch == 0), stop=(ch == DCH - 1))
                for g_ in range(4):
                    for ch in range(DCH):
                        t.matmul(bank[2][:, 36 + g_:37 + g_],
                                 sqA[:, ch * NT + g_ * P:ch * NT + (g_ + 1) * P],
                                 sSq[:, 0:1],
                                 start=(ch == 0), stop=(ch == DCH - 1))
                inc(t.matmul(bank[2][:, 128:384], sMean[:], zbuf[pz][:, 256:512],
                             start=True, stop=True), "sPE", 1)
            e_logit = evt("sPE")
            R["sqA_free"] = e_logit

            @blk.vector
            def _(v):
                wt(v, e_logit)
                v.tensor_copy(out=spacer[:], in_=iota_row[:])
                v.tensor_copy(out=lg[:], in_=bank[2][:, 0:32])
                inc(v.tensor_copy(out=spacer[:], in_=iota_row[:]), "sV", 1)
            e_lg = evt("sV")

            @blk.vector
            def _(v):
                wt(v, e_lg)
                v.tensor_copy(out=spacer[:], in_=iota_row[:])
                v.tensor_copy(out=negmhW[:, 0:128], in_=bank[2][:, 0:128])
                v.tensor_copy(out=spacer[:], in_=iota_row[:])
                v.tensor_tensor(out=hscW[:, 0:128], in0=negmhW[:, 0:128],
                                in1=negmhW[:, 0:128], op=Op.mult)
                v.tensor_tensor(out=rzc[:], in0=bank[2][:, 36:40],
                                in1=hscW[:, 32:36], op=Op.subtract)
                inc(v.tensor_copy(out=spacer[:], in_=iota_row[:]), "sV", 1)
            e_rzv = evt("sV")
            R["b2_free"] = e_rzv

            @blk.scalar
            def _(s):
                wt(s, e_rzv)
                s.activation(out=rzc[:], in_=rzc[:], func=AF.Ln, bias=eps_t[:])
                s.activation(out=spacerA[:], in_=iota_row[:], func=AF.Ln,
                             bias=eps_t[:])
                s.activation(out=rzc[:], in_=rzc[:], func=AF.Exp, scale=-0.5)
                inc(s.activation(out=spacerA[:], in_=iota_row[:], func=AF.Exp,
                                 scale=-1.0), "sA", 1)
            e_rzc = evt("sA")

            @blk.vector
            def _(v):
                wt(v, e_rzc)
                for g_ in range(4):
                    v.tensor_scalar(out=lg[:, 8 * g_:8 * g_ + 8],
                                    in0=lg[:, 8 * g_:8 * g_ + 8],
                                    scalar1=rzc[:, g_:g_ + 1],
                                    scalar2=None, op0=Op.mult)
                inc(v.engine_nop(), "sV", 1)
            e_rzap = evt("sV")

            @blk.vector
            def _(v):
                wt(v, e_rzap)
                for g_ in range(4):
                    v.tensor_tensor(out=lg[:, 8 * g_:8 * g_ + 8],
                                    in0=lg[:, 8 * g_:8 * g_ + 8], in1=c2bc[:],
                                    op=Op.add)
                    v.tensor_reduce(out=sm[:, g_:g_ + 1],
                                    in_=lg[:, 8 * g_:8 * g_ + 8],
                                    axis=mybir.AxisListType.X, op=Op.max,
                                    negate=True)
                inc(v.tensor_copy(out=spacer[:], in_=iota_row[:]), "sV", 1)
            e_nm = evt("sV")

            @blk.scalar
            def _(s):
                wt(s, e_nm)
                s.activation(out=spacerA[:], in_=iota_row[:], func=AF.Exp,
                             scale=-1.0)
                for g_ in range(4):
                    ex = s.activation(out=lg[:, 8 * g_:8 * g_ + 8],
                                      in_=lg[:, 8 * g_:8 * g_ + 8], func=AF.Exp,
                                      bias=sm[:, g_:g_ + 1])
                inc(ex, "sA", 1)
            e_exp = evt("sA")

            @blk.vector
            def _(v):
                wt(v, e_exp)
                v.tensor_copy(out=spacer[:], in_=iota_row[:])
                for g_ in range(4):
                    v.tensor_reduce(out=sm[:, 4 + g_:5 + g_],
                                    in_=lg[:, 8 * g_:8 * g_ + 8],
                                    axis=mybir.AxisListType.X, op=Op.add)
                v.tensor_copy(out=spacer[:], in_=iota_row[:])
                v.reciprocal(out=sm[:, 8:12], in_=sm[:, 4:8])
                v.tensor_copy(out=spacer[:], in_=iota_row[:])
                for g_ in range(4):
                    v.tensor_tensor(out=sm[:, 12 + g_:13 + g_],
                                    in0=lg[:, 8 * g_:8 * g_ + 1],
                                    in1=sm[:, 8 + g_:9 + g_], op=Op.mult)
                inc(v.tensor_copy(out=spacer[:], in_=iota_row[:]), "sV", 1)
            e_soft = evt("sV")

            @blk.tensor
            def _(t):
                wt(t, e_soft, R["b7_free"])
                t.transpose(bank[2][:, 128:256], ident[:], ident[:])
                for g_ in range(4):
                    tr = t.transpose(bank[7][0:1, g_ * P:(g_ + 1) * P],
                                     sm[:, 12 + g_:13 + g_], ident[:])
                inc(tr, "sPE", 1)
            e_p0T = evt("sPE")

            @blk.vector
            def _(v):
                wt(v, e_p0T)
                v.tensor_copy(out=spacer[:], in_=iota_row[:])
                v.tensor_copy(out=g0r[:], in_=bank[7][0:1, :])
                inc(v.tensor_copy(out=spacer[:], in_=iota_row[:]), "sV", 1)
            e_g0r = evt("sV")

            @blk.tensor
            def _(t):
                wt(t, e_g0r)
                inc(t.matmul(bank[7][:], ones1b[:], g0r[:], start=True,
                             stop=True), "sPE", 1)
            e_g0bc = evt("sPE")

            @blk.vector
            def _(v):
                wt(v, e_g0bc, R["g0s_free"][pz])
                inc(v.tensor_scalar(out=g0s[pz][:], in0=bank[7][:],
                                    scalar1=1.0, scalar2=None, op0=Op.mult),
                    "sV", 1)
            e_g0sE = evt("sV")
            R["b7_free"] = e_g0sE
            return {"e_zb": e_zb, "e_g0sE": e_g0sE, "e_x1done": e_x1done}

        def back_pre(ti, exp, fb):
            pz = ti % 2
            wsrc = w1_scr if exp == 0 else pw1_scr
            bcol = b1c if exp == 0 else pb1c
            e_pre = {}
            e_actstt = {}
            for f in range(FCH):
                par = f % 2
                p4 = f % 3

                @blk.sync
                def _(sy):
                    wt(sy, R["w1_roll"][p4])
                    inc(sy.dma_start(out=wbuf[p4][:], in_=wsrc.ap()[f]),
                        "sLD", 16)
                e_wld = evt("sLD")

                @blk.tensor
                def _(t):
                    wt(t, e_wld, R["pre_free"][par])
                    if f == 0:
                        wt(t, fb["e_zb"])
                        if exp == 0:
                            wt(t, R["act_free"])
                    mm = None
                    for k_ in range(DCH):
                        mm = t.matmul(bank[5 + par][:],
                                      wbuf[p4][:, k_ * P:(k_ + 1) * P],
                                      zbuf[pz][:, k_ * NT:(k_ + 1) * NT],
                                      start=(k_ == 0), stop=(k_ == DCH - 1))
                    inc(mm, "sPE", 1)
                e_pre[f] = evt("sPE")
                R["w1_roll"][p4] = e_pre[f]

                @blk.scalar
                def _(s):
                    wt(s, e_pre[f])
                    inc(s.activation(out=act_sb[:, f * NT:(f + 1) * NT],
                                     in_=bank[5 + par][:],
                                     func=AF.Gelu_apprx_tanh,
                                     bias=bcol[:, f:f + 1]), "sA", 1)
                e_actstt[f] = evt("sA")
                R["pre_free"][par] = e_actstt[f]
            return {"e_pre": e_pre, "e_actstt": e_actstt}

        def back_y(ti, exp, fb, pre_ev):
            pz = ti % 2
            w2src = w2_scr if exp == 0 else pw2_scr
            e_yfree = {}
            for d in range(DCH):
                par = d % 2

                @blk.sync
                def _(sy):
                    wt(sy, R["w2_roll"][par])
                    inc(sy.dma_start(
                        out=w2buf[2 * par + 0][:],
                        in_=w2src.ap()[d, :, 0:2048]), "sLD", 16)
                    inc(sy.dma_start(
                        out=w2buf[2 * par + 1][:],
                        in_=w2src.ap()[d, :, 2048:4096]), "sLD", 16)
                e_w2ld = evt("sLD")

                @blk.tensor
                def _(t):
                    wt(t, e_w2ld, R["y_free"][par])
                    if d == 0:
                        wt(t, pre_ev["e_actstt"][FCH - 1])
                    mm = None
                    for f in range(FCH):
                        mm = t.matmul(
                            bank[3 + par][:],
                            w2buf[2 * par + f // 16][:, (f % 16) * P:
                                                     (f % 16 + 1) * P],
                            act_sb[:, f * NT:(f + 1) * NT],
                            start=(f == 0), stop=(f == FCH - 1))
                    inc(mm, "sPE", 1)
                e_y = evt("sPE")
                R["w2_roll"][par] = e_y

                if exp == 0:
                    @blk.vector
                    def _(v):
                        wt(v, e_y)
                        if d == 0:
                            wt(v, R["y0s_free"])
                        inc(v.tensor_scalar(out=y0s[:, d * NT:(d + 1) * NT],
                                            in0=bank[3 + par][:],
                                            scalar1=b2c[:, d:d + 1],
                                            scalar2=None, op0=Op.add),
                            "sV", 1)
                    e_yfree[(exp, d)] = evt("sV")
                else:
                    @blk.vector
                    def _(v):
                        wt(v, e_y, fb["e_g0sE"], R["ot_dma"][par])
                        v.tensor_scalar(out=h1tmp[:], in0=bank[3 + par][:],
                                        scalar1=pb2c[:, d:d + 1],
                                        scalar2=None, op0=Op.add)
                        v.tensor_tensor(out=qs[:],
                                        in0=y0s[:, d * NT:(d + 1) * NT],
                                        in1=h1tmp[:], op=Op.subtract)
                        v.tensor_tensor(out=qs[:], in0=qs[:], in1=g0s[pz][:],
                                        op=Op.mult)
                        v.tensor_tensor(out=qs[:], in0=qs[:],
                                        in1=h1tmp[:], op=Op.add)
                        inc(v.tensor_tensor(out=otile[par][:], in0=qs[:],
                                            in1=x1[pz][:, d * NT:(d + 1) * NT],
                                            op=Op.add), "sV", 1)
                    e_yfree[(exp, d)] = evt("sV")

                    @blk.gpsimd
                    def _(g):
                        wt(g, e_yfree[(exp, d)])
                        inc(g.dma_start(
                            out=out_fm[d * P:(d + 1) * P,
                                       ti * NT:(ti + 1) * NT],
                            in_=otile[par][:]), "sST", 16)
                    R["ot_dma"][par] = evt("sST")
                R["y_free"][par] = e_yfree[(exp, d)]
            return e_yfree

        fa = front_a(0)
        fb = front_b(0, fa)
        fronts = {0: fb}
        for t_i in range(NTILES):
            fbt = fronts[t_i]
            pre0 = back_pre(t_i, 0, fbt)
            fa2 = None
            if t_i + 1 < NTILES:
                fa2 = front_a(t_i + 1)
            y0ev = back_y(t_i, 0, fbt, pre0)
            pre1 = back_pre(t_i, 1, fbt)
            if t_i + 1 < NTILES:
                fb2 = front_b(t_i + 1, fa2)
                fronts[t_i + 1] = fb2
            y1ev = back_y(t_i, 1, fbt, pre1)
            pz = t_i % 2
            R["zb_free"][pz] = pre1["e_pre"][FCH - 1]
            R["x1_free"][pz] = y1ev[(1, DCH - 1)]
            R["g0s_free"][pz] = y1ev[(1, DCH - 1)]
            R["y0s_free"] = y1ev[(1, DCH - 1)]
            R["act_free"] = y1ev[(1, DCH - 1)]

        @blk.gpsimd
        def _(g):
            g.wait_ge(sems["sST"], cnt["sST"])

    return nc


_NC_CACHE = {}


def _get_nc():
    if "nc" not in _NC_CACHE:
        _NC_CACHE["nc"] = build_program()
    return _NC_CACHE["nc"]


def _chunk_layout(v):
    return np.ascontiguousarray(v.reshape(DCH, P).T)


def kernel(L0_attn_x, L0_attn_decay_logit, L0_attn_W, L0_attn_b, L0_attn_conv_w,
           L0_attn_conv_b, L0_ffn_W1, L0_ffn_b1, L0_ffn_W2, L0_ffn_b2,
           L0_ffn_ln_gamma, L0_ffn_ln_beta, L0_ffn_Wr, L0_ffn_br):
    x = np.asarray(L0_attn_x, dtype=np.float32)
    B, T, _ = x.shape
    W1 = np.ascontiguousarray(np.asarray(L0_ffn_W1, dtype=np.float32))
    W2 = np.ascontiguousarray(np.asarray(L0_ffn_W2, dtype=np.float32))
    b1 = np.asarray(L0_ffn_b1, dtype=np.float32)
    b2 = np.asarray(L0_ffn_b2, dtype=np.float32)
    Wr = np.asarray(L0_ffn_Wr, dtype=np.float32)
    br = np.asarray(L0_ffn_br, dtype=np.float32)
    Wat = np.ascontiguousarray(np.asarray(L0_attn_W, dtype=np.float32))
    dec = np.asarray(L0_attn_decay_logit, dtype=np.float32)
    batt = np.asarray(L0_attn_b, dtype=np.float32)
    cw = np.asarray(L0_attn_conv_w, dtype=np.float32).reshape(D, 5)
    cb = np.asarray(L0_attn_conv_b, dtype=np.float32)
    gam = np.asarray(L0_ffn_ln_gamma, dtype=np.float32)
    bet = np.asarray(L0_ffn_ln_beta, dtype=np.float32)

    w1flat = W1.reshape(P, 32768)
    w2flat = W2.reshape(P, 32768)
    b1tile = np.ascontiguousarray(
        np.broadcast_to(b1.reshape(1, F_FF), (1024, F_FF))).reshape(P, 32768)
    b2tile = np.ascontiguousarray(
        np.broadcast_to(b2.reshape(1, D), (4096, D))).reshape(P, 32768)
    bis_datas = [w1flat, w1flat, w2flat, w2flat, b1tile, b1tile, b2tile, b2tile]
    bis_ks = [float(K_W), float(N_W - K_W + 1),
              float(K_W), float(N_W - K_W + 1),
              float(1024 * K_B1), float(1024 * (F_FF - K_B1 + 1)),
              float(4096 * K_B2), float(4096 * (D - K_B2 + 1))]

    convw_l = np.ascontiguousarray(
        cw.reshape(DCH, P, 5).transpose(1, 0, 2).reshape(P, 5 * DCH))
    wr_l = np.ascontiguousarray(
        Wr.reshape(DCH, P, 8).transpose(1, 0, 2).reshape(P, 8 * DCH))

    in_maps = []
    for c in range(8):
        b_, h_ = c // 2, c % 2
        xf = np.zeros((D, TCORE + 4), dtype=np.float32)
        lo = h_ * TCORE - 2
        hi = h_ * TCORE + TCORE + 2
        slo, shi = max(lo, 0), min(hi, T)
        xf[:, (slo - lo):(slo - lo) + (shi - slo)] = x[b_, slo:shi, :].T
        in_maps.append({
            "x_fm": np.ascontiguousarray(xf),
            "bis_data": np.ascontiguousarray(bis_datas[c]),
            "bis_k": np.full((P, 1), bis_ks[c], dtype=np.float32),
            "oddflag": np.full((P, 1), float(h_), dtype=np.float32),
            "w_attn": Wat,
            "dec_in": _chunk_layout(dec),
            "battn_in": _chunk_layout(batt),
            "convb_in": _chunk_layout(cb),
            "convw_in": convw_l,
            "w1_in": W1,
            "b1c_in": np.ascontiguousarray(b1.reshape(FCH, P).T),
            "w2_in": W2,
            "b2c_in": np.ascontiguousarray(b2.reshape(DCH, P).T),
            "gam_in": _chunk_layout(gam),
            "beta_in": _chunk_layout(bet),
            "wr_in": wr_l,
            "br_in": br.reshape(1, 8),
        })

    nc = _get_nc()
    res = run_bass_kernel_spmd(nc, in_maps, core_ids=list(range(8)))
    out = np.empty((B, T, D), dtype=np.float32)
    for c in range(8):
        b_, h_ = c // 2, c % 2
        out[b_, h_ * TCORE:(h_ + 1) * TCORE, :] = res.results[c]["out_fm"].T
    return out


# version 18
# speedup vs baseline: 1.1634x; 1.0111x over previous
import numpy as np
import contextlib
import concourse.bass as bass
import concourse.mybir as mybir
from concourse.bass_utils import run_bass_kernel_spmd
from concourse.alu_op_type import AluOpType as Op

F32 = mybir.dt.float32
BF16 = mybir.dt.bfloat16
AF = mybir.ActivationFunctionType

P = 128
D = 1024
DCH = 8
F_FF = 4096
FCH = 32
TCORE = 4096
NT = 512
NTILES = TCORE // NT
C6 = NT + 4
N_BIS = 36
GC1 = 0.7978845608028654
GC2 = GC1 * 0.044715
EPS = 1e-5

N_W = D * F_FF
K_W = int(N_W * 0.1 / 2)
K_B1 = max(1, int(F_FF * 0.1 / 2))
K_B2 = max(1, int(D * 0.1 / 2))


def build_program(debug=False, dbg_tile=0):
    nc = bass.Bass("TRN2", target_bir_lowering=False)

    def din(name, shape, dt=F32):
        return nc.dram_tensor(name, shape, dt, kind="ExternalInput").ap()

    x_fm = din("x_fm", [D, TCORE + 4])
    bis_data = din("bis_data", [P, 32768])
    bis_k = din("bis_k", [P, 1])
    oddflag = din("oddflag", [P, 1])
    w_attn = din("w_attn", [D, D])
    dec_in = din("dec_in", [P, DCH])
    battn_in = din("battn_in", [P, DCH])
    convb_in = din("convb_in", [P, DCH])
    convw_in = din("convw_in", [P, 5 * DCH])
    w1_in = din("w1_in", [D, F_FF])
    b1c_in = din("b1c_in", [P, FCH])
    w2_in = din("w2_in", [F_FF, D])
    b2c_in = din("b2c_in", [P, DCH])
    gam_in = din("gam_in", [P, DCH])
    beta_in = din("beta_in", [P, DCH])
    wr_in = din("wr_in", [P, 8 * DCH])
    br_in = din("br_in", [1, 8])

    out_fm = nc.dram_tensor("out_fm", [D, TCORE], F32, kind="ExternalOutput").ap()
    dbg = {}
    if debug:
        for nm, shape, dt in [
            ("dbg_small", [P, 64], F32), ("dbg_bis", [P, 8], F32),
            ("dbg_thrs", [P, 8], F32), ("dbg_carry", [P, 16], F32),
            ("dbg_sin", [P, DCH * NT], BF16), ("dbg_h1b", [P, DCH * (NT + 4)], BF16),
            ("dbg_x1", [P, DCH * NT], F32), ("dbg_zb", [P, DCH * NT], BF16),
            ("dbg_lg", [P, 32], F32), ("dbg_sm", [P, 16], F32),
            ("dbg_g0s", [P, NT], F32), ("dbg_y0", [P, DCH * NT], F32),
            ("dbg_stats", [P, 3 * NT], F32), ("dbg_act", [P, 2 * NT], BF16),
            ("dbg_bis0", [P, 8], F32), ("dbg_bisf", [P, 8], F32),
            ("dbg_trc", [P, 3 * N_BIS], F32), ("dbg_aw", [P, 96], F32),
            ("dbg_b2", [P, 16], F32),
        ]:
            dbg[nm] = nc.dram_tensor(nm, shape, dt, kind="ExternalOutput").ap()

    s_scr = nc.dram_tensor("s_scr", [DCH, P, TCORE], BF16)
    wt_scr = nc.dram_tensor("wt_scr", [DCH, P, D], BF16)
    w1_scr = nc.dram_tensor("w1_scr", [FCH, P, D], BF16)
    pw1_scr = nc.dram_tensor("pw1_scr", [FCH, P, D], BF16)
    w2_scr = nc.dram_tensor("w2_scr", [DCH, P, F_FF], BF16)
    pw2_scr = nc.dram_tensor("pw2_scr", [DCH, P, F_FF], BF16)
    carry_in_d = nc.dram_tensor("carry_in_d", [P, DCH], F32)
    carry_out_d = nc.dram_tensor("carry_out_d", [2 * P, DCH], F32)
    thr_in_d = nc.dram_tensor("thr_in_d", [1, 16], F32)
    thr_out_d = nc.dram_tensor("thr_out_d", [8, 16], F32)

    es = contextlib.ExitStack()
    with es:
        blk = es.enter_context(nc.Block())
        sems = {}
        cnt = {}
        for s in ("sLD", "sLB", "sWQ", "sV", "sA", "sPE", "sPO", "sCC", "sST"):
            sems[s] = es.enter_context(nc.semaphore(s))
            cnt[s] = 0

        def inc(inst, s, n):
            inst.then_inc(sems[s], n)
            cnt[s] += n
            return (s, cnt[s])

        def evt(s):
            return (s, cnt[s])

        def wt(eng, *evts):
            for e in evts:
                if e is None:
                    continue
                s, v = e
                if v > 0:
                    eng.wait_ge(sems[s], v)

        def sb(name, shape, dt=F32):
            return es.enter_context(nc.sbuf_tensor(name, shape, dt))

        ident = sb("ident", [P, P])
        ident1 = sb("ident1", [1, 1])
        ones_row = sb("ones_row", [1, NT], BF16)
        ones1b = sb("ones1b", [1, P], BF16)
        ones1f = sb("ones1f", [1, P])
        ones128f = sb("ones128f", [P, P])
        sMean = sb("sMean", [P, P], BF16)
        sSq = sb("sSq", [P, P], BF16)
        iota_row = sb("iota_row", [P, NT])
        eps_t = sb("eps_t", [P, 1])
        small = sb("small", [P, 64])
        convw = sb("convw", [P, 5 * DCH])
        convb_f = sb("convb_f", [P, DCH])
        gam_b = sb("gam_b", [P, DCH], BF16)
        beta_b = sb("beta_b", [P, DCH], BF16)
        wr_f = sb("wr_f", [P, 8 * DCH])
        wr_b = sb("wr_b", [P, 8 * DCH], BF16)
        br_f = sb("br_f", [1, 8])
        br_b = sb("br_b", [1, 8], BF16)
        c1r = sb("c1r", [1, 8], BF16)
        c2row_b = sb("c2row_b", [1, 8], BF16)
        c2bc = sb("c2bc", [P, 8])
        diag = sb("diag", [P, 40 * P], BF16)
        carry_state = sb("carry_state", [P, DCH])
        carry_eff = sb("carry_eff", [P, DCH])
        thrs = sb("thrs", [P, 8])
        thr_rows = sb("thr_rows", [1, 8])
        thrpad = sb("thrpad", [1, 16])
        spacer = sb("spacer", [P, NT])
        spacerA = spacer

        bank = [es.enter_context(nc.psum_tensor(f"bank{i}", [P, NT], F32))
                for i in range(8)]

        @blk.sync
        def _(sy):
            inc(sy.dma_start(out=small[:, 0:8], in_=dec_in), "sLD", 16)
            inc(sy.dma_start(out=small[:, 32:40], in_=battn_in), "sLD", 16)
            inc(sy.dma_start(out=convb_f[:], in_=convb_in), "sLD", 16)
            inc(sy.dma_start(out=convw[:], in_=convw_in), "sLD", 16)
            inc(sy.dma_start(out=small[:, 40:48], in_=gam_in), "sLD", 16)
            inc(sy.dma_start(out=small[:, 48:56], in_=beta_in), "sLD", 16)
            inc(sy.dma_start(out=small[:, 56:57], in_=bis_k), "sLD", 16)
            inc(sy.dma_start(out=small[:, 57:58], in_=oddflag), "sLD", 16)
            inc(sy.dma_start(out=wr_f[:], in_=wr_in), "sLD", 16)
            inc(sy.dma_start(out=br_f[:], in_=br_in), "sLD", 16)
        e_small = evt("sLD")

        @blk.gpsimd
        def _(g):
            g.memset(ones_row[:], 1.0)
            g.memset(ones1b[:], 1.0)
            g.memset(ones1f[:], 1.0)
            g.memset(ones128f[:], 1.0)
            g.memset(sMean[:], -1.0 / D)
            g.memset(sSq[:], 1.0 / D)
            g.memset(eps_t[:], EPS)
            g.memset(ident1[:], 1.0)
            g.memset(thrpad[:], 0.0)
            g.memset(carry_state[:], 0.0)
            g.iota(out=ident[:], pattern=[[1, P]], base=0, channel_multiplier=-1,
                   allow_small_or_imprecise_dtypes=True)
            inc(g.iota(out=iota_row[:], pattern=[[1, NT]], base=0,
                       channel_multiplier=0,
                       allow_small_or_imprecise_dtypes=True), "sPO", 1)
        e_const = evt("sPO")

        @blk.scalar
        def _(s):
            wt(s, e_small)
            s.activation(out=small[:, 8:16], in_=small[:, 0:8], func=AF.Sigmoid)
            s.activation(out=small[:, 16:24], in_=small[:, 0:8], func=AF.Sigmoid,
                         scale=-1.0)
            s.copy(out=spacerA[:], in_=iota_row[:])
            inc(s.activation(out=small[:, 24:32], in_=small[:, 8:16], func=AF.Ln),
                "sA", 1)
        e_sig = evt("sA")

        @blk.vector
        def _(v):
            wt(v, e_const, e_sig)
            v.tensor_scalar(out=ident[:], in0=ident[:], scalar1=0.0, scalar2=None,
                            op0=Op.is_equal)
            v.tensor_tensor(out=small[:, 32:40], in0=small[:, 32:40],
                            in1=convb_f[:], op=Op.add)
            v.tensor_copy(out=gam_b[:], in_=small[:, 40:48])
            v.tensor_copy(out=beta_b[:], in_=small[:, 48:56])
            v.tensor_copy(out=br_b[:], in_=br_f[:])
            for ch in range(DCH):
                v.tensor_scalar(out=wr_b[:, 8 * ch:8 * ch + 8],
                                in0=wr_f[:, 8 * ch:8 * ch + 8],
                                scalar1=small[:, 40 + ch:41 + ch], scalar2=None,
                                op0=Op.mult)
            for j in range(40):
                v.tensor_scalar(out=diag[:, j * P:(j + 1) * P], in0=ident[:],
                                scalar1=convw[:, j:j + 1], scalar2=None,
                                op0=Op.mult)
            inc(v.engine_nop(), "sV", 1)
        e_prolog = evt("sV")

        @blk.tensor
        def _(t):
            wt(t, e_prolog)
            for ch in range(DCH):
                t.matmul(bank[1][0:1, 0:8], gam_b[:, ch:ch + 1],
                         wr_b[:, 8 * ch:8 * ch + 8],
                         start=(ch == 0), stop=(ch == DCH - 1))
            for ch in range(DCH):
                t.matmul(bank[1][0:1, 8:16], beta_b[:, ch:ch + 1],
                         wr_b[:, 8 * ch:8 * ch + 8],
                         start=(ch == 0), stop=False)
            t.matmul(bank[1][0:1, 8:16], ones1b[0:1, 0:1], br_b[:],
                     start=False, stop=True)
            inc(t.matmul(bank[1][0:64, 128:256], wr_b[:, 0:64], diag[:, 0:P],
                         start=True, stop=True), "sPE", 1)
        e_c12 = evt("sPE")

        @blk.scalar
        def _(s):
            wt(s, e_c12)
            s.copy(out=spacerA[:], in_=iota_row[:])
            s.copy(out=c1r[:], in_=bank[1][0:1, 0:8])
            s.copy(out=c2row_b[:], in_=bank[1][0:1, 8:16])
            inc(s.copy(out=spacerA[:], in_=iota_row[:]), "sA", 1)
        e_c12s = evt("sA")

        @blk.tensor
        def _(t):
            wt(t, e_c12s)
            t.transpose(bank[2][:, 128:256], ident[:], ident[:])
            inc(t.matmul(bank[2][:, 0:8], ones1b[:], c2row_b[:],
                         start=True, stop=True), "sPE", 1)
        e_c2bcp = evt("sPE")

        @blk.scalar
        def _(s):
            wt(s, e_c2bcp)
            s.copy(out=spacerA[:], in_=iota_row[:])
            s.copy(out=c2bc[:], in_=bank[2][:, 0:8])
            inc(s.copy(out=spacerA[:], in_=iota_row[:]), "sA", 1)
        e_c2done = evt("sA")

        with contextlib.ExitStack() as es_s:
            aw = es_s.enter_context(nc.sbuf_tensor("aw", [P, 32768], F32))
            trc = es_s.enter_context(nc.sbuf_tensor("trc", [P, 3 * N_BIS], F32)) \
                if debug else None
            dump = es_s.enter_context(nc.sbuf_tensor("dump", [P, 8192], BF16))
            dump2 = es_s.enter_context(nc.sbuf_tensor("dump2", [P, 8192], BF16))
            loW = es_s.enter_context(nc.sbuf_tensor("loW", [P, NT], F32))
            hiW = es_s.enter_context(nc.sbuf_tensor("hiW", [P, NT], F32))
            midW = es_s.enter_context(nc.sbuf_tensor("midW", [P, NT], F32))
            totW = es_s.enter_context(nc.sbuf_tensor("totW", [P, NT], F32))
            prW = es_s.enter_context(nc.sbuf_tensor("prW", [P, NT], F32))
            pr2W = es_s.enter_context(nc.sbuf_tensor("pr2W", [P, NT], F32))
            t6W = es_s.enter_context(nc.sbuf_tensor("t6W", [P, NT], F32))
            t7W = es_s.enter_context(nc.sbuf_tensor("t7W", [P, NT], F32))
            kW = es_s.enter_context(nc.sbuf_tensor("kW", [P, NT], F32))
            bis = es_s.enter_context(nc.sbuf_tensor("bis", [P, 8], F32))
            maxrow = es_s.enter_context(nc.sbuf_tensor("maxrow", [1, P], F32))
            maxsc = es_s.enter_context(nc.sbuf_tensor("maxsc", [1, 1], F32))
            maxb = es_s.enter_context(nc.sbuf_tensor("maxb", [1, 1], BF16))

            @blk.sync
            def _(sy):
                for q in range(4):
                    inc(sy.dma_start(out=aw[:, q * 8192:(q + 1) * 8192],
                                     in_=bis_data[:, q * 8192:(q + 1) * 8192]),
                        "sLD", 16)
            e_bld = evt("sLD")

            @blk.scalar
            def _(s):
                wt(s, e_bld)
                s.activation(out=aw[:, 0:16384], in_=aw[:, 0:16384], func=AF.Abs)
                inc(s.activation(out=aw[:, 16384:32768], in_=aw[:, 16384:32768],
                                 func=AF.Abs), "sA", 1)
            e_abs = evt("sA")

            @blk.vector
            def _(v):
                wt(v, e_abs)
                for q in range(4):
                    v.tensor_reduce(out=bis[:, 3 + q:4 + q],
                                    in_=aw[:, q * 8192:(q + 1) * 8192],
                                    axis=mybir.AxisListType.X, op=Op.max)
                v.tensor_copy(out=spacer[:], in_=iota_row[:])
                v.tensor_tensor(out=bis[:, 6:7], in0=bis[:, 3:4], in1=bis[:, 4:5],
                                op=Op.max)
                v.tensor_tensor(out=bis[:, 6:7], in0=bis[:, 6:7], in1=bis[:, 5:6],
                                op=Op.max)
                v.tensor_copy(out=spacer[:], in_=iota_row[:])
                v.tensor_tensor(out=bis[:, 6:7], in0=bis[:, 6:7], in1=bis[:, 3 + 3:4 + 3],
                                op=Op.max)
                inc(v.tensor_copy(out=spacer[:], in_=iota_row[:]), "sV", 1)
            e_pmax = evt("sV")

            if debug:
                @blk.gpsimd
                def _(g):
                    wt(g, e_pmax)
                    inc(g.dma_start(out=dbg["dbg_aw"][:, 0:32],
                                    in_=aw[:, 16368:16400]), "sWQ", 16)
                    inc(g.dma_start(out=dbg["dbg_aw"][:, 32:64],
                                    in_=aw[:, 32700:32732]), "sWQ", 16)
                    inc(g.dma_start(out=dbg["dbg_aw"][:, 64:96],
                                    in_=aw[:, 24000:24032]), "sWQ", 16)

            @blk.vector
            def _(v):
                v.tensor_tensor(out=bis[:, 5:6], in0=bis[:, 6:7],
                                in1=bis[:, 6:7], op=Op.mult)
                inc(v.tensor_copy(out=spacer[:], in_=iota_row[:]), "sV", 1)
            e_pm2 = evt("sV")

            @blk.tensor
            def _(t):
                wt(t, e_pm2)
                inc(t.matmul(bank[3][:, 0:1], ones128f[:], bis[:, 5:6],
                             start=True, stop=True), "sPE", 1)
            e_pmsum = evt("sPE")

            @blk.scalar
            def _(s):
                wt(s, e_pmsum)
                s.copy(out=spacerA[:], in_=iota_row[:])
                s.activation(out=bis[:, 1:2], in_=bank[3][:, 0:1], func=AF.Sqrt)
                inc(s.copy(out=spacerA[:], in_=iota_row[:]), "sA", 1)
            e_gmax = evt("sA")

            @blk.vector
            def _(v):
                wt(v, e_gmax)
                v.tensor_copy(out=spacer[:], in_=iota_row[:])
                v.tensor_scalar(out=hiW[:], in0=iota_row[:], scalar1=0.0,
                                scalar2=bis[:, 1:2], op0=Op.mult, op1=Op.add)
                v.tensor_scalar(out=hiW[:], in0=hiW[:], scalar1=1.05,
                                scalar2=None, op0=Op.mult)
                v.memset(loW[:], -1.0)
                v.tensor_scalar(out=kW[:], in0=iota_row[:], scalar1=0.0,
                                scalar2=small[:, 56:57], op0=Op.mult, op1=Op.add)
                v.tensor_scalar(out=kW[:], in0=kW[:], scalar1=1.0 / D,
                                scalar2=None, op0=Op.mult)
                v.tensor_tensor(out=midW[:], in0=loW[:], in1=hiW[:], op=Op.add)
                inc(v.tensor_scalar(out=midW[:], in0=midW[:], scalar1=0.5,
                                    scalar2=None, op0=Op.mult), "sV", 1)

            dumps = [dump, dump2]
            peU = [None, None]
            e_postR = None
            for it in range(N_BIS):
                e_cmp = []
                for q in range(4):
                    @blk.vector
                    def _(v):
                        wt(v, peU[q % 2])
                        inc(v.tensor_scalar(out=dumps[q % 2][:],
                                            in0=aw[:, q * 8192:(q + 1) * 8192],
                                            scalar1=midW[:, 0:1], scalar2=None,
                                            op0=Op.is_le), "sV", 1)
                    e_cmp.append(evt("sV"))
                for q in range(4):
                    @blk.tensor
                    def _(t):
                        wt(t, e_cmp[q])
                        if q == 0:
                            wt(t, e_postR)
                        mm = None
                        for c in range(16):
                            mm = t.matmul(bank[3][:, 0:NT], sSq[:],
                                          dumps[q % 2][:, c * NT:(c + 1) * NT],
                                          start=(q == 0 and c == 0),
                                          stop=(q == 3 and c == 15))
                        inc(mm, "sPE", 1)
                    peU[q % 2] = evt("sPE")
                e_tt = peU[1]

                @blk.vector
                def _(v):
                    wt(v, e_tt)
                    v.tensor_copy(out=spacer[:], in_=iota_row[:])
                    v.tensor_scalar(out=t6W[:], in0=bank[3][:, 0:NT],
                                    scalar1=1.0, scalar2=0.0, op0=Op.mult,
                                    op1=Op.add, accum_out=bis[:, 3:4])
                    v.tensor_copy(out=spacer[:], in_=iota_row[:])
                    v.tensor_scalar(out=totW[:], in0=iota_row[:], scalar1=0.0,
                                    scalar2=bis[:, 3:4], op0=Op.mult,
                                    op1=Op.add)
                    v.tensor_tensor(out=prW[:], in0=totW[:], in1=kW[:],
                                    op=Op.is_lt)
                    v.tensor_scalar(out=pr2W[:], in0=prW[:], scalar1=-1.0,
                                    scalar2=1.0, op0=Op.mult, op1=Op.add)
                    v.tensor_tensor(out=t6W[:], in0=midW[:], in1=loW[:],
                                    op=Op.subtract)
                    v.tensor_tensor(out=t6W[:], in0=t6W[:], in1=prW[:],
                                    op=Op.mult)
                    v.tensor_tensor(out=loW[:], in0=loW[:], in1=t6W[:],
                                    op=Op.add)
                    v.tensor_tensor(out=t7W[:], in0=midW[:], in1=hiW[:],
                                    op=Op.subtract)
                    v.tensor_tensor(out=t7W[:], in0=t7W[:], in1=pr2W[:],
                                    op=Op.mult)
                    v.tensor_tensor(out=hiW[:], in0=hiW[:], in1=t7W[:],
                                    op=Op.add)
                    v.tensor_tensor(out=midW[:], in0=loW[:], in1=hiW[:],
                                    op=Op.add)
                    inc(v.tensor_scalar(out=midW[:], in0=midW[:], scalar1=0.5,
                                        scalar2=None, op0=Op.mult), "sV", 1)
                e_postR = evt("sV")
            @blk.vector
            def _(v):
                v.tensor_copy(out=spacer[:], in_=iota_row[:])
                v.tensor_copy(out=thrpad[0:1, 0:1], in_=hiW[0:1, 0:1])
                inc(v.tensor_copy(out=spacer[:], in_=iota_row[:]), "sV", 1)
            e_bisd = evt("sV")

            @blk.gpsimd
            def _(g):
                wt(g, e_bisd)
                inc(g.dma_start(out=thr_in_d.ap(), in_=thrpad[:]), "sWQ", 16)
                if debug:
                    inc(g.dma_start(out=dbg["dbg_bisf"], in_=bis[:]), "sWQ", 16)
                    inc(g.dma_start(out=dbg["dbg_trc"], in_=trc[:]), "sWQ", 16)
            e_thrpub = evt("sWQ")

        @blk.gpsimd
        def _(g):
            wt(g, e_thrpub)
            inc(g.collective_compute(
                "AllGather", Op.bypass,
                replica_groups=[[0, 1, 2, 3, 4, 5, 6, 7]],
                ins=[thr_in_d.ap().opt()], outs=[thr_out_d.ap().opt()]),
                "sCC", 1)
        e_thcc = evt("sCC")

        with contextlib.ExitStack() as es_a:
            def sba(name, shape, dt=F32):
                return es_a.enter_context(nc.sbuf_tensor(name, shape, dt))
            xaA = sba("xaA", [P, DCH * NT])
            xb16A = sba("xb16A", [P, DCH * NT], BF16)
            xsqA = sba("xsqA", [P, DCH * NT], BF16)
            a_bc = sba("a_bcA", [P, DCH * NT])
            negmA = sba("negmA", [P, NT])
            rA = sba("rA", [P, NT])
            varA = sba("varA", [P, NT])
            h1f = sba("h1fA", [P, NT])
            uf = sba("ufA", [P, NT])
            sf = sba("sfA", [P, NT])
            sb16 = [sba(f"sb16A{i}", [P, NT], BF16) for i in range(2)]

            @blk.vector
            def _(v):
                wt(v, e_sig, e_bisd)
                for ch in range(DCH):
                    v.tensor_scalar(out=a_bc[:, ch * NT:(ch + 1) * NT],
                                    in0=iota_row[:], scalar1=0.0,
                                    scalar2=small[:, 8 + ch:9 + ch],
                                    op0=Op.mult, op1=Op.add)
                inc(v.engine_nop(), "sV", 1)
            e_abc = evt("sV")

            sdma_evt = [None, None]
            e_var_prev = None
            e_negm_prev = None
            e_xaA_free = None
            for ti in range(NTILES):
                c0 = ti * NT + 2

                @blk.sync
                def _(sy):
                    if ti == 0:
                        wt(sy, e_bisd, e_thrpub)
                    wt(sy, e_xaA_free)
                    for ch in range(DCH):
                        inc(sy.dma_start(out=xaA[:, ch * NT:(ch + 1) * NT],
                                         in_=x_fm[ch * P:(ch + 1) * P, c0:c0 + NT]),
                            "sLD", 16)
                e_x = evt("sLD")

                @blk.vector
                def _(v):
                    wt(v, e_x, e_abc)
                    inc(v.tensor_copy(out=xb16A[:], in_=xaA[:]), "sV", 1)
                e_cast = evt("sV")

                @blk.scalar
                def _(s):
                    wt(s, e_x)
                    inc(s.activation(out=xsqA[:], in_=xaA[:], func=AF.Square),
                        "sA", 1)
                e_sq = evt("sA")

                @blk.tensor
                def _(t):
                    wt(t, e_cast, e_sq, e_var_prev, e_negm_prev)
                    for ch in range(DCH):
                        t.matmul(bank[4][:], sMean[:],
                                 xb16A[:, ch * NT:(ch + 1) * NT],
                                 start=(ch == 0), stop=(ch == DCH - 1))
                    for ch in range(DCH):
                        mm = t.matmul(bank[5][:], sSq[:],
                                      xsqA[:, ch * NT:(ch + 1) * NT],
                                      start=(ch == 0), stop=(ch == DCH - 1))
                    inc(mm, "sPE", 1)
                e_stat = evt("sPE")

                @blk.scalar
                def _(s):
                    wt(s, e_stat)
                    inc(s.copy(out=negmA[:], in_=bank[4][:]), "sA", 1)
                e_negm_prev = evt("sA")
                e_negm = e_negm_prev

                @blk.vector
                def _(v):
                    wt(v, e_negm)
                    v.tensor_tensor(out=varA[:], in0=negmA[:], in1=negmA[:],
                                    op=Op.mult)
                    inc(v.tensor_tensor(out=varA[:], in0=bank[5][:], in1=varA[:],
                                        op=Op.subtract), "sV", 1)
                e_var_prev = evt("sV")
                e_var = e_var_prev

                @blk.scalar
                def _(s):
                    wt(s, e_var)
                    s.activation(out=rA[:], in_=varA[:], func=AF.Ln, bias=eps_t[:])
                    inc(s.activation(out=rA[:], in_=rA[:], func=AF.Exp,
                                     scale=-0.5), "sA", 1)
                e_rs = evt("sA")

                for ch in range(DCH):
                    par = ch % 2

                    @blk.vector
                    def _(v):
                        if ch == 0:
                            wt(v, e_rs)
                        wt(v, sdma_evt[par])
                        v.tensor_tensor(out=h1f[:],
                                        in0=xaA[:, ch * NT:(ch + 1) * NT],
                                        in1=negmA[:], op=Op.add)
                        v.tensor_tensor(out=h1f[:], in0=h1f[:], in1=rA[:],
                                        op=Op.mult)
                        v.tensor_scalar(out=uf[:], in0=h1f[:],
                                        scalar1=small[:, 16 + ch:17 + ch],
                                        scalar2=None, op0=Op.mult)
                        v.tensor_tensor_scan(out=sf[:],
                                             data0=a_bc[:, ch * NT:(ch + 1) * NT],
                                             data1=uf[:],
                                             initial=carry_state[:, ch:ch + 1],
                                             op0=Op.mult, op1=Op.add)
                        v.tensor_copy(out=sb16[par][:], in_=sf[:])
                        inc(v.tensor_copy(out=carry_state[:, ch:ch + 1],
                                          in_=sf[:, NT - 1:NT]), "sV", 1)
                    e_sc = evt("sV")

                    @blk.gpsimd
                    def _(g):
                        wt(g, e_sc)
                        inc(g.dma_start(
                            out=s_scr.ap()[ch, :, ti * NT:(ti + 1) * NT],
                            in_=sb16[par][:]), "sWQ", 16)
                    sdma_evt[par] = evt("sWQ")
                e_xaA_free = evt("sV")

            @blk.vector
            def _(v):
                wt(v, sdma_evt[0], sdma_evt[1])
                inc(v.engine_nop(), "sV", 1)
            e_phA = evt("sV")

        @blk.gpsimd
        def _(g):
            wt(g, e_thcc)
            for r in range(8):
                inc(g.dma_start(out=thr_rows[0:1, r:r + 1],
                                in_=thr_out_d.ap()[r:r + 1, 0:1]), "sWQ", 16)
        e_thg = evt("sWQ")

        @blk.gpsimd
        def _(g):
            wt(g, e_phA)
            e1 = inc(g.dma_start(out=carry_in_d.ap(), in_=carry_state[:]),
                     "sWQ", 16)
            wt(g, e1)
            inc(g.collective_compute(
                "AllGather", Op.bypass,
                replica_groups=[[0, 1], [2, 3], [4, 5], [6, 7]],
                ins=[carry_in_d.ap().opt()], outs=[carry_out_d.ap().opt()]),
                "sCC", 1)
            wt(g, evt("sCC"))
            inc(g.dma_start(out=carry_eff[:], in_=carry_out_d.ap()[0:P, :]),
                "sWQ", 16)
        e_ccd = evt("sWQ")

        @blk.vector
        def _(v):
            wt(v, e_ccd)
            inc(v.tensor_scalar(out=carry_eff[:], in0=carry_eff[:],
                                scalar1=small[:, 57:58], scalar2=None,
                                op0=Op.mult), "sV", 1)
        e_ceff = evt("sV")

        @blk.tensor
        def _(t):
            wt(t, e_thg)
            mm = None
            for r in range(8):
                mm = t.matmul(bank[3][:, r:r + 1], ones1f[:],
                              thr_rows[0:1, r:r + 1], start=True, stop=True)
            inc(mm, "sPE", 1)
        e_thbc = evt("sPE")

        @blk.vector
        def _(v):
            wt(v, e_thbc)
            v.tensor_copy(out=spacer[:], in_=iota_row[:])
            v.tensor_copy(out=thrs[:], in_=bank[3][:, 0:8])
            inc(v.tensor_copy(out=spacer[:], in_=iota_row[:]), "sV", 1)
        e_thrs = evt("sV")

        if debug:
            @blk.gpsimd
            def _(g):
                wt(g, e_thrs, e_ceff, e_sig)
                inc(g.dma_start(out=dbg["dbg_small"], in_=small[:]), "sWQ", 16)
                inc(g.dma_start(out=dbg["dbg_thrs"], in_=thrs[:]), "sWQ", 16)
                inc(g.dma_start(out=dbg["dbg_carry"][:, 0:8], in_=carry_state[:]),
                    "sWQ", 16)
                inc(g.dma_start(out=dbg["dbg_carry"][:, 8:16], in_=carry_eff[:]),
                    "sWQ", 16)

        b1c = sb("b1cK", [P, FCH])
        pb1c = sb("pb1cK", [P, FCH])
        b2c = sb("b2cK", [P, DCH])
        pb2c = sb("pb2cK", [P, DCH])
        with contextlib.ExitStack() as es_m:
            def sbm(name, shape, dt=F32):
                return es_m.enter_context(nc.sbuf_tensor(name, shape, dt))
            wrow = [sbm(f"wrowM{i}", [P, D]) for i in range(2)]
            wrb16 = [sbm(f"wrbM{i}", [P, D], BF16) for i in range(2)]
            wtT = [sbm(f"wtTM{i}", [P, DCH, P], BF16) for i in range(2)]
            awb = sbm("awbM", [P, D])
            t1m = sbm("t1M", [P, D])
            t2m = sbm("t2M", [P, D])
            pwb = [sbm(f"pwbM{i}", [P, D], BF16) for i in range(2)]

            use_evt = [None, None]
            wq_evt = [None, None]
            for r in range(DCH):
                pr = r % 2

                @blk.sync
                def _(sy):
                    if r == 0:
                        wt(sy, e_phA)
                    wt(sy, use_evt[pr])
                    e_l = inc(sy.dma_start(out=wrow[pr][:],
                                           in_=w_attn[r * P:(r + 1) * P, :]),
                              "sLD", 16)
                e_ld = evt("sLD")

                @blk.vector
                def _(v):
                    wt(v, e_ld, wq_evt[pr])
                    inc(v.tensor_copy(out=wrb16[pr][:], in_=wrow[pr][:]), "sV", 1)
                use_evt[pr] = evt("sV")
                e_cast = evt("sV")

                @blk.sync
                def _(sy):
                    wt(sy, e_cast)
                    inc(sy.dma_start_transpose(out=wtT[pr][:], in_=wrb16[pr][:]),
                        "sLD", 16)
                e_tr = evt("sLD")

                @blk.sync
                def _(sy):
                    wt(sy, e_tr)
                    inc(sy.dma_start(out=wt_scr.ap()[r],
                                     in_=wtT[pr][:].rearrange("p m q -> p (m q)")),
                        "sWQ", 16)
                wq_evt[pr] = evt("sWQ")

            use_evt = [evt("sWQ"), evt("sWQ")]
            for f in range(FCH):
                pr = f % 2

                @blk.sync
                def _(sy):
                    wt(sy, use_evt[pr])
                    for k in range(DCH):
                        inc(sy.dma_start(
                            out=wrow[pr][:, k * P:(k + 1) * P],
                            in_=w1_in[k * P:(k + 1) * P, f * P:(f + 1) * P]),
                            "sLD", 16)
                e_ld = evt("sLD")

                @blk.scalar
                def _(s):
                    wt(s, e_ld, e_thrs)
                    inc(s.activation(out=awb[:], in_=wrow[pr][:], func=AF.Abs),
                        "sA", 1)
                e_ab = evt("sA")

                @blk.vector
                def _(v):
                    wt(v, e_ab)
                    v.tensor_scalar(out=t1m[:], in0=awb[:], scalar1=thrs[:, 0:1],
                                    scalar2=0.8, op0=Op.is_le, op1=Op.mult)
                    v.tensor_scalar(out=t2m[:], in0=awb[:], scalar1=thrs[:, 1:2],
                                    scalar2=-0.8, op0=Op.is_ge, op1=Op.mult)
                    v.scalar_tensor_tensor(out=t1m[:], in0=t1m[:], scalar=1.0,
                                           in1=t2m[:], op0=Op.add, op1=Op.add)
                    v.tensor_tensor(out=pwb[pr][:], in0=wrow[pr][:], in1=t1m[:],
                                    op=Op.mult)
                    inc(v.tensor_copy(out=wrb16[pr][:], in_=wrow[pr][:]), "sV", 1)
                e_wv = evt("sV")

                @blk.gpsimd
                def _(g):
                    wt(g, e_wv)
                    inc(g.dma_start(out=w1_scr.ap()[f], in_=wrb16[pr][:]),
                        "sWQ", 16)
                    inc(g.dma_start(out=pw1_scr.ap()[f], in_=pwb[pr][:]),
                        "sWQ", 16)
                use_evt[pr] = evt("sWQ")

            for fc in range(FCH):
                pr = fc % 2

                @blk.sync
                def _(sy):
                    wt(sy, use_evt[pr])
                    inc(sy.dma_start(out=wrow[pr][:],
                                     in_=w2_in[fc * P:(fc + 1) * P, :]), "sLD", 16)
                e_ld = evt("sLD")

                @blk.scalar
                def _(s):
                    wt(s, e_ld)
                    inc(s.activation(out=awb[:], in_=wrow[pr][:], func=AF.Abs),
                        "sA", 1)
                e_ab = evt("sA")

                @blk.vector
                def _(v):
                    wt(v, e_ab)
                    v.tensor_scalar(out=t1m[:], in0=awb[:], scalar1=thrs[:, 2:3],
                                    scalar2=0.8, op0=Op.is_le, op1=Op.mult)
                    v.tensor_scalar(out=t2m[:], in0=awb[:], scalar1=thrs[:, 3:4],
                                    scalar2=-0.8, op0=Op.is_ge, op1=Op.mult)
                    v.scalar_tensor_tensor(out=t1m[:], in0=t1m[:], scalar=1.0,
                                           in1=t2m[:], op0=Op.add, op1=Op.add)
                    v.tensor_tensor(out=pwb[pr][:], in0=wrow[pr][:], in1=t1m[:],
                                    op=Op.mult)
                    inc(v.tensor_scalar(out=wrb16[pr][:], in0=wrow[pr][:],
                                        scalar1=1.0, scalar2=None, op0=Op.mult),
                        "sV", 1)
                e_wv = evt("sV")

                @blk.gpsimd
                def _(g):
                    wt(g, e_wv)
                    inc(g.dma_start(
                        out=w2_scr.ap()[:, :, fc * P:(fc + 1) * P]
                        .rearrange("d p m -> p d m"),
                        in_=wrb16[pr][:].rearrange("p (d m) -> p d m", d=DCH)),
                        "sWQ", 16)
                    inc(g.dma_start(
                        out=pw2_scr.ap()[:, :, fc * P:(fc + 1) * P]
                        .rearrange("d p m -> p d m"),
                        in_=pwb[pr][:].rearrange("p (d m) -> p d m", d=DCH)),
                        "sWQ", 16)
                use_evt[pr] = evt("sWQ")

            @blk.sync
            def _(sy):
                inc(sy.dma_start(out=b1c[:], in_=b1c_in), "sLD", 16)
                inc(sy.dma_start(out=b2c[:], in_=b2c_in), "sLD", 16)
            e_bc = evt("sLD")

            @blk.scalar
            def _(s):
                wt(s, e_bc)
                s.activation(out=awb[:, 0:FCH], in_=b1c[:], func=AF.Abs)
                s.activation(out=awb[:, 64:64 + DCH], in_=b2c[:], func=AF.Abs)
                inc(s.copy(out=t2m[:, 0:NT], in_=awb[:, 0:NT]), "sA", 1)
            e_abc = evt("sA")

            @blk.vector
            def _(v):
                wt(v, e_abc, e_thrs)
                v.tensor_scalar(out=t1m[:, 0:FCH], in0=awb[:, 0:FCH],
                                scalar1=thrs[:, 4:5], scalar2=0.8,
                                op0=Op.is_le, op1=Op.mult)
                v.tensor_scalar(out=t1m[:, 64:64 + FCH], in0=awb[:, 0:FCH],
                                scalar1=thrs[:, 5:6], scalar2=-0.8,
                                op0=Op.is_ge, op1=Op.mult)
                v.tensor_scalar(out=t1m[:, 128:128 + DCH],
                                in0=awb[:, 64:64 + DCH],
                                scalar1=thrs[:, 6:7], scalar2=0.8,
                                op0=Op.is_le, op1=Op.mult)
                v.tensor_scalar(out=t1m[:, 192:192 + DCH],
                                in0=awb[:, 64:64 + DCH],
                                scalar1=thrs[:, 7:8], scalar2=-0.8,
                                op0=Op.is_ge, op1=Op.mult)
                v.tensor_copy(out=t2m[:, 0:NT], in_=t1m[:, 0:NT])
                v.scalar_tensor_tensor(out=t1m[:, 0:FCH], in0=t1m[:, 0:FCH],
                                       scalar=1.0, in1=t1m[:, 64:64 + FCH],
                                       op0=Op.add, op1=Op.add)
                v.scalar_tensor_tensor(out=t1m[:, 128:128 + DCH],
                                       in0=t1m[:, 128:128 + DCH], scalar=1.0,
                                       in1=t1m[:, 192:192 + DCH],
                                       op0=Op.add, op1=Op.add)
                v.tensor_tensor(out=pb1c[:], in0=b1c[:], in1=t1m[:, 0:FCH],
                                op=Op.mult)
                v.tensor_tensor(out=pb2c[:], in0=b2c[:],
                                in1=t1m[:, 128:128 + DCH], op=Op.mult)
                inc(v.tensor_copy(out=t2m[:, 0:NT], in_=t1m[:, 0:NT]), "sV", 1)
            e_m_done = evt("sWQ")
            e_m_v = evt("sV")
            e_m_a = evt("sA")

        xa = sb("xaB", [P, DCH * C6])
        castA = sb("castAB", [P, DCH * C6], BF16)
        sqA = sb("sqAB", [P, DCH * C6], BF16)
        h1b = sb("h1bB", [P, DCH * C6], BF16)
        s_in = sb("s_inB", [P, DCH * NT], BF16)
        x1 = [sb(f"x1B{i}", [P, DCH * NT]) for i in range(2)]
        zbuf = [sb(f"zbB{i}", [P, DCH * NT], BF16) for i in range(2)]
        act_sb = sb("actB", [P, FCH * NT], BF16)
        y0s = sb("y0sB", [P, DCH * NT])
        negm_s = sb("negmB", [P, NT])
        r_s = sb("rB", [P, NT])
        negmhW = sb("negmhB", [P, NT])
        rhW = sb("rhB", [P, NT])
        hsc = sb("hscsB", [P, 4])
        pw_one = sb("pwB0", [P, NT])
        pw_t = [pw_one, pw_one]
        bct_all = sb("bctB", [P, 8])
        qs = sb("qsB", [P, NT])
        vs = sb("vsB", [P, NT])
        h1tmp = sb("h1tmpB", [P, NT])
        ws = h1tmp
        hscW = qs
        var_s = vs
        lg = sb("lgB", [P, 32])
        sm = sb("smB", [P, 16])
        g0r = sb("g0rB", [1, NT], BF16)
        rzc = sb("rzcB", [P, 4])
        nmzr = sb("nmzrB", [1, NT], BF16)
        g0s = [sb(f"g0sB{i}", [P, NT]) for i in range(2)]
        wbuf = [sb(f"wbufB{i}", [P, D], BF16) for i in range(3)]
        wtbuf = wbuf
        w2buf = [sb(f"w2bufB{i}", [P, 16 * P], BF16) for i in range(4)]
        otile = [pw_one, rhW]

        gateB = (e_m_done, e_m_v, e_m_a, e_ceff, e_thrs, e_c2done)

        R = {
            "xa_free": None, "sin_free": None, "castA_free": None,
            "sqA_free": None, "b0_free": None, "b1_free": None,
            "b2_free": None, "b7_free": None,
            "zb_free": [None, None], "x1_free": [None, None],
            "g0s_free": [None, None],
            "att_free": [None, None], "pre_free": [None, None],
            "y_free": [None, None], "wt_roll": [None, None],
            "w1_roll": [None, None, None], "w2_roll": [None, None],
            "ot_dma": [None, None], "y0s_free": None, "act_free": None,
        }

        def front_a(ti):
            cs = ti * NT
            pz = ti % 2

            @blk.sync
            def _(sy):
                if ti == 0:
                    wt(sy, *gateB)
                wt(sy, R["xa_free"], R["sin_free"])
                for ch in range(DCH):
                    inc(sy.dma_start(out=xa[:, ch * C6:ch * C6 + C6],
                                     in_=x_fm[ch * P:(ch + 1) * P, cs:cs + C6]),
                        "sLD", 16)
                for ch in range(DCH):
                    inc(sy.dma_start(out=s_in[:, ch * NT:(ch + 1) * NT],
                                     in_=s_scr.ap()[ch, :, ti * NT:(ti + 1) * NT]),
                        "sLD", 16)
            e_x = evt("sLD")

            @blk.vector
            def _(v):
                if ti == 0:
                    wt(v, *gateB)
                wt(v, e_x, R["castA_free"])
                inc(v.tensor_copy(out=castA[:], in_=xa[:]), "sV", 1)
            e_cast = evt("sV")

            @blk.vector
            def _(v):
                if ti == 0:
                    wt(v, *gateB)
                wt(v, e_x, R["sqA_free"])
                inc(v.tensor_tensor(out=sqA[:], in0=xa[:], in1=xa[:],
                                    op=Op.mult), "sV", 1)
            e_sq = evt("sV")

            @blk.tensor
            def _(t):
                if ti == 0:
                    wt(t, *gateB)
                wt(t, e_cast, e_sq, R["b0_free"], R["b1_free"], R["b2_free"])
                for ch in range(DCH):
                    t.matmul(bank[0][:], sMean[:],
                             castA[:, ch * C6 + 2:ch * C6 + 2 + NT],
                             start=(ch == 0), stop=(ch == DCH - 1))
                for ch in range(DCH):
                    t.matmul(bank[1][:], sSq[:],
                             sqA[:, ch * C6 + 2:ch * C6 + 2 + NT],
                             start=(ch == 0), stop=(ch == DCH - 1))
                for ch in range(DCH):
                    t.matmul(bank[2][:, 0:2], sMean[:],
                             castA[:, ch * C6:ch * C6 + 2],
                             start=(ch == 0), stop=(ch == DCH - 1))
                for ch in range(DCH):
                    t.matmul(bank[2][:, 2:4], sMean[:],
                             castA[:, ch * C6 + 514:ch * C6 + 516],
                             start=(ch == 0), stop=(ch == DCH - 1))
                for ch in range(DCH):
                    t.matmul(bank[2][:, 4:6], sSq[:],
                             sqA[:, ch * C6:ch * C6 + 2],
                             start=(ch == 0), stop=(ch == DCH - 1))
                for ch in range(DCH):
                    t.matmul(bank[2][:, 6:8], sSq[:],
                             sqA[:, ch * C6 + 514:ch * C6 + 516],
                             start=(ch == 0), stop=(ch == DCH - 1))
                inc(t.matmul(bank[2][:, 128:384], sMean[:], castA[:, 0:256],
                             start=True, stop=True), "sPE", 1)
            e_stat1 = evt("sPE")

            @blk.vector
            def _(v):
                wt(v, e_stat1, R["ot_dma"][0], R["ot_dma"][1])
                v.tensor_copy(out=spacer[:], in_=iota_row[:])
                v.tensor_scalar(out=negm_s[:], in0=bank[0][:], scalar1=1.0,
                                scalar2=None, op0=Op.mult)
                v.tensor_copy(out=negmhW[:], in_=bank[2][:])
                v.tensor_tensor(out=var_s[:], in0=negm_s[:], in1=negm_s[:],
                                op=Op.mult)
                v.tensor_tensor(out=var_s[:], in0=bank[1][:], in1=var_s[:],
                                op=Op.subtract)
                v.tensor_tensor(out=hscW[:], in0=negmhW[:], in1=negmhW[:],
                                op=Op.mult)
                v.tensor_tensor(out=rhW[:, 0:NT - 4], in0=bank[2][:, 4:NT],
                                in1=hscW[:, 0:NT - 4], op=Op.subtract)
                inc(v.tensor_copy(out=spacer[:], in_=iota_row[:]), "sV", 1)
            e_var1 = evt("sV")
            e_negm1 = e_var1

            @blk.scalar
            def _(s):
                wt(s, e_var1)
                s.activation(out=rhW[:, 0:8], in_=rhW[:, 0:8], func=AF.Ln,
                             bias=eps_t[:])
                s.activation(out=r_s[:], in_=var_s[:], func=AF.Ln, bias=eps_t[:])
                s.activation(out=rhW[:, 0:8], in_=rhW[:, 0:8], func=AF.Exp,
                             scale=-0.5)
                inc(s.activation(out=r_s[:], in_=r_s[:], func=AF.Exp, scale=-0.5),
                    "sA", 1)
            e_rs1 = evt("sA")

            @blk.vector
            def _(v):
                wt(v, e_rs1)
                for ch in range(DCH):
                    v.tensor_tensor(out=h1tmp[:],
                                    in0=xa[:, ch * C6 + 2:ch * C6 + 2 + NT],
                                    in1=negm_s[:], op=Op.add)
                    v.tensor_tensor(out=h1b[:, ch * C6 + 2:ch * C6 + 2 + NT],
                                    in0=h1tmp[:], in1=r_s[:], op=Op.mult)
                    v.tensor_tensor(out=hsc[:, 0:2],
                                    in0=xa[:, ch * C6:ch * C6 + 2],
                                    in1=negmhW[:, 0:2], op=Op.add)
                    v.tensor_tensor(out=hsc[:, 2:4],
                                    in0=xa[:, ch * C6 + 514:ch * C6 + 516],
                                    in1=negmhW[:, 2:4], op=Op.add)
                    v.tensor_copy(out=spacer[:], in_=iota_row[:])
                    v.tensor_tensor(out=h1b[:, ch * C6:ch * C6 + 2],
                                    in0=hsc[:, 0:2], in1=rhW[:, 0:2], op=Op.mult)
                    v.tensor_tensor(out=h1b[:, ch * C6 + 514:ch * C6 + 516],
                                    in0=hsc[:, 2:4], in1=rhW[:, 2:4], op=Op.mult)
                inc(v.engine_nop(), "sV", 1)
            e_h1 = evt("sV")

            @blk.scalar
            def _(s):
                for ch in range(DCH):
                    s.activation(out=bct_all[:, ch:ch + 1],
                                 in_=small[:, 24 + ch:25 + ch],
                                 func=AF.Copy, scale=float(ti * NT + 1))
                inc(s.copy(out=spacerA[:], in_=iota_row[:]), "sA", 1)
            e_pw = [None, None]
            for ch in range(DCH):
                par = ch % 2

                @blk.scalar
                def _(s):
                    wt(s, e_pw[par])
                    if ch == 0:
                        wt(s, e_x)
                    inc(s.activation(out=pw_t[par][:], in_=iota_row[:],
                                     func=AF.Exp,
                                     scale=small[:, 24 + ch:25 + ch],
                                     bias=bct_all[:, ch:ch + 1]), "sA", 1)
                e_pwa = evt("sA")

                @blk.vector
                def _(v):
                    wt(v, e_pwa)
                    inc(v.scalar_tensor_tensor(
                        out=s_in[:, ch * NT:(ch + 1) * NT], in0=pw_t[par][:],
                        scalar=carry_eff[:, ch:ch + 1],
                        in1=s_in[:, ch * NT:(ch + 1) * NT],
                        op0=Op.mult, op1=Op.add), "sV", 1)
                e_pw[par] = evt("sV")
            e_scorr = evt("sV")
            return {"e_x": e_x, "e_cast": e_cast, "e_sq": e_sq,
                    "e_stat1": e_stat1, "e_negm1": e_negm1, "e_var1": e_var1,
                    "e_rs1": e_rs1, "e_h1": e_h1, "e_scorr": e_scorr}

        def front_b(ti, fa):
            pz = ti % 2
            e_attmm = [None, None]
            e_attev = [None, None]
            for e in range(DCH):
                par = e % 2

                @blk.sync
                def _(sy):
                    wt(sy, R["w1_roll"][par])
                    inc(sy.dma_start(out=wtbuf[par][:], in_=wt_scr.ap()[e]),
                        "sLD", 16)
                e_wtld = evt("sLD")

                @blk.tensor
                def _(t):
                    wt(t, e_wtld, e_attev[par])
                    if e == 0:
                        wt(t, fa["e_scorr"], fa["e_h1"], fa["e_negm1"])
                    if e == 1:
                        wt(t, fa["e_var1"])
                    for k_ in range(DCH):
                        t.matmul(bank[par][:], wtbuf[par][:, k_ * P:(k_ + 1) * P],
                                 s_in[:, k_ * NT:(k_ + 1) * NT],
                                 start=(k_ == 0), stop=False)
                    for k_ in range(5):
                        mm = t.matmul(bank[par][:],
                                      diag[:, (e * 5 + k_) * P:(e * 5 + k_ + 1) * P],
                                      h1b[:, e * C6 + k_:e * C6 + k_ + NT],
                                      start=False, stop=(k_ == 4))
                    inc(mm, "sPE", 1)
                e_attmm[par] = evt("sPE")
                R["w1_roll"][par] = e_attmm[par]

                @blk.vector
                def _(v):
                    wt(v, e_attmm[par])
                    if e == 0:
                        wt(v, R["x1_free"][pz])
                    inc(v.scalar_tensor_tensor(
                        out=x1[pz][:, e * NT:(e + 1) * NT], in0=bank[par][:],
                        scalar=small[:, 32 + e:33 + e],
                        in1=xa[:, e * C6 + 2:e * C6 + 2 + NT],
                        op0=Op.add, op1=Op.add), "sV", 1)
                e_attev[par] = evt("sV")
            e_x1done = evt("sV")
            R["xa_free"] = e_x1done
            R["sin_free"] = e_attmm[1]

            @blk.vector
            def _(v):
                wt(v, e_x1done)
                v.tensor_copy(out=castA[:, 0:DCH * NT], in_=x1[pz][:])
                inc(v.tensor_tensor(out=sqA[:, 0:DCH * NT], in0=x1[pz][:],
                                    in1=x1[pz][:], op=Op.mult), "sV", 1)
            e_x1b = evt("sV")
            e_sq2 = e_x1b

            @blk.tensor
            def _(t):
                wt(t, e_x1b, e_sq2, fa["e_rs1"], e_attev[0], e_attev[1])
                for ch in range(DCH):
                    t.matmul(bank[0][:], sMean[:],
                             castA[:, ch * NT:(ch + 1) * NT],
                             start=(ch == 0), stop=(ch == DCH - 1))
                for ch in range(DCH):
                    mm = t.matmul(bank[1][:], sSq[:],
                                  sqA[:, ch * NT:(ch + 1) * NT],
                                  start=(ch == 0), stop=(ch == DCH - 1))
                inc(mm, "sPE", 1)
            e_stat2 = evt("sPE")
            R["castA_free"] = e_stat2

            @blk.vector
            def _(v):
                wt(v, e_stat2, fa["e_h1"])
                v.tensor_scalar(out=negm_s[:], in0=bank[0][:], scalar1=1.0,
                                scalar2=None, op0=Op.mult)
                v.tensor_tensor(out=var_s[:], in0=negm_s[:], in1=negm_s[:],
                                op=Op.mult)
                inc(v.tensor_tensor(out=var_s[:], in0=bank[1][:], in1=var_s[:],
                                    op=Op.subtract), "sV", 1)
            e_var2 = evt("sV")
            e_negm2 = e_var2

            @blk.scalar
            def _(s):
                wt(s, e_var2)
                s.activation(out=r_s[:], in_=var_s[:], func=AF.Ln, bias=eps_t[:])
                inc(s.activation(out=r_s[:], in_=r_s[:], func=AF.Exp, scale=-0.5),
                    "sA", 1)
            e_rs2 = evt("sA")
            R["b1_free"] = e_var2

            @blk.vector
            def _(v):
                wt(v, e_rs2, R["zb_free"][pz])
                for ch in range(DCH):
                    v.tensor_tensor(out=h1tmp[:],
                                    in0=x1[pz][:, ch * NT:(ch + 1) * NT],
                                    in1=negm_s[:], op=Op.add)
                    v.tensor_tensor(out=zbuf[pz][:, ch * NT:(ch + 1) * NT],
                                    in0=h1tmp[:], in1=r_s[:], op=Op.mult)
                inc(v.engine_nop(), "sV", 1)
            e_zb = evt("sV")

            @blk.vector
            def _(v):
                wt(v, e_zb)
                inc(v.tensor_tensor(out=sqA[:, 0:DCH * NT], in0=zbuf[pz][:],
                                    in1=zbuf[pz][:], op=Op.mult), "sV", 1)
            e_sq3 = evt("sV")

            @blk.tensor
            def _(t):
                wt(t, e_zb, e_sq3, e_negm2)
                for ch in range(DCH):
                    mm = t.matmul(bank[0][:], sMean[:],
                                  zbuf[pz][:, ch * NT:(ch + 1) * NT],
                                  start=(ch == 0), stop=(ch == DCH - 1))
                inc(mm, "sPE", 1)
            e_stat3 = evt("sPE")

            @blk.vector
            def _(v):
                wt(v, e_stat3)
                v.tensor_copy(out=spacer[:], in_=iota_row[:])
                v.tensor_copy(out=nmzr[:], in_=bank[0][0:1, :])
                inc(v.tensor_copy(out=spacer[:], in_=iota_row[:]), "sV", 1)
            e_negm3 = evt("sV")
            R["b0_free"] = e_negm3

            @blk.tensor
            def _(t):
                wt(t, e_zb, e_negm3, R["b2_free"] or fa["e_var1"], fa["e_var1"])
                for g_ in range(4):
                    for ch in range(DCH):
                        t.matmul(bank[2][:, 8 * g_:8 * g_ + 8],
                                 zbuf[pz][:, ch * NT + g_ * P:ch * NT + (g_ + 1) * P],
                                 wr_b[:, 8 * ch:8 * ch + 8],
                                 start=(ch == 0), stop=False)
                    t.matmul(bank[2][:, 8 * g_:8 * g_ + 8],
                             nmzr[0:1, g_ * P:(g_ + 1) * P], c1r[:],
                             start=False, stop=True)
                for g_ in range(4):
                    for ch in range(DCH):
                        t.matmul(bank[2][:, 32 + g_:33 + g_],
                                 zbuf[pz][:, ch * NT + g_ * P:ch * NT + (g_ + 1) * P],
                                 sMean[:, 0:1],
                                 start=(ch == 0), stop=(ch == DCH - 1))
                for g_ in range(4):
                    for ch in range(DCH):
                        t.matmul(bank[2][:, 36 + g_:37 + g_],
                                 sqA[:, ch * NT + g_ * P:ch * NT + (g_ + 1) * P],
                                 sSq[:, 0:1],
                                 start=(ch == 0), stop=(ch == DCH - 1))
                inc(t.matmul(bank[2][:, 128:384], sMean[:], zbuf[pz][:, 256:512],
                             start=True, stop=True), "sPE", 1)
            e_logit = evt("sPE")
            R["sqA_free"] = e_logit

            @blk.vector
            def _(v):
                wt(v, e_logit)
                v.tensor_copy(out=spacer[:], in_=iota_row[:])
                v.tensor_copy(out=lg[:], in_=bank[2][:, 0:32])
                inc(v.tensor_copy(out=spacer[:], in_=iota_row[:]), "sV", 1)
            e_lg = evt("sV")

            @blk.vector
            def _(v):
                wt(v, e_lg)
                v.tensor_copy(out=spacer[:], in_=iota_row[:])
                v.tensor_copy(out=negmhW[:, 0:128], in_=bank[2][:, 0:128])
                v.tensor_copy(out=spacer[:], in_=iota_row[:])
                v.tensor_tensor(out=hscW[:, 0:128], in0=negmhW[:, 0:128],
                                in1=negmhW[:, 0:128], op=Op.mult)
                v.tensor_tensor(out=rzc[:], in0=bank[2][:, 36:40],
                                in1=hscW[:, 32:36], op=Op.subtract)
                inc(v.tensor_copy(out=spacer[:], in_=iota_row[:]), "sV", 1)
            e_rzv = evt("sV")
            R["b2_free"] = e_rzv

            @blk.scalar
            def _(s):
                wt(s, e_rzv)
                s.activation(out=rzc[:], in_=rzc[:], func=AF.Ln, bias=eps_t[:])
                s.activation(out=spacerA[:], in_=iota_row[:], func=AF.Ln,
                             bias=eps_t[:])
                s.activation(out=rzc[:], in_=rzc[:], func=AF.Exp, scale=-0.5)
                inc(s.activation(out=spacerA[:], in_=iota_row[:], func=AF.Exp,
                                 scale=-1.0), "sA", 1)
            e_rzc = evt("sA")

            @blk.vector
            def _(v):
                wt(v, e_rzc)
                for g_ in range(4):
                    v.tensor_scalar(out=lg[:, 8 * g_:8 * g_ + 8],
                                    in0=lg[:, 8 * g_:8 * g_ + 8],
                                    scalar1=rzc[:, g_:g_ + 1],
                                    scalar2=None, op0=Op.mult)
                inc(v.engine_nop(), "sV", 1)
            e_rzap = evt("sV")

            @blk.vector
            def _(v):
                wt(v, e_rzap)
                for g_ in range(4):
                    v.tensor_tensor(out=lg[:, 8 * g_:8 * g_ + 8],
                                    in0=lg[:, 8 * g_:8 * g_ + 8], in1=c2bc[:],
                                    op=Op.add)
                    v.tensor_reduce(out=sm[:, g_:g_ + 1],
                                    in_=lg[:, 8 * g_:8 * g_ + 8],
                                    axis=mybir.AxisListType.X, op=Op.max,
                                    negate=True)
                inc(v.tensor_copy(out=spacer[:], in_=iota_row[:]), "sV", 1)
            e_nm = evt("sV")

            @blk.scalar
            def _(s):
                wt(s, e_nm)
                s.activation(out=spacerA[:], in_=iota_row[:], func=AF.Exp,
                             scale=-1.0)
                for g_ in range(4):
                    ex = s.activation(out=lg[:, 8 * g_:8 * g_ + 8],
                                      in_=lg[:, 8 * g_:8 * g_ + 8], func=AF.Exp,
                                      bias=sm[:, g_:g_ + 1])
                inc(ex, "sA", 1)
            e_exp = evt("sA")

            @blk.vector
            def _(v):
                wt(v, e_exp)
                v.tensor_copy(out=spacer[:], in_=iota_row[:])
                for g_ in range(4):
                    v.tensor_reduce(out=sm[:, 4 + g_:5 + g_],
                                    in_=lg[:, 8 * g_:8 * g_ + 8],
                                    axis=mybir.AxisListType.X, op=Op.add)
                v.tensor_copy(out=spacer[:], in_=iota_row[:])
                v.reciprocal(out=sm[:, 8:12], in_=sm[:, 4:8])
                v.tensor_copy(out=spacer[:], in_=iota_row[:])
                for g_ in range(4):
                    v.tensor_tensor(out=sm[:, 12 + g_:13 + g_],
                                    in0=lg[:, 8 * g_:8 * g_ + 1],
                                    in1=sm[:, 8 + g_:9 + g_], op=Op.mult)
                inc(v.tensor_copy(out=spacer[:], in_=iota_row[:]), "sV", 1)
            e_soft = evt("sV")

            @blk.tensor
            def _(t):
                wt(t, e_soft, R["b7_free"])
                t.transpose(bank[2][:, 128:256], ident[:], ident[:])
                for g_ in range(4):
                    tr = t.transpose(bank[7][0:1, g_ * P:(g_ + 1) * P],
                                     sm[:, 12 + g_:13 + g_], ident[:])
                inc(tr, "sPE", 1)
            e_p0T = evt("sPE")

            @blk.vector
            def _(v):
                wt(v, e_p0T)
                v.tensor_copy(out=spacer[:], in_=iota_row[:])
                v.tensor_copy(out=g0r[:], in_=bank[7][0:1, :])
                inc(v.tensor_copy(out=spacer[:], in_=iota_row[:]), "sV", 1)
            e_g0r = evt("sV")

            @blk.tensor
            def _(t):
                wt(t, e_g0r)
                inc(t.matmul(bank[7][:], ones1b[:], g0r[:], start=True,
                             stop=True), "sPE", 1)
            e_g0bc = evt("sPE")

            @blk.vector
            def _(v):
                wt(v, e_g0bc, R["g0s_free"][pz])
                inc(v.tensor_scalar(out=g0s[pz][:], in0=bank[7][:],
                                    scalar1=1.0, scalar2=None, op0=Op.mult),
                    "sV", 1)
            e_g0sE = evt("sV")
            R["b7_free"] = e_g0sE
            return {"e_zb": e_zb, "e_g0sE": e_g0sE, "e_x1done": e_x1done}

        def back_pre(ti, exp, fb):
            pz = ti % 2
            wsrc = w1_scr if exp == 0 else pw1_scr
            bcol = b1c if exp == 0 else pb1c
            e_pre = {}
            e_actstt = {}
            for f in range(FCH):
                par = f % 2
                p4 = f % 3

                @blk.sync
                def _(sy):
                    wt(sy, R["w1_roll"][p4])
                    inc(sy.dma_start(out=wbuf[p4][:], in_=wsrc.ap()[f]),
                        "sLD", 16)
                e_wld = evt("sLD")

                @blk.tensor
                def _(t):
                    wt(t, e_wld, R["pre_free"][par])
                    if f == 0:
                        wt(t, fb["e_zb"])
                    mm = None
                    for k_ in range(DCH):
                        mm = t.matmul(bank[5 + par][:],
                                      wbuf[p4][:, k_ * P:(k_ + 1) * P],
                                      zbuf[pz][:, k_ * NT:(k_ + 1) * NT],
                                      start=(k_ == 0), stop=(k_ == DCH - 1))
                    inc(mm, "sPE", 1)
                e_pre[f] = evt("sPE")
                R["w1_roll"][p4] = e_pre[f]

                @blk.scalar
                def _(s):
                    wt(s, e_pre[f])
                    if f == 0 and exp == 0:
                        wt(s, R["act_free"])
                    inc(s.activation(out=act_sb[:, f * NT:(f + 1) * NT],
                                     in_=bank[5 + par][:],
                                     func=AF.Gelu_apprx_tanh,
                                     bias=bcol[:, f:f + 1]), "sA", 1)
                e_actstt[f] = evt("sA")
                R["pre_free"][par] = e_actstt[f]
            return {"e_pre": e_pre, "e_actstt": e_actstt}

        def back_y(ti, exp, fb, pre_ev):
            pz = ti % 2
            w2src = w2_scr if exp == 0 else pw2_scr
            e_yfree = {}
            for d in range(DCH):
                par = d % 2

                @blk.sync
                def _(sy):
                    wt(sy, R["w2_roll"][par])
                    inc(sy.dma_start(
                        out=w2buf[2 * par + 0][:],
                        in_=w2src.ap()[d, :, 0:2048]), "sLD", 16)
                    inc(sy.dma_start(
                        out=w2buf[2 * par + 1][:],
                        in_=w2src.ap()[d, :, 2048:4096]), "sLD", 16)
                e_w2ld = evt("sLD")

                @blk.tensor
                def _(t):
                    wt(t, e_w2ld, R["y_free"][par])
                    if d == 0:
                        wt(t, pre_ev["e_actstt"][FCH - 1])
                    mm = None
                    for f in range(FCH):
                        mm = t.matmul(
                            bank[3 + par][:],
                            w2buf[2 * par + f // 16][:, (f % 16) * P:
                                                     (f % 16 + 1) * P],
                            act_sb[:, f * NT:(f + 1) * NT],
                            start=(f == 0), stop=(f == FCH - 1))
                    inc(mm, "sPE", 1)
                e_y = evt("sPE")
                R["w2_roll"][par] = e_y

                if exp == 0:
                    @blk.vector
                    def _(v):
                        wt(v, e_y)
                        if d == 0:
                            wt(v, R["y0s_free"])
                        inc(v.tensor_scalar(out=y0s[:, d * NT:(d + 1) * NT],
                                            in0=bank[3 + par][:],
                                            scalar1=b2c[:, d:d + 1],
                                            scalar2=None, op0=Op.add),
                            "sV", 1)
                    e_yfree[(exp, d)] = evt("sV")
                else:
                    @blk.vector
                    def _(v):
                        wt(v, e_y, fb["e_g0sE"], R["ot_dma"][par])
                        v.tensor_scalar(out=h1tmp[:], in0=bank[3 + par][:],
                                        scalar1=pb2c[:, d:d + 1],
                                        scalar2=None, op0=Op.add)
                        v.tensor_tensor(out=qs[:],
                                        in0=y0s[:, d * NT:(d + 1) * NT],
                                        in1=h1tmp[:], op=Op.subtract)
                        v.tensor_tensor(out=qs[:], in0=qs[:], in1=g0s[pz][:],
                                        op=Op.mult)
                        v.tensor_tensor(out=qs[:], in0=qs[:],
                                        in1=h1tmp[:], op=Op.add)
                        inc(v.tensor_tensor(out=otile[par][:], in0=qs[:],
                                            in1=x1[pz][:, d * NT:(d + 1) * NT],
                                            op=Op.add), "sV", 1)
                    e_yfree[(exp, d)] = evt("sV")

                    @blk.gpsimd
                    def _(g):
                        wt(g, e_yfree[(exp, d)])
                        inc(g.dma_start(
                            out=out_fm[d * P:(d + 1) * P,
                                       ti * NT:(ti + 1) * NT],
                            in_=otile[par][:]), "sST", 16)
                    R["ot_dma"][par] = evt("sST")
                R["y_free"][par] = e_yfree[(exp, d)]
            e_yfree["last_mm"] = e_y
            return e_yfree

        fa = front_a(0)
        fb = front_b(0, fa)
        fronts = {0: fb}
        for t_i in range(NTILES):
            fbt = fronts[t_i]
            pre0 = back_pre(t_i, 0, fbt)
            fa2 = None
            if t_i + 1 < NTILES:
                fa2 = front_a(t_i + 1)
            y0ev = back_y(t_i, 0, fbt, pre0)
            pre1 = back_pre(t_i, 1, fbt)
            if t_i + 1 < NTILES:
                fb2 = front_b(t_i + 1, fa2)
                fronts[t_i + 1] = fb2
            y1ev = back_y(t_i, 1, fbt, pre1)
            pz = t_i % 2
            R["zb_free"][pz] = pre1["e_pre"][FCH - 1]
            R["x1_free"][pz] = y1ev[(1, DCH - 1)]
            R["g0s_free"][pz] = y1ev[(1, DCH - 1)]
            R["y0s_free"] = y1ev[(1, DCH - 1)]
            R["act_free"] = y1ev["last_mm"]

        @blk.gpsimd
        def _(g):
            g.wait_ge(sems["sST"], cnt["sST"])

    return nc


_NC_CACHE = {}


def _get_nc():
    if "nc" not in _NC_CACHE:
        _NC_CACHE["nc"] = build_program()
    return _NC_CACHE["nc"]


def _chunk_layout(v):
    return np.ascontiguousarray(v.reshape(DCH, P).T)


def kernel(L0_attn_x, L0_attn_decay_logit, L0_attn_W, L0_attn_b, L0_attn_conv_w,
           L0_attn_conv_b, L0_ffn_W1, L0_ffn_b1, L0_ffn_W2, L0_ffn_b2,
           L0_ffn_ln_gamma, L0_ffn_ln_beta, L0_ffn_Wr, L0_ffn_br):
    x = np.asarray(L0_attn_x, dtype=np.float32)
    B, T, _ = x.shape
    W1 = np.ascontiguousarray(np.asarray(L0_ffn_W1, dtype=np.float32))
    W2 = np.ascontiguousarray(np.asarray(L0_ffn_W2, dtype=np.float32))
    b1 = np.asarray(L0_ffn_b1, dtype=np.float32)
    b2 = np.asarray(L0_ffn_b2, dtype=np.float32)
    Wr = np.asarray(L0_ffn_Wr, dtype=np.float32)
    br = np.asarray(L0_ffn_br, dtype=np.float32)
    Wat = np.ascontiguousarray(np.asarray(L0_attn_W, dtype=np.float32))
    dec = np.asarray(L0_attn_decay_logit, dtype=np.float32)
    batt = np.asarray(L0_attn_b, dtype=np.float32)
    cw = np.asarray(L0_attn_conv_w, dtype=np.float32).reshape(D, 5)
    cb = np.asarray(L0_attn_conv_b, dtype=np.float32)
    gam = np.asarray(L0_ffn_ln_gamma, dtype=np.float32)
    bet = np.asarray(L0_ffn_ln_beta, dtype=np.float32)

    w1flat = W1.reshape(P, 32768)
    w2flat = W2.reshape(P, 32768)
    b1tile = np.ascontiguousarray(
        np.broadcast_to(b1.reshape(1, F_FF), (1024, F_FF))).reshape(P, 32768)
    b2tile = np.ascontiguousarray(
        np.broadcast_to(b2.reshape(1, D), (4096, D))).reshape(P, 32768)
    bis_datas = [w1flat, w1flat, w2flat, w2flat, b1tile, b1tile, b2tile, b2tile]
    bis_ks = [float(K_W), float(N_W - K_W + 1),
              float(K_W), float(N_W - K_W + 1),
              float(1024 * K_B1), float(1024 * (F_FF - K_B1 + 1)),
              float(4096 * K_B2), float(4096 * (D - K_B2 + 1))]

    convw_l = np.ascontiguousarray(
        cw.reshape(DCH, P, 5).transpose(1, 0, 2).reshape(P, 5 * DCH))
    wr_l = np.ascontiguousarray(
        Wr.reshape(DCH, P, 8).transpose(1, 0, 2).reshape(P, 8 * DCH))

    in_maps = []
    for c in range(8):
        b_, h_ = c // 2, c % 2
        xf = np.zeros((D, TCORE + 4), dtype=np.float32)
        lo = h_ * TCORE - 2
        hi = h_ * TCORE + TCORE + 2
        slo, shi = max(lo, 0), min(hi, T)
        xf[:, (slo - lo):(slo - lo) + (shi - slo)] = x[b_, slo:shi, :].T
        in_maps.append({
            "x_fm": np.ascontiguousarray(xf),
            "bis_data": np.ascontiguousarray(bis_datas[c]),
            "bis_k": np.full((P, 1), bis_ks[c], dtype=np.float32),
            "oddflag": np.full((P, 1), float(h_), dtype=np.float32),
            "w_attn": Wat,
            "dec_in": _chunk_layout(dec),
            "battn_in": _chunk_layout(batt),
            "convb_in": _chunk_layout(cb),
            "convw_in": convw_l,
            "w1_in": W1,
            "b1c_in": np.ascontiguousarray(b1.reshape(FCH, P).T),
            "w2_in": W2,
            "b2c_in": np.ascontiguousarray(b2.reshape(DCH, P).T),
            "gam_in": _chunk_layout(gam),
            "beta_in": _chunk_layout(bet),
            "wr_in": wr_l,
            "br_in": br.reshape(1, 8),
        })

    nc = _get_nc()
    res = run_bass_kernel_spmd(nc, in_maps, core_ids=list(range(8)))
    out = np.empty((B, T, D), dtype=np.float32)
    for c in range(8):
        b_, h_ = c // 2, c % 2
        out[b_, h_ * TCORE:(h_ + 1) * TCORE, :] = res.results[c]["out_fm"].T
    return out
